# version 09
# baseline (speedup 1.0000x reference)
import numpy as np
import ml_dtypes
from contextlib import ExitStack
import concourse.bass as bass
import concourse.mybir as mybir
from concourse.bass_utils import run_bass_kernel_spmd

F32 = mybir.dt.float32
BF16 = mybir.dt.bfloat16
U8 = mybir.dt.uint8
AF = mybir.ActivationFunctionType
ALU = mybir.AluOpType
AX = mybir.AxisListType

D = 1024
E_EXP = 8
H = 8
DK = 128
SEG = 512
NTC = SEG // 128
CG = D // 128
S_ATTN = float(DK) ** -0.5
F1 = D * 4

DEFAULT_CFG = dict(nbatch=2, segs_per_batch=8, nexp=8, debug=False)


def build_program(cfg):
    nb = cfg["nbatch"]
    spb = cfg["segs_per_batch"]
    nexp = cfg["nexp"]
    seqlen = spb * SEG
    nseg = nb * spb

    nc = bass.Bass("TRN2", target_bir_lowering=False)

    x_d = nc.dram_tensor("x", [nb, seqlen, D], F32, kind="ExternalInput").ap()
    gate_d = nc.dram_tensor("gate_wT", [D, E_EXP], F32, kind="ExternalInput").ap()
    qwT_d = nc.dram_tensor("q_wT", [D, D], F32, kind="ExternalInput").ap()
    kwT_d = nc.dram_tensor("k_wT", [E_EXP, D, D], F32, kind="ExternalInput").ap()
    vwT_d = nc.dram_tensor("v_wT", [E_EXP, D, D], F32, kind="ExternalInput").ap()
    fm_d = [nc.dram_tensor(n, [DK, DK], F32, kind="ExternalInput").ap()
            for n in ["fmq1T", "fmq2T", "fmk1T", "fmk2T"]]
    nw_d = nc.dram_tensor("norm_w", [DK, 1], F32, kind="ExternalInput").ap()
    owT_d = nc.dram_tensor("o_wT", [D, D], F32, kind="ExternalInput").ap()
    k0_d = nc.dram_tensor("k0", [2 * D, F1], U8, kind="ExternalInput").ap()
    k1_d = nc.dram_tensor("k1", [F1, F1], U8, kind="ExternalInput").ap()
    k2_d = nc.dram_tensor("k2", [F1, D], U8, kind="ExternalInput").ap()
    s0_d = nc.dram_tensor("s0", [1, 1], F32, kind="ExternalInput").ap()
    s1_d = nc.dram_tensor("s1", [1, 1], F32, kind="ExternalInput").ap()
    s2_d = nc.dram_tensor("s2", [1, 1], F32, kind="ExternalInput").ap()
    out_d = nc.dram_tensor("out", [nb, seqlen, D], F32, kind="ExternalOutput").ap()

    kv_scr = nc.dram_tensor("kv_scr", [E_EXP, 2, D, D], BF16, kind="Internal").ap()
    w0_scr = nc.dram_tensor("w0_scr", [D, F1], BF16, kind="Internal").ap()
    w1_scr = nc.dram_tensor("w1_scr", [F1 // 128, 128, F1], BF16, kind="Internal").ap()
    w2_scr = nc.dram_tensor("w2_scr", [F1, D], BF16, kind="Internal").ap()
    st_scr = nc.dram_tensor("st_scr", [E_EXP, 128, H, DK], F32, kind="Internal").ap()
    qw_scr = nc.dram_tensor("qw_scr", [D, D], BF16, kind="Internal").ap()
    ow_scr = nc.dram_tensor("ow_scr", [D, D], BF16, kind="Internal").ap()

    dbg = {}
    if cfg["debug"]:
        for n, shape, dt in [
            ("xn_fm", [128, CG, SEG], BF16), ("wgt01", [128, NTC, E_EXP], F32),
            ("mask01", [128, NTC, E_EXP], F32), ("Qp", [128, H, SEG], BF16),
            ("Kp", [128, H, SEG], BF16), ("Kptm", [128, NTC, H, DK], BF16),
            ("Vp", [128, NTC, H, DK], BF16),
            ("Oacc", [128, NTC, D], F32), ("H1", [128, F1 // 128, SEG], BF16),
            ("H2", [128, F1 // 128, SEG], BF16), ("Onrmfm", [128, CG, SEG], BF16),
        ]:
            dbg[n] = nc.dram_tensor("dbg_" + n, shape, dt, kind="ExternalOutput").ap()

    es = ExitStack()
    SEMS = {}
    for s in ["sX", "sKV", "sQW", "sOW", "sST", "sW0", "sW1A", "sW1B", "sW2", "sW2B", "sQS", "sPL",
              "sP", "sOUT", "sSV", "sPS", "sG1", "sG2", "sPE", "sVE", "sSC", "sGP"]:
        SEMS[s] = es.enter_context(nc.semaphore(s))
    C = {k: 0 for k in SEMS}

    def adv(sem, k):
        C[sem] += k
        return C[sem]

    def sb(name, shape, dt):
        return es.enter_context(nc.sbuf_tensor(name, shape, dt))

    gate_sb = sb("gate_sb", [128, CG, E_EXP], BF16)
    fm_sb = sb("fm_sb", [128, 4, DK], BF16)
    identb = sb("identb", [128, 128], BF16)
    identf = sb("identf", [128, 128], F32)
    tri_s = sb("tri_s", [128, 128], F32)
    tri_b = sb("tri_b", [128, 128], BF16)
    onesr = sb("onesr", [1, 128], F32)
    scol = sb("scol", [128, 4], F32)
    nw_sb = sb("nw_sb", [128, 1], F32)
    ssc_sb = sb("ssc_sb", [1, 4], F32)
    S_mast = sb("S_mast", [128, H, DK], F32)
    S_bf = sb("S_bf", [128, H, DK], BF16)

    x_seg = sb("x_seg", [128, NTC, D], F32)
    xn_fm = sb("xn_fm", [128, CG, SEG], BF16)
    Qp_fm = sb("Qp_fm", [128, H, SEG], BF16)
    wgt01 = sb("wgt01", [128, NTC, E_EXP], F32)
    mask01 = sb("mask01", [128, NTC, E_EXP], F32)
    O_acc = sb("O_acc", [128, NTC, D], F32)
    Onrm_fm = sb("Onrm_fm", [128, CG, SEG], BF16)
    xn_tmp = sb("xn_tmp", [128, D], BF16)
    Onrm = xn_tmp
    tmp_sq = sb("tmp_sq", [128, D], F32)
    rt = sb("rt", [128, 24], F32)
    rt3a = sb("rt3a", [128, NTC, E_EXP], F32)
    rt3b = sb("rt3b", [128, NTC, E_EXP], F32)
    rt3c = sb("rt3c", [128, NTC, E_EXP], F32)
    rms_s = sb("rms_s", [128, NTC, H], F32)
    rms_r = sb("rms_r", [128, NTC, H], F32)
    cst = sb("cst", [128, 6], F32)
    _cvals = [1e-5, -0.5, 1.0 / DK, S_ATTN, 1.0, 0.0]
    for _ci, _cv in enumerate(_cvals):
        nc.const_aps.register(F32, _cv, cst[:, _ci:_ci + 1])

    ARENA_BYTES = 124928
    arena = sb("arena", [128, ARENA_BYTES // 2], BF16)

    def carve(off_bytes, shape, dt):
        nelem = 1
        for s in shape[1:]:
            nelem *= s
        esz = {BF16: 2, F32: 4, U8: 1}[dt]
        base = arena[:].bitcast(dt) if dt != BF16 else arena[:]
        v = base[:, off_bytes // esz: off_bytes // esz + nelem]
        if len(shape) > 2:
            dims = "".join(chr(ord('a') + i) for i in range(len(shape) - 1))
            v = v.rearrange(f"p ({' '.join(dims)}) -> p {' '.join(dims)}",
                            **{chr(ord('a') + i): shape[1 + i] for i in range(len(shape) - 1)})
        return v

    kvw_sb = carve(0, [128, CG, D], BF16)
    Khat2 = carve(16384, [128, 3, SEG], BF16)
    Kp_fm = carve(19456, [128, H, SEG], BF16)
    Kp_tm = carve(27648, [128, NTC, H, DK], BF16)
    Vp_tm = carve(35840, [128, NTC, H, DK], BF16)
    aT_big = carve(44032, [128, 2, 1280], BF16)
    A1s = carve(49152, [128, SEG], BF16)
    A1s2 = carve(50176, [128, SEG], BF16)
    H1 = carve(51200, [128, F1 // 128, SEG], BF16)
    H2 = carve(83968, [128, F1 // 128, SEG], BF16)
    w01_sb = carve(116736, [128, CG, 512], BF16)
    w1v_sb = carve(116736, [128, F1 // 128, 128], BF16)
    owT_sb = carve(0, [128, CG, D], BF16)
    out_sb = carve(16384, [128, 2, D], F32)
    w2_sb = carve(27648, [128, 2, 4, D], BF16)
    pstg_a = sb("pstg_a", [128, F1 // 2], U8)
    pstg_b = sb("pstg_b", [128, F1 // 2], U8)
    pstg_f = sb("pstg_f", [128, 512], F32)
    pstg_o = sb("pstg_o", [128, F1 // 2], BF16)

    pALL = es.enter_context(nc.psum_tensor("pALL", [128, 4096], F32))

    def bank(i, lo=0, n=512):
        return pALL[:, i * 512 + lo: i * 512 + lo + n]

    def pflat(lo, n):
        return pALL[:, lo: lo + n]

    def bankbf(i, lo=0, n=1024):
        return pALL[:].bitcast(BF16)[:, i * 1024 + lo: i * 1024 + lo + n]

    def pbf(off, n):
        return pALL[:].bitcast(BF16)[:, off: off + n]

    REC = []

    def rec(eng, fn, **kw):
        REC.append((eng, fn, kw))

    def _dma(e, out=None, in_=None, sem=None):
        e.dma_start(out=out, in_=in_).then_inc(SEMS[sem], 16)

    def _wait(e, sem=None, val=0):
        e.wait_ge(SEMS[sem], val)

    def _mm(e, out=None, lhsT=None, rhs=None, start=True, stop=True, sem=None):
        m = e.matmul(out, lhsT, rhs, start=start, stop=stop)
        if sem:
            m.then_inc(SEMS[sem], 1)

    def _tr(e, out=None, in_=None, ident=None, sem=None):
        m = e.transpose(out, in_, ident)
        if sem:
            m.then_inc(SEMS[sem], 1)

    def _cp(e, out=None, in_=None, sem=None):
        m = e.tensor_copy(out=out, in_=in_)
        if sem:
            m.then_inc(SEMS[sem], 1)

    def _scp(e, out=None, in_=None, sem=None):
        m = e.copy(out=out, in_=in_)
        if sem:
            m.then_inc(SEMS[sem], 1)

    def _tt(e, out=None, in0=None, in1=None, op=None, sem=None):
        m = e.tensor_tensor(out=out, in0=in0, in1=in1, op=op)
        if sem:
            m.then_inc(SEMS[sem], 1)

    def _ts(e, out=None, in0=None, s1=None, s2=None, op0=None, op1=None, sem=None):
        kw = {}
        if op1 is not None:
            kw["op1"] = op1
        m = e.tensor_scalar(out=out, in0=in0, scalar1=s1, scalar2=s2, op0=op0, **kw)
        if sem:
            m.then_inc(SEMS[sem], 1)

    def _stt(e, out=None, in0=None, scalar=None, in1=None, op0=None, op1=None, sem=None):
        m = e.scalar_tensor_tensor(out=out, in0=in0, scalar=scalar, in1=in1, op0=op0, op1=op1)
        if sem:
            m.then_inc(SEMS[sem], 1)

    def _red(e, out=None, in_=None, op=None, sem=None):
        m = e.tensor_reduce(out=out, in_=in_, axis=AX.X, op=op)
        if sem:
            m.then_inc(SEMS[sem], 1)

    def _act(e, out=None, in_=None, func=None, bias=0.0, scale=1.0, sem=None):
        m = e.activation(out=out, in_=in_, func=func, bias=bias, scale=scale)
        if sem:
            m.then_inc(SEMS[sem], 1)

    def _mul(e, out=None, in_=None, mul=None, sem=None):
        m = e.mul(out=out, in_=in_, mul=mul)
        if sem:
            m.then_inc(SEMS[sem], 1)

    def _recip(e, out=None, in_=None, sem=None):
        m = e.reciprocal(out=out, in_=in_)
        if sem:
            m.then_inc(SEMS[sem], 1)

    def prep_consts(e):
        e.memset(identb[:], 0.0).then_inc(SEMS["sGP"], 1)
        e.memset(identf[:], 0.0).then_inc(SEMS["sGP"], 1)
        e.memset(tri_s[:], 1.0).then_inc(SEMS["sGP"], 1)
        e.memset(tri_b[:], 1.0).then_inc(SEMS["sGP"], 1)
        e.memset(onesr[:], 1.0).then_inc(SEMS["sGP"], 1)
        for _ci, _cv in enumerate([1e-5, -0.5, 1.0 / DK, S_ATTN, 1.0, 0.0]):
            e.memset(cst[:, _ci:_ci + 1], _cv).then_inc(SEMS["sGP"], 1)
        e.wait_ge(SEMS["sGP"], 11)
        e.affine_select(out=identb[:], in_=identb[:], compare_op=ALU.not_equal,
                        fill=1.0, base=0, pattern=[[-1, 128]],
                        channel_multiplier=1).then_inc(SEMS["sGP"], 1)
        e.affine_select(out=identf[:], in_=identf[:], compare_op=ALU.not_equal,
                        fill=1.0, base=0, pattern=[[-1, 128]],
                        channel_multiplier=1).then_inc(SEMS["sGP"], 1)
        e.affine_select(out=tri_s[:], in_=tri_s[:], compare_op=ALU.is_ge,
                        fill=0.0, base=0, pattern=[[1, 128]],
                        channel_multiplier=-1).then_inc(SEMS["sGP"], 1)
        e.affine_select(out=tri_b[:], in_=tri_b[:], compare_op=ALU.is_ge,
                        fill=0.0, base=0, pattern=[[1, 128]],
                        channel_multiplier=-1).then_inc(SEMS["sGP"], 1)
    rec("gpsimd", prep_consts)
    v_consts = adv("sGP", 15)

    small_loads = [
        (gate_sb[:], gate_d.rearrange("(cg p) e -> p cg e", p=128)),
    ]
    for i in range(4):
        small_loads.append((fm_sb[:, i, :], fm_d[i]))
    for dst, srcap in small_loads:
        rec("gpsimd", _dma, out=dst, in_=srcap, sem="sG1")
    v_small = adv("sG1", 16 * len(small_loads))

    rec("sync", _dma, out=nw_sb[:], in_=nw_d, sem="sP")
    rec("sync", _dma, out=ssc_sb[:, 0:1], in_=s0_d, sem="sP")
    rec("sync", _dma, out=ssc_sb[:, 1:2], in_=s1_d, sem="sP")
    rec("sync", _dma, out=ssc_sb[:, 2:3], in_=s2_d, sem="sP")
    v_scal = adv("sP", 64)
    rec("sync", _wait, sem="sP", val=v_scal)

    rec("gpsimd", _dma, out=qw_scr, in_=qwT_d, sem="sQS")
    v_qwscr = adv("sQS", 16)
    v_kvscr_e = []
    rec("gpsimd", _dma, out=kv_scr[0, 0], in_=kwT_d[0], sem="sG2")
    rec("gpsimd", _dma, out=kv_scr[0, 1], in_=vwT_d[0], sem="sG2")
    v_kvscr_e.append(adv("sG2", 32))

    rec("tensor", _wait, sem="sP", val=v_scal)
    rec("tensor", _wait, sem="sGP", val=v_consts)
    rec("tensor", _mm, out=bank(3, 0, 3), lhsT=onesr[:], rhs=ssc_sb[:, 0:3],
        start=True, stop=True, sem="sPE")
    v_scolmm = adv("sPE", 1)
    rec("vector", _wait, sem="sPE", val=v_scolmm)
    rec("vector", _ts, out=scol[:, 0:3], in0=bank(3, 0, 3), s1=2.0, op0=ALU.mult, sem="sVE")
    v1 = adv("sVE", 1)
    rec("vector", _wait, sem="sVE", val=v1)
    rec("vector", _ts, out=scol[:, 2:3], in0=bank(3, 2, 1), s1=4.0, op0=ALU.mult, sem="sVE")
    rec("vector", _ts, out=scol[:, 3:4], in0=bank(3, 2, 1), s1=2.0, op0=ALU.mult, sem="sVE")
    adv("sVE", 2)
    v_scol = C["sVE"]

    HF = F1 // 2

    for cg in range(CG):
        for hf in range(2):
            rec("gpsimd", _dma, out=pstg_a[:],
                in_=k0_d[cg * 128:(cg + 1) * 128, hf * HF:(hf + 1) * HF], sem="sPL")
            rec("gpsimd", _dma, out=pstg_b[:],
                in_=k0_d[D + cg * 128: D + (cg + 1) * 128, hf * HF:(hf + 1) * HF], sem="sPL")
            vload = adv("sPL", 32)
            rec("gpsimd", _wait, sem="sPL", val=vload)
            rec("gpsimd", _tt, out=pstg_o[:], in0=pstg_a[:], in1=pstg_b[:],
                op=ALU.subtract, sem="sGP")
            vconv = adv("sGP", 1)
            rec("gpsimd", _wait, sem="sGP", val=vconv)
            rec("gpsimd", _dma, out=w0_scr[cg * 128:(cg + 1) * 128, hf * HF:(hf + 1) * HF],
                in_=pstg_o[:], sem="sPS")
            vstore = adv("sPS", 16)
            rec("gpsimd", _wait, sem="sPS", val=vstore)
    v_w0scr = C["sPS"]
    for e_i in range(1, nexp):
        rec("gpsimd", _wait, sem="sG2", val=v_kvscr_e[-1])
        rec("gpsimd", _dma, out=kv_scr[e_i, 0], in_=kwT_d[e_i], sem="sG2")
        rec("gpsimd", _dma, out=kv_scr[e_i, 1], in_=vwT_d[e_i], sem="sG2")
        v_kvscr_e.append(adv("sG2", 32))
    while len(v_kvscr_e) < E_EXP:
        v_kvscr_e.append(v_kvscr_e[-1])

    for g in range(F1 // 128):
        for hf in range(2):
            rec("gpsimd", _dma, out=pstg_a[:],
                in_=k1_d[g * 128:(g + 1) * 128, hf * HF:(hf + 1) * HF], sem="sPL")
            vload = adv("sPL", 16)
            rec("gpsimd", _wait, sem="sPL", val=vload)
            rec("gpsimd", _ts, out=pstg_o[:], in0=pstg_a[:], s1=2.0, s2=-1.0,
                op0=ALU.mult, op1=ALU.add, sem="sGP")
            vconv = adv("sGP", 1)
            rec("gpsimd", _wait, sem="sGP", val=vconv)
            rec("gpsimd", _dma, out=w1_scr[g][:, hf * HF:(hf + 1) * HF],
                in_=pstg_o[:], sem="sPS")
            vstore = adv("sPS", 16)
            rec("gpsimd", _wait, sem="sPS", val=vstore)
    v_w1scr = C["sPS"]

    rec("gpsimd", _wait, sem="sVE", val=v_scol)
    for g in range(F1 // 128):
        rec("gpsimd", _dma, out=pstg_a[:, 0:D], in_=k2_d[g * 128:(g + 1) * 128, :], sem="sPL")
        vload = adv("sPL", 16)
        rec("gpsimd", _wait, sem="sPL", val=vload)
        rec("gpsimd", _ts, out=pstg_o[:, 0:D], in0=pstg_a[:, 0:D], s1=scol[:, 2:3],
            s2=scol[:, 3:4], op0=ALU.mult, op1=ALU.subtract, sem="sGP")
        vconv = adv("sGP", 1)
        rec("gpsimd", _wait, sem="sGP", val=vconv)
        rec("gpsimd", _dma, out=w2_scr[g * 128:(g + 1) * 128, :], in_=pstg_o[:, 0:D], sem="sPS")
        vstore = adv("sPS", 16)
        rec("gpsimd", _wait, sem="sPS", val=vstore)
    v_w2scr = C["sPS"]
    rec("gpsimd", _wait, sem="sP", val=v_scal)
    for cg in range(CG):
        for os_ in range(2):
            srcap = owT_d.rearrange("(cg p) o -> cg p o", p=128)[cg][:, os_ * 512:(os_ + 1) * 512]
            rec("gpsimd", _dma, out=pstg_f[:], in_=srcap, sem="sPL")
            vload = adv("sPL", 16)
            rec("gpsimd", _wait, sem="sPL", val=vload)
            rec("gpsimd", _ts, out=pstg_o[:, 0:512], in0=pstg_f[:],
                s1=nw_sb[:, 0:1], op0=ALU.mult, sem="sGP")
            vconv = adv("sGP", 1)
            rec("gpsimd", _wait, sem="sGP", val=vconv)
            rec("gpsimd", _dma, out=ow_scr[cg * 128:(cg + 1) * 128, os_ * 512:(os_ + 1) * 512],
                in_=pstg_o[:, 0:512], sem="sPS")
            vstore = adv("sPS", 16)
            rec("gpsimd", _wait, sem="sPS", val=vstore)
    v_owscr = C["sPS"]


    for eng in ["tensor", "vector", "scalar"]:
        rec(eng, _wait, sem="sG1", val=v_small)
        rec(eng, _wait, sem="sVE", val=v_scol)
        rec(eng, _wait, sem="sGP", val=v_consts)

    last = dict(
        kv_user=0, e3_evict=0, b2_free=0, e_pe_done=0,
        w0_user=0, w1_user=[0, 0], w2_user=[0, 0], out_user=[0, 0],
        stsave=[0] * E_EXP, c5_ve_done=0,
    )

    for seg in range(nseg):
        b = seg // spb
        sib = seg % spb
        t0 = sib * SEG
        first_seg = (sib == 0)
        last_seg = (sib == spb - 1)

        rec("sync", _wait, sem="sVE", val=last["e3_evict"])
        rec("sync", _wait, sem="sX", val=C["sX"])
        if cfg["debug"] and seg > 0:
            for eng in ["vector", "scalar", "tensor"]:
                rec(eng, _wait, sem="sOUT", val=C["sOUT"])
        for tc in range(NTC):
            rec("sync", _dma, out=x_seg[:, tc, :],
                in_=x_d[b, t0 + tc * 128: t0 + (tc + 1) * 128, :], sem="sX")
            adv("sX", 16)
        xw_total = C["sX"]
        xw = [xw_total] * NTC

        v_xn_tr = [0] * NTC
        v_xnfm = [0] * NTC
        xall = x_seg[:].rearrange("p a d -> p (a d)")
        tmp_sq4 = O_acc[:].rearrange("p a d -> p (a d)")
        rec("vector", _wait, sem="sX", val=xw[-1])
        rec("vector", _wait, sem="sVE", val=last["e3_evict"])
        rec("vector", _tt, out=tmp_sq4, in0=xall, in1=xall, op=ALU.mult, sem="sVE")
        vsq = adv("sVE", 1)
        rec("vector", _red, out=rt[:, 0:NTC], in_=x_seg[:], op=ALU.add, sem="sVE")
        adv("sVE", 1)
        rec("vector", _wait, sem="sVE", val=vsq)
        rec("vector", _red, out=rt[:, 4:4 + NTC],
            in_=tmp_sq4.rearrange("p (a d) -> p a d", a=NTC), op=ALU.add, sem="sVE")
        vred = adv("sVE", 1)
        rec("vector", _wait, sem="sVE", val=vred)
        rec("vector", _ts, out=rt[:, 0:NTC], in0=rt[:, 0:NTC], s1=1.0 / D, op0=ALU.mult, sem="sVE")
        vm_ = adv("sVE", 1)
        rec("vector", _wait, sem="sVE", val=vm_)
        rec("vector", _ts, out=rt[:, 4:4 + NTC], in0=rt[:, 4:4 + NTC], s1=1.0 / D,
            op0=ALU.mult, sem="sVE")
        vq_ = adv("sVE", 1)
        rec("vector", _wait, sem="sVE", val=vq_)
        rec("vector", _tt, out=rt[:, 8:8 + NTC], in0=rt[:, 0:NTC], in1=rt[:, 0:NTC],
            op=ALU.mult, sem="sVE")
        vmm2 = adv("sVE", 1)
        rec("vector", _wait, sem="sVE", val=vmm2)
        rec("vector", _tt, out=rt[:, 8:8 + NTC], in0=rt[:, 4:4 + NTC], in1=rt[:, 8:8 + NTC],
            op=ALU.subtract, sem="sVE")
        vvar = adv("sVE", 1)
        rec("scalar", _wait, sem="sVE", val=vvar)
        rec("scalar", _act, out=rt[:, 8:8 + NTC], in_=rt[:, 8:8 + NTC], func=AF.Ln, bias=1e-5,
            sem="sSC")
        vln = adv("sSC", 1)
        rec("scalar", _wait, sem="sSC", val=vln)
        rec("scalar", _act, out=rt[:, 12:12 + NTC], in_=rt[:, 8:8 + NTC], func=AF.Exp, scale=-0.5,
            sem="sSC")
        vrs = adv("sSC", 1)
        rec("vector", _wait, sem="sSC", val=vrs)
        rec("vector", _tt, out=rt[:, 16:16 + NTC], in0=rt[:, 0:NTC], in1=rt[:, 12:12 + NTC],
            op=ALU.mult, sem="sVE")
        vmr = adv("sVE", 1)
        rec("vector", _wait, sem="sVE", val=vmr)
        rec("vector", _ts, out=rt[:, 16:16 + NTC], in0=rt[:, 16:16 + NTC], s1=-1.0,
            op0=ALU.mult, sem="sVE")
        vnm = adv("sVE", 1)
        for tc in range(NTC):
            rec("scalar", _wait, sem="sVE", val=vnm)
            if tc > 0:
                rec("scalar", _wait, sem="sPE", val=v_xn_tr[tc - 1])
            rec("scalar", _act, out=xn_tmp[:], in_=x_seg[:, tc, :], func=AF.Identity,
                scale=rt[:, 12 + tc:13 + tc], bias=rt[:, 16 + tc:17 + tc], sem="sSC")
            vxn = adv("sSC", 1)
            rec("tensor", _wait, sem="sSC", val=vxn)
            rec("tensor", _wait, sem="sVE", val=last["b2_free"])
            for g in range(CG):
                rec("tensor", _tr, out=bankbf(2, g * 128, 128),
                    in_=xn_tmp[:, g * 128:(g + 1) * 128], ident=identb[:],
                    sem="sPE" if g == CG - 1 else None)
            v_xn_tr[tc] = adv("sPE", 1)
            rec("vector", _wait, sem="sPE", val=v_xn_tr[tc])
            rec("vector", _cp,
                out=xn_fm[:, :, tc * 128:(tc + 1) * 128],
                in_=bankbf(2, 0, 1024).rearrange("p (g t) -> p g t", g=CG), sem="sVE")
            v_xnfm[tc] = adv("sVE", 1)
            last["b2_free"] = v_xnfm[tc]

        for tc in range(NTC):
            rec("tensor", _wait, sem="sVE", val=v_xnfm[tc])
            for cg in range(CG):
                rec("tensor", _mm, out=bank(3, tc * E_EXP, E_EXP),
                    lhsT=xn_fm[:, cg, tc * 128:(tc + 1) * 128],
                    rhs=gate_sb[:, cg, :], start=(cg == 0), stop=(cg == CG - 1),
                    sem="sPE" if cg == CG - 1 else None)
        v_lg = adv("sPE", NTC)
        lg3 = bank(3, 0, NTC * E_EXP).rearrange("p (a e) -> p a e", a=NTC)
        rec("vector", _wait, sem="sPE", val=v_lg)
        rec("vector", _red, out=rt[:, 10:10 + NTC], in_=lg3, op=ALU.max, sem="sVE")
        vm1 = adv("sVE", 1)
        rec("vector", _wait, sem="sVE", val=vm1)
        m1b = rt[:, 10:10 + NTC].unsqueeze(2).broadcast_to([128, NTC, E_EXP])
        rec("vector", _tt, out=rt3a[:], in0=lg3, in1=m1b, op=ALU.is_equal, sem="sVE")
        veq1 = adv("sVE", 1)
        rec("vector", _wait, sem="sVE", val=veq1)
        rec("vector", _stt, out=rt3b[:], in0=rt3a[:], scalar=-1e30, in1=lg3,
            op0=ALU.mult, op1=ALU.add, sem="sVE")
        vl2 = adv("sVE", 1)
        rec("vector", _wait, sem="sVE", val=vl2)
        rec("vector", _red, out=rt[:, 14:14 + NTC], in_=rt3b[:], op=ALU.max, sem="sVE")
        vm2 = adv("sVE", 1)
        rec("vector", _wait, sem="sVE", val=vm2)
        rec("vector", _tt, out=rt[:, 18:18 + NTC], in0=rt[:, 14:14 + NTC],
            in1=rt[:, 10:10 + NTC], op=ALU.subtract, sem="sVE")
        vd = adv("sVE", 1)
        m2b = rt[:, 14:14 + NTC].unsqueeze(2).broadcast_to([128, NTC, E_EXP])
        rec("vector", _tt, out=rt3c[:], in0=rt3b[:], in1=m2b, op=ALU.is_equal, sem="sVE")
        veq2 = adv("sVE", 1)
        rec("scalar", _wait, sem="sVE", val=vd)
        rec("scalar", _act, out=rt[:, 10:10 + NTC], in_=rt[:, 18:18 + NTC], func=AF.Exp, sem="sSC")
        vex = adv("sSC", 1)
        rec("vector", _wait, sem="sSC", val=vex)
        rec("vector", _wait, sem="sVE", val=veq2)
        rec("vector", _ts, out=rt[:, 14:14 + NTC], in0=rt[:, 10:10 + NTC], s1=1.0,
            op0=ALU.add, sem="sVE")
        vp1 = adv("sVE", 1)
        rec("vector", _wait, sem="sVE", val=vp1)
        rec("vector", _recip, out=rt[:, 18:18 + NTC], in_=rt[:, 14:14 + NTC], sem="sVE")
        vw1 = adv("sVE", 1)
        rec("vector", _wait, sem="sVE", val=vw1)
        rec("vector", _tt, out=rt[:, 14:14 + NTC], in0=rt[:, 18:18 + NTC],
            in1=rt[:, 10:10 + NTC], op=ALU.mult, sem="sVE")
        vw2 = adv("sVE", 1)
        rec("vector", _wait, sem="sVE", val=max(vw2, veq2))
        w1b = rt[:, 18:18 + NTC].unsqueeze(2).broadcast_to([128, NTC, E_EXP])
        w2b = rt[:, 14:14 + NTC].unsqueeze(2).broadcast_to([128, NTC, E_EXP])
        rec("vector", _tt, out=mask01[:], in0=rt3a[:], in1=rt3c[:], op=ALU.add, sem="sVE")
        adv("sVE", 1)
        rec("vector", _tt, out=rt3b[:], in0=rt3a[:], in1=w1b, op=ALU.mult, sem="sVE")
        vt1 = adv("sVE", 1)
        rec("vector", _wait, sem="sVE", val=vt1)
        rec("vector", _tt, out=rt3c[:], in0=rt3c[:], in1=w2b, op=ALU.mult, sem="sVE")
        vt2 = adv("sVE", 1)
        rec("vector", _wait, sem="sVE", val=vt2)
        rec("vector", _tt, out=wgt01[:], in0=rt3b[:], in1=rt3c[:], op=ALU.add, sem="sVE")
        v_route = adv("sVE", 1)

        rec("sync", _wait, sem="sQS", val=v_qwscr)
        rec("sync", _wait, sem="sPE", val=max(last["e_pe_done"], last["kv_user"]))
        rec("sync", _wait, sem="sOW", val=C["sOW"])
        if cfg["debug"]:
            rec("sync", _wait, sem="sOUT", val=C["sOUT"])
        rec("sync", _dma, out=kvw_sb[:],
            in_=qw_scr.rearrange("(cg p) o -> p cg o", p=128), sem="sQW")
        v_qw = adv("sQW", 16)
        v_qh = [0] * H
        v_qe = [0] * H
        v_fm = [0] * H
        v_qp = [0] * H
        v_ae = [0] * H

        def b_qhat(h):
            kb = 0 if h % 2 == 0 else 4
            rec("tensor", _wait, sem="sVE", val=v_xnfm[-1])
            rec("tensor", _wait, sem="sQW", val=v_qw)
            if h >= 2:
                rec("tensor", _wait, sem="sSC", val=v_qe[h - 2])
            for cg in range(CG):
                rec("tensor", _mm, out=bank(kb), lhsT=kvw_sb[:, cg, h * 128:(h + 1) * 128],
                    rhs=xn_fm[:, cg, :], start=(cg == 0), stop=(cg == CG - 1),
                    sem="sPE" if cg == CG - 1 else None)
            v_qh[h] = adv("sPE", 1)

        def b_ke(h):
            kb = 0 if h % 2 == 0 else 4
            rec("scalar", _wait, sem="sPE", val=v_qh[h])
            if h == 0:
                rec("scalar", _wait, sem="sOUT", val=C["sOUT"])
            rec("scalar", _scp, out=Khat2[:, h % 2, :], in_=bank(kb), sem="sSC")
            v_qe[h] = adv("sSC", 1)

        def b_fm(h):
            rec("tensor", _wait, sem="sSC", val=v_qe[h])
            if h >= 1:
                rec("tensor", _wait, sem="sSC", val=v_ae[h - 1])
                rec("tensor", _wait, sem="sVE", val=v_qp[h - 1])
            rec("tensor", _mm, out=bank(1), lhsT=fm_sb[:, 0, :], rhs=Khat2[:, h % 2, :],
                start=True, stop=True)
            rec("tensor", _mm, out=bank(5), lhsT=fm_sb[:, 1, :], rhs=Khat2[:, h % 2, :],
                start=True, stop=True, sem="sPE")
            v_fm[h] = adv("sPE", 1)

        def b_fme(h):
            rec("scalar", _wait, sem="sPE", val=v_fm[h])
            if h >= 1:
                rec("scalar", _wait, sem="sVE", val=v_qp[h - 1])
            rec("scalar", _mul, out=A1s[:], in_=bank(1), mul=S_ATTN, sem="sSC")
            v_ae[h] = adv("sSC", 1)
            rec("vector", _wait, sem="sSC", val=v_ae[h])
            rec("vector", _tt, out=Qp_fm[:, h, :], in0=A1s[:], in1=bank(5), op=ALU.mult, sem="sVE")
            v_qp[h] = adv("sVE", 1)

        v_h1 = []
        e1 = dict(next=0, w0=0)
        E1BANKS = [2, 3, 6, 7]

        def emit_e1_group():
            og_abs = e1["next"]
            if og_abs >= F1 // 128:
                return
            e1["next"] += 1
            if og_abs % 4 == 0:
                os_ = og_abs // 4
                rec("sync", _wait, sem="sPE", val=max(last["w0_user"], last["e_pe_done"]))
                rec("sync", _wait, sem="sPS", val=v_w0scr)
                rec("sync", _dma, out=w01_sb[:],
                    in_=w0_scr[:, os_ * 512:(os_ + 1) * 512].rearrange(
                        "(cg p) o -> p cg o", p=128), sem="sW0")
                e1["w0"] = adv("sW0", 16)
            bk = E1BANKS[og_abs % 4]
            rec("tensor", _wait, sem="sW0", val=e1["w0"])
            if og_abs == 0:
                rec("tensor", _wait, sem="sVE", val=v_route)
            if len(v_h1) >= 4:
                rec("tensor", _wait, sem="sSC", val=v_h1[-4])
            for cg in range(CG):
                rec("tensor", _mm, out=bank(bk),
                    lhsT=w01_sb[:, cg, (og_abs % 4) * 128:(og_abs % 4) * 128 + 128],
                    rhs=xn_fm[:, cg, :], start=(cg == 0), stop=(cg == CG - 1),
                    sem="sPE" if cg == CG - 1 else None)
            vm = adv("sPE", 1)
            last["w0_user"] = vm
            rec("scalar", _wait, sem="sPE", val=vm)
            rec("scalar", _act, out=H1[:, og_abs, :], in_=bank(bk), func=AF.Relu,
                scale=scol[:, 0:1], sem="sSC")
            v_h1.append(adv("sSC", 1))

        b_qhat(0); b_ke(0); b_qhat(1); b_ke(1)
        for h in range(H):
            b_fm(h)
            if h + 2 < H:
                b_qhat(h + 2); b_ke(h + 2)
            b_fme(h)
            for _ in range(4):
                emit_e1_group()
        while e1["next"] < F1 // 128:
            emit_e1_group()
        v_h1_done = v_h1[-1]
        v_qp_done = v_qp[-1]
        last["kv_user"] = max(last["kv_user"], v_fm[-1])

        fil = dict(ring=0, cons=[0, 0], vcons=[0, 0], next_h2=0, v_h2_last=0,
                   w1_user=last["w0_user"], slot=0, nslots=nexp * 10 + NTC)

        def maybe_fill():
            fil["slot"] += 1
            target = (F1 // 128) * fil["slot"] // fil["nslots"]
            while fil["next_h2"] < target:
                emit_h2_slice()

        def emit_h2_slice():
            o2g_abs = fil["next_h2"]
            if o2g_abs >= F1 // 128:
                return
            fil["next_h2"] += 1
            off = 3584
            rec("sync", _wait, sem="sPE", val=fil["w1_user"])
            rec("sync", _wait, sem="sPS", val=v_w1scr)
            rec("sync", _dma, out=w1v_sb[:],
                in_=w1_scr[:, :, o2g_abs * 128:(o2g_abs + 1) * 128].rearrange("g p o -> p g o"),
                sem="sW1A")
            v_w1 = adv("sW1A", 16)
            rec("tensor", _wait, sem="sW1A", val=v_w1)
            rec("tensor", _wait, sem="sSC", val=max(v_h1_done, fil["cons"][0]))
            rec("tensor", _wait, sem="sVE", val=fil["vcons"][0])
            for o1g in range(F1 // 128):
                rec("tensor", _mm, out=pflat(off, 512),
                    lhsT=w1v_sb[:, o1g, :],
                    rhs=H1[:, o1g, :], start=(o1g == 0), stop=(o1g == F1 // 128 - 1),
                    sem="sPE" if o1g == F1 // 128 - 1 else None)
            vm = adv("sPE", 1)
            fil["w1_user"] = vm
            rec("scalar", _wait, sem="sPE", val=vm)
            rec("scalar", _act, out=H2[:, o2g_abs, :], in_=pflat(off, 512), func=AF.Relu,
                scale=scol[:, 1:2], sem="sSC")
            fil["cons"][0] = adv("sSC", 1)
            fil["v_h2_last"] = fil["cons"][0]

        for ei in range(nexp):
            rec("sync", _wait, sem="sG2", val=v_kvscr_e[ei])
            rec("sync", _wait, sem="sPE", val=max(last["kv_user"], last["e_pe_done"]))
            rec("sync", _dma, out=kvw_sb[:],
                in_=kv_scr[ei, 0].rearrange("(cg p) o -> p cg o", p=128), sem="sKV")
            v_kv = adv("sKV", 16)
            kw_v = kvw_sb
            vw_v = kvw_sb


            v_kh = [0] * H
            v_ke = [0] * H
            v_fmk = [0] * H
            v_kph = [0] * H
            v_aek = [0] * H
            rec("tensor", _wait, sem="sVE", val=max(v_qp_done, last["c5_ve_done"]))
            rec("tensor", _wait, sem="sSC", val=v_h1_done)

            def c_khat(h):
                kb = [0, 4, 6][h % 3]
                rec("tensor", _wait, sem="sKV", val=v_kv)
                if h >= 3:
                    rec("tensor", _wait, sem="sSC", val=v_ke[h - 3])
                for cg in range(CG):
                    rec("tensor", _mm, out=bank(kb), lhsT=kw_v[:, cg, h * 128:(h + 1) * 128],
                        rhs=xn_fm[:, cg, :], start=(cg == 0), stop=(cg == CG - 1),
                        sem="sPE" if cg == CG - 1 else None)
                v_kh[h] = adv("sPE", 1)

            def c_ke(h):
                kb = [0, 4, 6][h % 3]
                rec("scalar", _wait, sem="sPE", val=v_kh[h])
                rec("scalar", _scp, out=Khat2[:, h % 3, :], in_=bank(kb), sem="sSC")
                v_ke[h] = adv("sSC", 1)

            def c_fm(h):
                rec("tensor", _wait, sem="sSC", val=v_ke[h])
                if h >= 1:
                    rec("tensor", _wait, sem="sSC", val=v_aek[h - 1])
                    rec("tensor", _wait, sem="sVE", val=v_kph[h - 1])
                rec("tensor", _mm, out=bank(1), lhsT=fm_sb[:, 2, :], rhs=Khat2[:, h % 3, :],
                    start=True, stop=True)
                rec("tensor", _mm, out=bank(5), lhsT=fm_sb[:, 3, :], rhs=Khat2[:, h % 3, :],
                    start=True, stop=True)
                for tc in range(NTC):
                    rec("tensor", _mm, out=bank(2, tc * 128, 128),
                        lhsT=Khat2[:, h % 3, tc * 128:(tc + 1) * 128], rhs=fm_sb[:, 2, :],
                        start=True, stop=True)
                for tc in range(NTC):
                    rec("tensor", _mm, out=bank(3, tc * 128, 128),
                        lhsT=Khat2[:, h % 3, tc * 128:(tc + 1) * 128], rhs=fm_sb[:, 3, :],
                        start=True, stop=True, sem="sPE" if tc == NTC - 1 else None)
                v_fmk[h] = adv("sPE", 1)

            def c_fme(h):
                rec("scalar", _wait, sem="sPE", val=v_fmk[h])
                if h >= 1:
                    rec("scalar", _wait, sem="sVE", val=v_kph[h - 1])
                rec("scalar", _scp, out=A1s[:], in_=bank(1), sem="sSC")
                rec("scalar", _scp, out=A1s2[:], in_=bank(2), sem="sSC")
                adv("sSC", 1)
                v_aek[h] = adv("sSC", 1)
                rec("vector", _wait, sem="sSC", val=v_aek[h])
                rec("vector", _tt, out=Kp_fm[:, h, :], in0=A1s[:], in1=bank(5), op=ALU.mult,
                    sem="sVE")
                rec("vector", _tt, out=Kp_tm[:, :, h, :],
                    in0=A1s2[:].rearrange("p (a d) -> p a d", a=NTC),
                    in1=bank(3).rearrange("p (a d) -> p a d", a=NTC), op=ALU.mult, sem="sVE")
                adv("sVE", 1)
                v_kph[h] = adv("sVE", 1)

            c_khat(0); c_ke(0); c_khat(1); c_ke(1); c_khat(2); c_ke(2)
            for h in range(H):
                c_fm(h)
                if h + 3 < H:
                    c_khat(h + 3); c_ke(h + 3)
                c_fme(h)
                if h % 2 == 1:
                    maybe_fill()

            rec("sync", _wait, sem="sPE", val=v_kh[H - 1])
            rec("sync", _dma, out=kvw_sb[:],
                in_=kv_scr[ei, 1].rearrange("(cg p) o -> p cg o", p=128), sem="sKV")
            v_kv2 = adv("sKV", 16)

            rec("vector", _wait, sem="sVE", val=v_route)
            v_vp = []
            v_vp_pe = 0
            for tc in range(NTC):
                for vs in range(2):
                    bk = (tc * 2 + vs) % 2
                    if tc == 0 and vs == 0:
                        rec("tensor", _wait, sem="sVE", val=v_kph[-1])
                        rec("tensor", _wait, sem="sKV", val=v_kv2)
                    if len(v_vp) >= 2:
                        rec("tensor", _wait, sem="sVE", val=v_vp[-2])
                    for cg in range(CG):
                        rec("tensor", _mm, out=bank(bk),
                            lhsT=xn_fm[:, cg, tc * 128:(tc + 1) * 128],
                            rhs=vw_v[:, cg, vs * 512:(vs + 1) * 512],
                            start=(cg == 0), stop=(cg == CG - 1),
                            sem="sPE" if cg == CG - 1 else None)
                    vvh = adv("sPE", 1)
                    v_vp_pe = vvh
                    rec("vector", _wait, sem="sPE", val=vvh)
                    rec("vector", _ts, out=Vp_tm[:, tc, vs * 4:(vs + 1) * 4, :],
                        in0=bank(bk).rearrange("p (a d) -> p a d", a=4),
                        s1=mask01[:, tc, ei:ei + 1], op0=ALU.mult, sem="sVE")
                    v_vp.append(adv("sVE", 1))
                if tc % 2 == 1:
                    maybe_fill()
            v_vp_done = v_vp[-1]
            last["kv_user"] = v_vp_pe

            has_state = not first_seg
            rec("tensor", _wait, sem="sVE", val=v_vp_done)
            rec("tensor", _wait, sem="sSC", val=C["sSC"])
            v_sbfcopy = C["sSC"]
            if not first_seg:
                rec("sync", _wait, sem="sSV", val=C["sSV"])
                rec("sync", _wait, sem="sSC", val=C["sSC"])
                rec("sync", _wait, sem="sVE", val=C["sVE"])
                rec("sync", _dma, out=S_mast[:], in_=st_scr[ei], sem="sST")
                v_strest = adv("sST", 16)
                rec("scalar", _wait, sem="sST", val=v_strest)
                rec("scalar", _wait, sem="sPE", val=v_vp_pe)
                rec("scalar", _scp, out=S_bf[:].rearrange("p h v -> p (h v)"),
                    in_=S_mast[:].rearrange("p h v -> p (h v)"), sem="sSC")
                v_sbfcopy = adv("sSC", 1)

            p_even = [0, 512, 1024, 896]
            p_odd = [1536, 2048, 1280, 2432]
            a_off = [0, 512, 896, 1152]
            slot_n = [512, 384, 256, 128]
            v_aT = [0] * H
            v_omm = [0] * H
            v_dtt = [0] * H
            v_oc = [0] * H
            v_oev = [0] * H
            v_dadd = [0] * H

            def c5_delta(j):
                off = 3584
                rec("tensor", _wait, sem="sSC", val=fil["cons"][0])
                rec("tensor", _wait, sem="sVE", val=fil["vcons"][0])
                for jc in range(NTC):
                    rec("tensor", _mm, out=pflat(off, 128),
                        lhsT=Kp_tm[:, jc, j, :], rhs=Vp_tm[:, jc, j, :],
                        start=(jc == 0), stop=(jc == NTC - 1),
                        sem="sPE" if jc == NTC - 1 else None)
                vm = adv("sPE", 1)
                rec("vector", _wait, sem="sPE", val=vm)
                if j == 0:
                    rec("vector", _wait, sem="sSC", val=v_sbfcopy)
                    if first_seg:
                        rec("vector", _wait, sem="sSV", val=C["sSV"])
                if first_seg:
                    rec("vector", _cp, out=S_mast[:, j, :], in_=pflat(off, 128), sem="sVE")
                else:
                    rec("vector", _stt, out=S_mast[:, j, :], in0=pflat(off, 128),
                        scalar=1.0, in1=S_mast[:, j, :], op0=ALU.mult, op1=ALU.add, sem="sVE")
                fil["vcons"][0] = adv("sVE", 1)
                v_dadd[j] = fil["vcons"][0]

            def c5_av(j):
                par = j % 2
                o_off = 2560 if par == 0 else 3072
                rec("tensor", _wait, sem="sVE", val=v_dtt[j])
                rec("tensor", _wait, sem="sSC", val=v_oc[j])
                if j >= 2:
                    rec("tensor", _wait, sem="sVE", val=v_oev[j - 2])
                if has_state and j == 0:
                    rec("tensor", _wait, sem="sSC", val=v_sbfcopy)
                for il in range(NTC):
                    nmm = il + 1 + (1 if has_state else 0)
                    k = 0
                    for jc in range(il + 1):
                        k += 1
                        rec("tensor", _mm, out=pflat(o_off + il * 128, 128),
                            lhsT=aT_big[:, par, a_off[jc] + (il - jc) * 128:
                                        a_off[jc] + (il - jc) * 128 + 128],
                            rhs=Vp_tm[:, jc, j, :], start=(k == 1), stop=(k == nmm),
                            sem="sPE" if (il == NTC - 1 and k == nmm) else None)
                    if has_state:
                        k += 1
                        rec("tensor", _mm, out=pflat(o_off + il * 128, 128),
                            lhsT=Qp_fm[:, j, il * 128:(il + 1) * 128], rhs=S_bf[:, j, :],
                            start=False, stop=True,
                            sem="sPE" if (il == NTC - 1) else None)
                v_omm[j] = adv("sPE", 1)
                rec("vector", _wait, sem="sPE", val=v_omm[j])
                for il in range(NTC):
                    wcol = wgt01[:, il, ei:ei + 1]
                    if ei == 0:
                        rec("vector", _ts, out=O_acc[:, il, j * 128:(j + 1) * 128],
                            in0=pflat(o_off + il * 128, 128), s1=wcol, op0=ALU.mult, sem="sVE")
                    else:
                        rec("vector", _stt, out=O_acc[:, il, j * 128:(j + 1) * 128],
                            in0=pflat(o_off + il * 128, 128), scalar=wcol,
                            in1=O_acc[:, il, j * 128:(j + 1) * 128],
                            op0=ALU.mult, op1=ALU.add, sem="sVE")
                adv("sVE", 3)
                v_oev[j] = adv("sVE", 1)

            for h in range(H):
                par = h % 2
                poff = p_even if par == 0 else p_odd
                if h >= 2:
                    rec("tensor", _wait, sem="sVE", val=v_dtt[h - 2])
                    rec("tensor", _wait, sem="sSC", val=v_oc[h - 2])
                for jc in (0, 1, 3):
                    rec("tensor", _mm, out=pflat(poff[jc], slot_n[jc]),
                        lhsT=Kp_fm[:, h, jc * 128:(jc + 1) * 128],
                        rhs=Qp_fm[:, h, jc * 128:512], start=True, stop=True)
                c5_delta(h)
                if h >= 1:
                    rec("tensor", _wait, sem="sVE", val=v_dtt[h - 1])
                    rec("tensor", _wait, sem="sSC", val=v_oc[h - 1])
                rec("tensor", _mm, out=pflat(poff[2], slot_n[2]),
                    lhsT=Kp_fm[:, h, 256:384], rhs=Qp_fm[:, h, 256:512],
                    start=True, stop=True, sem="sPE")
                v_aT[h] = adv("sPE", 1)
                rec("vector", _wait, sem="sPE", val=v_aT[h])
                for jc in range(NTC):
                    rec("vector", _tt, out=aT_big[:, par, a_off[jc]: a_off[jc] + 128],
                        in0=pflat(poff[jc], 128), in1=tri_s[:], op=ALU.mult,
                        sem="sVE")
                adv("sVE", 3)
                v_dtt[h] = adv("sVE", 1)
                rec("scalar", _wait, sem="sPE", val=v_aT[h])
                rec("scalar", _wait, sem="sVE", val=v_dtt[h])
                for jc, w in ((0, 384), (1, 256), (2, 128)):
                    rec("scalar", _scp,
                        out=aT_big[:, par, a_off[jc] + 128: a_off[jc] + 128 + w],
                        in_=pflat(poff[jc] + 128, w), sem="sSC")
                adv("sSC", 2)
                v_oc[h] = adv("sSC", 1)

                if h >= 1:
                    c5_av(h - 1)
                if h % 2 == 1:
                    maybe_fill()
            c5_av(H - 1)

            if not last_seg:
                rec("scalar", _wait, sem="sVE", val=v_dadd[H - 1])
                rec("scalar", _dma, out=st_scr[ei], in_=S_mast[:], sem="sSV")
                last["stsave"][ei] = adv("sSV", 16)
            last["c5_ve_done"] = C["sVE"]

        c_pe_done = C["sPE"]

        v_onrm_tr = [0] * NTC
        rec("vector", _wait, sem="sVE", val=v_oev[H - 1])
        vsq_d = 0
        for tc in range(NTC):
            o3 = O_acc[:, tc, :].rearrange("p (h v) -> p h v", h=H)
            if vsq_d:
                rec("vector", _wait, sem="sVE", val=C["sVE"])
            rec("vector", _tt, out=tmp_sq[:].rearrange("p (h v) -> p h v", h=H),
                in0=o3, in1=o3, op=ALU.mult, sem="sVE")
            vsq_d = adv("sVE", 1)
            rec("vector", _wait, sem="sVE", val=vsq_d)
            rec("vector", _red, out=rms_s[:, tc, :],
                in_=tmp_sq[:].rearrange("p (h v) -> p h v", h=H), op=ALU.add, sem="sVE")
            adv("sVE", 1)
        vred_d = C["sVE"]
        rec("scalar", _wait, sem="sVE", val=vred_d)
        rec("scalar", _act, out=rms_s[:].rearrange("p a h -> p (a h)"),
            in_=rms_s[:].rearrange("p a h -> p (a h)"), func=AF.Ln, scale=1.0 / DK,
            bias=1e-5, sem="sSC")
        vln_d = adv("sSC", 1)
        rec("scalar", _wait, sem="sSC", val=vln_d)
        rec("scalar", _act, out=rms_r[:].rearrange("p a h -> p (a h)"),
            in_=rms_s[:].rearrange("p a h -> p (a h)"), func=AF.Exp, scale=-0.5, sem="sSC")
        vrs_d = adv("sSC", 1)
        for tc in range(NTC):
            maybe_fill()
            o3 = O_acc[:, tc, :].rearrange("p (h v) -> p h v", h=H)
            rec("vector", _wait, sem="sSC", val=vrs_d)
            if tc > 0:
                rec("vector", _wait, sem="sPE", val=v_onrm_tr[tc - 1])
            rb = rms_r[:, tc, :].unsqueeze(2).broadcast_to([128, H, DK])
            rec("vector", _tt, out=Onrm[:].rearrange("p (h v) -> p h v", h=H),
                in0=o3, in1=rb, op=ALU.mult, sem="sVE")
            vON = adv("sVE", 1)
            rec("tensor", _wait, sem="sVE", val=vON)
            rec("tensor", _wait, sem="sVE", val=last["b2_free"])
            for g in range(CG):
                rec("tensor", _tr, out=bankbf(2, g * 128, 128),
                    in_=Onrm[:, g * 128:(g + 1) * 128], ident=identb[:],
                    sem="sPE" if g == CG - 1 else None)
            v_onrm_tr[tc] = adv("sPE", 1)
            rec("vector", _wait, sem="sPE", val=v_onrm_tr[tc])
            rec("vector", _cp, out=Onrm_fm[:, :, tc * 128:(tc + 1) * 128],
                in_=bankbf(2, 0, 1024).rearrange("p (g t) -> p g t", g=CG), sem="sVE")
            vev = adv("sVE", 1)
            last["b2_free"] = vev
        v_onrm_done = last["b2_free"]

        while fil["next_h2"] < F1 // 128:
            emit_h2_slice()
        v_h2_done = fil["v_h2_last"]

        rec("sync", _wait, sem="sPE", val=c_pe_done)
        rec("sync", _wait, sem="sOW", val=C["sOW"])
        rec("sync", _wait, sem="sPS", val=v_owscr)
        rec("sync", _dma, out=owT_sb[:],
            in_=ow_scr.rearrange("(cg p) o -> p cg o", p=128), sem="sOW")
        v_ow = adv("sOW", 16)
        rec("tensor", _wait, sem="sSC", val=v_h2_done)
        rec("tensor", _wait, sem="sVE", val=v_onrm_done)
        rec("tensor", _wait, sem="sOW", val=v_ow)
        for tc in range(NTC):
            for g in range(CG):
                for ds in range(2):
                    rec("tensor", _mm, out=pflat(tc * 1024 + ds * 512, 512),
                        lhsT=Onrm_fm[:, g, tc * 128:(tc + 1) * 128],
                        rhs=owT_sb[:, g, ds * 512:(ds + 1) * 512],
                        start=(g == 0), stop=False,
                        sem="sPE" if (tc == NTC - 1 and g == CG - 1 and ds == 1) else None)
        v_attn_mm = adv("sPE", 1)
        for o2s in range(8):
            par = o2s % 2
            w2sem = "sW2" if par == 0 else "sW2B"
            rec("sync", _wait, sem="sPE", val=max(last["w2_user"][par], c_pe_done))
            rec("sync", _wait, sem="sPS", val=v_w2scr)
            rec("sync", _dma, out=w2_sb[:, par],
                in_=w2_scr[o2s * 512:(o2s + 1) * 512, :].rearrange("(s p) d -> p s d", p=128),
                sem=w2sem)
            v_w2 = adv(w2sem, 16)
            rec("tensor", _wait, sem=w2sem, val=v_w2)
            lastslice = (o2s == 7)
            for tc in range(NTC):
                for sub in range(4):
                    o2g = o2s * 4 + sub
                    for ds in range(2):
                        is_last = lastslice and sub == 3
                        rec("tensor", _mm,
                            out=pflat(tc * 1024 + ds * 512, 512),
                            lhsT=H2[:, o2g, tc * 128:(tc + 1) * 128],
                            rhs=w2_sb[:, par, sub, ds * 512:(ds + 1) * 512],
                            start=False, stop=is_last,
                            sem="sPE" if (tc == NTC - 1 and sub == 3 and ds == 1) else None)
            last["w2_user"][par] = adv("sPE", 1)
        v_e3 = C["sPE"]
        last["w2_user"] = [v_e3, v_e3]
        for tc in range(NTC):
            par = tc % 2
            rec("vector", _wait, sem="sPE", val=v_e3)
            if tc >= 2:
                rec("vector", _wait, sem="sOUT", val=C["sOUT"])
            rec("vector", _tt, out=out_sb[:, par, :], in0=pflat(tc * 1024, 1024),
                in1=x_seg[:, tc, :], op=ALU.add, sem="sVE")
            vev = adv("sVE", 1)
            rec("scalar", _wait, sem="sVE", val=vev)
            rec("scalar", _dma, out=out_d[b, t0 + tc * 128: t0 + (tc + 1) * 128, :],
                in_=out_sb[:, par, :], sem="sOUT")
            last["out_user"][par] = adv("sOUT", 16)
        last["e3_evict"] = C["sVE"]
        last["b2_free"] = C["sVE"]
        last["e_pe_done"] = v_e3

        if cfg["debug"] and seg == 0:
            rec("scalar", _wait, sem="sVE", val=C["sVE"])
            for name, srcap in [("xn_fm", xn_fm[:]), ("wgt01", wgt01[:]), ("mask01", mask01[:]),
                              ("Qp", Qp_fm[:]), ("Oacc", O_acc[:]), ("H1", H1[:]),
                              ("H2", H2[:]), ("Onrmfm", Onrm_fm[:])]:
                rec("scalar", _dma, out=dbg[name], in_=srcap, sem="sOUT")
                adv("sOUT", 16)

    rec("scalar", _wait, sem="sOUT", val=C["sOUT"])
    rec("scalar", _wait, sem="sSV", val=C["sSV"])

    chk = {k: 0 for k in SEMS}
    for _en, _fn, _kw in REC:
        s = _kw.get("sem")
        if s and _fn is not _wait:
            chk[s] += 16 if _fn is _dma else 1
    chk["sGP"] += 15
    assert chk == C, f"sem count mismatch: {chk} vs {C}"

    with es:
        block = nc.Block()
        with block:
            for eng_name in ["sync", "gpsimd", "tensor", "vector", "scalar"]:
                def body(e, _n=eng_name):
                    for en, fn, kw in REC:
                        if en == _n:
                            fn(e, **kw)
                getattr(block, eng_name)(body)
    return nc


def kernel(**inputs):
    cfg = dict(DEFAULT_CFG)
    x = np.ascontiguousarray(inputs["L0_attn_x"])
    n_cores = 8
    shared = make_shared_inputs(inputs)
    nc = build_program(cfg)
    in_maps = []
    for c in range(n_cores):
        m = dict(shared)
        m["x"] = np.ascontiguousarray(x[2 * c: 2 * c + 2])
        in_maps.append(m)
    res = run_bass_kernel_spmd(nc, in_maps, core_ids=list(range(n_cores)))
    return np.concatenate([r["out"] for r in res.results], axis=0)


def make_shared_inputs(inputs):
    return dict(
        gate_wT=np.ascontiguousarray(inputs["L0_attn_gate_w"].T),
        q_wT=np.ascontiguousarray(inputs["L0_attn_q_w"].T),
        k_wT=np.ascontiguousarray(np.transpose(inputs["L0_attn_k_w"], (0, 2, 1))),
        v_wT=np.ascontiguousarray(np.transpose(inputs["L0_attn_v_w"], (0, 2, 1))),
        fmq1T=np.ascontiguousarray(inputs["L0_attn_fmq1_w"].T),
        fmq2T=np.ascontiguousarray(inputs["L0_attn_fmq2_w"].T),
        fmk1T=np.ascontiguousarray(inputs["L0_attn_fmk1_w"].T),
        fmk2T=np.ascontiguousarray(inputs["L0_attn_fmk2_w"].T),
        norm_w=np.ascontiguousarray(np.asarray(inputs["L0_attn_norm_w"]).reshape(DK, 1)),
        o_wT=np.ascontiguousarray(inputs["L0_attn_o_w"].T),
        k0=np.ascontiguousarray(inputs["L0_ffn_k0"]).view(np.uint8),
        k1=np.ascontiguousarray(inputs["L0_ffn_k1"]).view(np.uint8),
        k2=np.ascontiguousarray(inputs["L0_ffn_k2"]).view(np.uint8),
        s0=np.asarray(inputs["L0_ffn_s0"], dtype=np.float32).reshape(1, 1),
        s1=np.asarray(inputs["L0_ffn_s1"], dtype=np.float32).reshape(1, 1),
        s2=np.asarray(inputs["L0_ffn_s2"], dtype=np.float32).reshape(1, 1),
    )


# version 10
# speedup vs baseline: 1.0397x; 1.0397x over previous
import numpy as np
import ml_dtypes
from contextlib import ExitStack
import concourse.bass as bass
import concourse.mybir as mybir
from concourse.bass_utils import run_bass_kernel_spmd

F32 = mybir.dt.float32
BF16 = mybir.dt.bfloat16
U8 = mybir.dt.uint8
AF = mybir.ActivationFunctionType
ALU = mybir.AluOpType
AX = mybir.AxisListType

D = 1024
E_EXP = 8
H = 8
DK = 128
SEG = 512
NTC = SEG // 128
CG = D // 128
S_ATTN = float(DK) ** -0.5
F1 = D * 4

DEFAULT_CFG = dict(nbatch=2, segs_per_batch=8, nexp=8, debug=False)


def build_program(cfg):
    nb = cfg["nbatch"]
    spb = cfg["segs_per_batch"]
    nexp = cfg["nexp"]
    seqlen = spb * SEG
    nseg = nb * spb

    nc = bass.Bass("TRN2", target_bir_lowering=False)

    x_d = nc.dram_tensor("x", [nb, seqlen, D], F32, kind="ExternalInput").ap()
    gate_d = nc.dram_tensor("gate_wT", [D, E_EXP], F32, kind="ExternalInput").ap()
    qwT_d = nc.dram_tensor("q_wT", [D, D], F32, kind="ExternalInput").ap()
    kwT_d = nc.dram_tensor("k_wT", [E_EXP, D, D], F32, kind="ExternalInput").ap()
    vwT_d = nc.dram_tensor("v_wT", [E_EXP, D, D], F32, kind="ExternalInput").ap()
    fm_d = [nc.dram_tensor(n, [DK, DK], F32, kind="ExternalInput").ap()
            for n in ["fmq1T", "fmq2T", "fmk1T", "fmk2T"]]
    nw_d = nc.dram_tensor("norm_w", [DK, 1], F32, kind="ExternalInput").ap()
    owT_d = nc.dram_tensor("o_wT", [D, D], F32, kind="ExternalInput").ap()
    k0_d = nc.dram_tensor("k0", [2 * D, F1], U8, kind="ExternalInput").ap()
    k1_d = nc.dram_tensor("k1", [F1, F1], U8, kind="ExternalInput").ap()
    k2_d = nc.dram_tensor("k2", [F1, D], U8, kind="ExternalInput").ap()
    s0_d = nc.dram_tensor("s0", [1, 1], F32, kind="ExternalInput").ap()
    s1_d = nc.dram_tensor("s1", [1, 1], F32, kind="ExternalInput").ap()
    s2_d = nc.dram_tensor("s2", [1, 1], F32, kind="ExternalInput").ap()
    out_d = nc.dram_tensor("out", [nb, seqlen, D], F32, kind="ExternalOutput").ap()

    kv_scr = nc.dram_tensor("kv_scr", [E_EXP, 2, D, D], BF16, kind="Internal").ap()
    w0_scr = nc.dram_tensor("w0_scr", [D, F1], BF16, kind="Internal").ap()
    w1_scr = nc.dram_tensor("w1_scr", [F1 // 128, 128, F1], BF16, kind="Internal").ap()
    w2_scr = nc.dram_tensor("w2_scr", [F1, D], BF16, kind="Internal").ap()
    st_scr = nc.dram_tensor("st_scr", [E_EXP, 128, H, DK], F32, kind="Internal").ap()
    qw_scr = nc.dram_tensor("qw_scr", [D, D], BF16, kind="Internal").ap()
    ow_scr = nc.dram_tensor("ow_scr", [D, D], BF16, kind="Internal").ap()

    dbg = {}
    if cfg["debug"]:
        for n, shape, dt in [
            ("xn_fm", [128, CG, SEG], BF16), ("wgt01", [128, NTC, E_EXP], F32),
            ("mask01", [128, NTC, E_EXP], F32), ("Qp", [128, H, SEG], BF16),
            ("Kp", [128, H, SEG], BF16), ("Kptm", [128, NTC, H, DK], BF16),
            ("Vp", [128, NTC, H, DK], BF16),
            ("Oacc", [128, NTC, D], F32), ("H1", [128, F1 // 128, SEG], BF16),
            ("H2", [128, F1 // 128, SEG], BF16), ("Onrmfm", [128, CG, SEG], BF16),
        ]:
            dbg[n] = nc.dram_tensor("dbg_" + n, shape, dt, kind="ExternalOutput").ap()

    es = ExitStack()
    SEMS = {}
    for s in ["sX", "sKV", "sQW", "sOW", "sST", "sW0", "sW1A", "sW1B", "sW2", "sW2B",
              "sP", "sOUT", "sSV", "sPS", "sG1", "sG2", "sPE", "sVE", "sSC", "sGP"]:
        SEMS[s] = es.enter_context(nc.semaphore(s))
    C = {k: 0 for k in SEMS}

    def adv(sem, k):
        C[sem] += k
        return C[sem]

    def sb(name, shape, dt):
        return es.enter_context(nc.sbuf_tensor(name, shape, dt))

    gate_sb = sb("gate_sb", [128, CG, E_EXP], BF16)
    fm_sb = sb("fm_sb", [128, 4, DK], BF16)
    identb = sb("identb", [128, 128], BF16)
    identf = sb("identf", [128, 128], F32)
    tri_s = sb("tri_s", [128, 128], F32)
    tri_b = sb("tri_b", [128, 128], BF16)
    onesr = sb("onesr", [1, 128], F32)
    scol = sb("scol", [128, 4], F32)
    nw_sb = sb("nw_sb", [128, 1], F32)
    ssc_sb = sb("ssc_sb", [1, 4], F32)
    S_mast = sb("S_mast", [128, H, DK], F32)
    S_bf = sb("S_bf", [128, H, DK], BF16)

    x_seg = sb("x_seg", [128, NTC, D], F32)
    xn_fm = sb("xn_fm", [128, CG, SEG], BF16)
    Qp_fm = sb("Qp_fm", [128, H, SEG], BF16)
    wgt01 = sb("wgt01", [128, NTC, E_EXP], F32)
    mask01 = sb("mask01", [128, NTC, E_EXP], F32)
    O_acc = sb("O_acc", [128, NTC, D], F32)
    Onrm_fm = sb("Onrm_fm", [128, CG, SEG], BF16)
    xn_tmp = sb("xn_tmp", [128, D], BF16)
    Onrm = xn_tmp
    tmp_sq = sb("tmp_sq", [128, D], F32)
    rt = sb("rt", [128, 24], F32)
    rt3a = sb("rt3a", [128, NTC, E_EXP], F32)
    rt3b = sb("rt3b", [128, NTC, E_EXP], F32)
    rt3c = sb("rt3c", [128, NTC, E_EXP], F32)
    rms_s = sb("rms_s", [128, NTC, H], F32)
    rms_r = sb("rms_r", [128, NTC, H], F32)
    cst = sb("cst", [128, 6], F32)
    _cvals = [1e-5, -0.5, 1.0 / DK, S_ATTN, 1.0, 0.0]
    for _ci, _cv in enumerate(_cvals):
        nc.const_aps.register(F32, _cv, cst[:, _ci:_ci + 1])

    ARENA_BYTES = 124928
    arena = sb("arena", [128, ARENA_BYTES // 2], BF16)

    def carve(off_bytes, shape, dt):
        nelem = 1
        for s in shape[1:]:
            nelem *= s
        esz = {BF16: 2, F32: 4, U8: 1}[dt]
        base = arena[:].bitcast(dt) if dt != BF16 else arena[:]
        v = base[:, off_bytes // esz: off_bytes // esz + nelem]
        if len(shape) > 2:
            dims = "".join(chr(ord('a') + i) for i in range(len(shape) - 1))
            v = v.rearrange(f"p ({' '.join(dims)}) -> p {' '.join(dims)}",
                            **{chr(ord('a') + i): shape[1 + i] for i in range(len(shape) - 1)})
        return v

    kvw_sb = carve(0, [128, CG, D], BF16)
    Khat2 = carve(16384, [128, 3, SEG], BF16)
    Kp_fm = carve(19456, [128, H, SEG], BF16)
    Kp_tm = carve(27648, [128, NTC, H, DK], BF16)
    Vp_tm = carve(35840, [128, NTC, H, DK], BF16)
    aT_big = carve(44032, [128, 2, 1280], BF16)
    A1s = carve(49152, [128, SEG], BF16)
    A1s2 = carve(50176, [128, SEG], BF16)
    H1 = carve(51200, [128, F1 // 128, SEG], BF16)
    H2 = carve(83968, [128, F1 // 128, SEG], BF16)
    w01_sb = carve(116736, [128, CG, 512], BF16)
    w1v_sb = carve(116736, [128, F1 // 128, 128], BF16)
    owT_sb = carve(0, [128, CG, D], BF16)
    out_sb = carve(16384, [128, 2, D], F32)
    w2_sb = carve(27648, [128, 2, 4, D], BF16)
    pstg_a = carve(0, [128, F1], U8)
    pstg_b = carve(4096, [128, F1], U8)
    pstg_f = carve(8192, [128, 512], F32)
    pstg_o = carve(10240, [128, F1], BF16)

    pALL = es.enter_context(nc.psum_tensor("pALL", [128, 4096], F32))

    def bank(i, lo=0, n=512):
        return pALL[:, i * 512 + lo: i * 512 + lo + n]

    def pflat(lo, n):
        return pALL[:, lo: lo + n]

    def bankbf(i, lo=0, n=1024):
        return pALL[:].bitcast(BF16)[:, i * 1024 + lo: i * 1024 + lo + n]

    def pbf(off, n):
        return pALL[:].bitcast(BF16)[:, off: off + n]

    REC = []

    def rec(eng, fn, **kw):
        REC.append((eng, fn, kw))

    def _dma(e, out=None, in_=None, sem=None):
        e.dma_start(out=out, in_=in_).then_inc(SEMS[sem], 16)

    def _wait(e, sem=None, val=0):
        e.wait_ge(SEMS[sem], val)

    def _mm(e, out=None, lhsT=None, rhs=None, start=True, stop=True, sem=None):
        m = e.matmul(out, lhsT, rhs, start=start, stop=stop)
        if sem:
            m.then_inc(SEMS[sem], 1)

    def _tr(e, out=None, in_=None, ident=None, sem=None):
        m = e.transpose(out, in_, ident)
        if sem:
            m.then_inc(SEMS[sem], 1)

    def _cp(e, out=None, in_=None, sem=None):
        m = e.tensor_copy(out=out, in_=in_)
        if sem:
            m.then_inc(SEMS[sem], 1)

    def _scp(e, out=None, in_=None, sem=None):
        m = e.copy(out=out, in_=in_)
        if sem:
            m.then_inc(SEMS[sem], 1)

    def _tt(e, out=None, in0=None, in1=None, op=None, sem=None):
        m = e.tensor_tensor(out=out, in0=in0, in1=in1, op=op)
        if sem:
            m.then_inc(SEMS[sem], 1)

    def _ts(e, out=None, in0=None, s1=None, s2=None, op0=None, op1=None, sem=None):
        kw = {}
        if op1 is not None:
            kw["op1"] = op1
        m = e.tensor_scalar(out=out, in0=in0, scalar1=s1, scalar2=s2, op0=op0, **kw)
        if sem:
            m.then_inc(SEMS[sem], 1)

    def _stt(e, out=None, in0=None, scalar=None, in1=None, op0=None, op1=None, sem=None):
        m = e.scalar_tensor_tensor(out=out, in0=in0, scalar=scalar, in1=in1, op0=op0, op1=op1)
        if sem:
            m.then_inc(SEMS[sem], 1)

    def _red(e, out=None, in_=None, op=None, sem=None):
        m = e.tensor_reduce(out=out, in_=in_, axis=AX.X, op=op)
        if sem:
            m.then_inc(SEMS[sem], 1)

    def _act(e, out=None, in_=None, func=None, bias=0.0, scale=1.0, sem=None):
        m = e.activation(out=out, in_=in_, func=func, bias=bias, scale=scale)
        if sem:
            m.then_inc(SEMS[sem], 1)

    def _mul(e, out=None, in_=None, mul=None, sem=None):
        m = e.mul(out=out, in_=in_, mul=mul)
        if sem:
            m.then_inc(SEMS[sem], 1)

    def _recip(e, out=None, in_=None, sem=None):
        m = e.reciprocal(out=out, in_=in_)
        if sem:
            m.then_inc(SEMS[sem], 1)

    def prep_consts(e):
        e.memset(identb[:], 0.0).then_inc(SEMS["sGP"], 1)
        e.memset(identf[:], 0.0).then_inc(SEMS["sGP"], 1)
        e.memset(tri_s[:], 1.0).then_inc(SEMS["sGP"], 1)
        e.memset(tri_b[:], 1.0).then_inc(SEMS["sGP"], 1)
        e.memset(onesr[:], 1.0).then_inc(SEMS["sGP"], 1)
        for _ci, _cv in enumerate([1e-5, -0.5, 1.0 / DK, S_ATTN, 1.0, 0.0]):
            e.memset(cst[:, _ci:_ci + 1], _cv).then_inc(SEMS["sGP"], 1)
        e.wait_ge(SEMS["sGP"], 11)
        e.affine_select(out=identb[:], in_=identb[:], compare_op=ALU.not_equal,
                        fill=1.0, base=0, pattern=[[-1, 128]],
                        channel_multiplier=1).then_inc(SEMS["sGP"], 1)
        e.affine_select(out=identf[:], in_=identf[:], compare_op=ALU.not_equal,
                        fill=1.0, base=0, pattern=[[-1, 128]],
                        channel_multiplier=1).then_inc(SEMS["sGP"], 1)
        e.affine_select(out=tri_s[:], in_=tri_s[:], compare_op=ALU.is_ge,
                        fill=0.0, base=0, pattern=[[1, 128]],
                        channel_multiplier=-1).then_inc(SEMS["sGP"], 1)
        e.affine_select(out=tri_b[:], in_=tri_b[:], compare_op=ALU.is_ge,
                        fill=0.0, base=0, pattern=[[1, 128]],
                        channel_multiplier=-1).then_inc(SEMS["sGP"], 1)
    rec("gpsimd", prep_consts)
    v_consts = adv("sGP", 15)

    small_loads = [
        (gate_sb[:], gate_d.rearrange("(cg p) e -> p cg e", p=128)),
    ]
    for i in range(4):
        small_loads.append((fm_sb[:, i, :], fm_d[i]))
    for dst, srcap in small_loads:
        rec("gpsimd", _dma, out=dst, in_=srcap, sem="sG1")
    v_small = adv("sG1", 16 * len(small_loads))

    rec("sync", _dma, out=nw_sb[:], in_=nw_d, sem="sP")
    rec("sync", _dma, out=ssc_sb[:, 0:1], in_=s0_d, sem="sP")
    rec("sync", _dma, out=ssc_sb[:, 1:2], in_=s1_d, sem="sP")
    rec("sync", _dma, out=ssc_sb[:, 2:3], in_=s2_d, sem="sP")
    v_scal = adv("sP", 64)
    rec("sync", _wait, sem="sP", val=v_scal)

    for e_i in range(nexp):
        rec("gpsimd", _dma, out=kv_scr[e_i, 0], in_=kwT_d[e_i], sem="sG2")
        rec("gpsimd", _dma, out=kv_scr[e_i, 1], in_=vwT_d[e_i], sem="sG2")
    rec("gpsimd", _dma, out=qw_scr, in_=qwT_d, sem="sG2")
    v_kvscr = adv("sG2", 16 * (2 * nexp + 1))

    rec("tensor", _wait, sem="sP", val=v_scal)
    rec("tensor", _wait, sem="sGP", val=v_consts)
    rec("tensor", _mm, out=bank(3, 0, 3), lhsT=onesr[:], rhs=ssc_sb[:, 0:3],
        start=True, stop=True, sem="sPE")
    v_scolmm = adv("sPE", 1)
    rec("vector", _wait, sem="sPE", val=v_scolmm)
    rec("vector", _ts, out=scol[:, 0:3], in0=bank(3, 0, 3), s1=2.0, op0=ALU.mult, sem="sVE")
    v1 = adv("sVE", 1)
    rec("vector", _wait, sem="sVE", val=v1)
    rec("vector", _ts, out=scol[:, 2:3], in0=bank(3, 2, 1), s1=4.0, op0=ALU.mult, sem="sVE")
    rec("vector", _ts, out=scol[:, 3:4], in0=bank(3, 2, 1), s1=2.0, op0=ALU.mult, sem="sVE")
    adv("sVE", 2)
    v_scol = C["sVE"]

    rec("vector", _wait, sem="sP", val=v_scal)
    for cg in range(CG):
        for os_ in range(2):
            srcap = owT_d.rearrange("(cg p) o -> cg p o", p=128)[cg][:, os_ * 512:(os_ + 1) * 512]
            rec("sync", _dma, out=pstg_f[:], in_=srcap, sem="sP")
            vload = adv("sP", 16)
            rec("vector", _wait, sem="sP", val=vload)
            rec("vector", _ts, out=pstg_o[:, 0:512], in0=pstg_f[:],
                s1=nw_sb[:, 0:1], op0=ALU.mult, sem="sVE")
            vconv = adv("sVE", 1)
            rec("scalar", _wait, sem="sVE", val=vconv)
            rec("scalar", _dma, out=ow_scr[cg * 128:(cg + 1) * 128, os_ * 512:(os_ + 1) * 512],
                in_=pstg_o[:, 0:512], sem="sPS")
            vstore = adv("sPS", 16)
            rec("sync", _wait, sem="sPS", val=vstore)

    for cg in range(CG):
        rec("sync", _dma, out=pstg_a[:], in_=k0_d[cg * 128:(cg + 1) * 128, :], sem="sP")
        rec("sync", _dma, out=pstg_b[:], in_=k0_d[D + cg * 128: D + (cg + 1) * 128, :], sem="sP")
        vload = adv("sP", 32)
        rec("vector", _wait, sem="sP", val=vload)
        rec("vector", _tt, out=pstg_o[:], in0=pstg_a[:], in1=pstg_b[:], op=ALU.subtract, sem="sVE")
        vconv = adv("sVE", 1)
        rec("scalar", _wait, sem="sVE", val=vconv)
        rec("scalar", _dma, out=w0_scr[cg * 128:(cg + 1) * 128, :], in_=pstg_o[:], sem="sPS")
        vstore = adv("sPS", 16)
        rec("sync", _wait, sem="sPS", val=vstore)

    for g in range(F1 // 128):
        rec("sync", _dma, out=pstg_a[:], in_=k1_d[g * 128:(g + 1) * 128, :], sem="sP")
        vload = adv("sP", 16)
        rec("vector", _wait, sem="sP", val=vload)
        rec("vector", _ts, out=pstg_o[:], in0=pstg_a[:], s1=2.0, s2=-1.0,
            op0=ALU.mult, op1=ALU.add, sem="sVE")
        vconv = adv("sVE", 1)
        rec("scalar", _wait, sem="sVE", val=vconv)
        rec("scalar", _dma, out=w1_scr[g], in_=pstg_o[:], sem="sPS")
        vstore = adv("sPS", 16)
        rec("sync", _wait, sem="sPS", val=vstore)

    rec("vector", _wait, sem="sVE", val=v_scol)
    for g in range(F1 // 128):
        rec("sync", _dma, out=pstg_a[:, 0:D], in_=k2_d[g * 128:(g + 1) * 128, :], sem="sP")
        vload = adv("sP", 16)
        rec("vector", _wait, sem="sP", val=vload)
        rec("vector", _ts, out=pstg_o[:, 0:D], in0=pstg_a[:, 0:D], s1=scol[:, 2:3],
            s2=scol[:, 3:4], op0=ALU.mult, op1=ALU.subtract, sem="sVE")
        vconv = adv("sVE", 1)
        rec("scalar", _wait, sem="sVE", val=vconv)
        rec("scalar", _dma, out=w2_scr[g * 128:(g + 1) * 128, :], in_=pstg_o[:, 0:D], sem="sPS")
        vstore = adv("sPS", 16)
        rec("sync", _wait, sem="sPS", val=vstore)

    v_prep_sS = C["sPS"]
    v_prep_sVE = C["sVE"]

    for eng in ["tensor", "vector", "scalar"]:
        rec(eng, _wait, sem="sG1", val=v_small)
        rec(eng, _wait, sem="sG2", val=v_kvscr)
        rec(eng, _wait, sem="sPS", val=v_prep_sS)
        rec(eng, _wait, sem="sVE", val=v_prep_sVE)
        rec(eng, _wait, sem="sGP", val=v_consts)

    last = dict(
        kv_user=0, e3_evict=0, b2_free=0, e_pe_done=0,
        w0_user=0, w1_user=[0, 0], w2_user=[0, 0], out_user=[0, 0],
        stsave=[0] * E_EXP, c5_ve_done=0,
    )

    for seg in range(nseg):
        b = seg // spb
        sib = seg % spb
        t0 = sib * SEG
        first_seg = (sib == 0)
        last_seg = (sib == spb - 1)

        rec("sync", _wait, sem="sVE", val=last["e3_evict"])
        rec("sync", _wait, sem="sX", val=C["sX"])
        if cfg["debug"] and seg > 0:
            for eng in ["vector", "scalar", "tensor"]:
                rec(eng, _wait, sem="sOUT", val=C["sOUT"])
        for tc in range(NTC):
            rec("sync", _dma, out=x_seg[:, tc, :],
                in_=x_d[b, t0 + tc * 128: t0 + (tc + 1) * 128, :], sem="sX")
            adv("sX", 16)
        xw_total = C["sX"]
        xw = [xw_total] * NTC

        v_xn_tr = [0] * NTC
        v_xnfm = [0] * NTC
        xall = x_seg[:].rearrange("p a d -> p (a d)")
        tmp_sq4 = O_acc[:].rearrange("p a d -> p (a d)")
        rec("vector", _wait, sem="sX", val=xw[-1])
        rec("vector", _wait, sem="sVE", val=last["e3_evict"])
        rec("vector", _tt, out=tmp_sq4, in0=xall, in1=xall, op=ALU.mult, sem="sVE")
        vsq = adv("sVE", 1)
        rec("vector", _red, out=rt[:, 0:NTC], in_=x_seg[:], op=ALU.add, sem="sVE")
        adv("sVE", 1)
        rec("vector", _wait, sem="sVE", val=vsq)
        rec("vector", _red, out=rt[:, 4:4 + NTC],
            in_=tmp_sq4.rearrange("p (a d) -> p a d", a=NTC), op=ALU.add, sem="sVE")
        vred = adv("sVE", 1)
        rec("vector", _wait, sem="sVE", val=vred)
        rec("vector", _ts, out=rt[:, 0:NTC], in0=rt[:, 0:NTC], s1=1.0 / D, op0=ALU.mult, sem="sVE")
        vm_ = adv("sVE", 1)
        rec("vector", _wait, sem="sVE", val=vm_)
        rec("vector", _ts, out=rt[:, 4:4 + NTC], in0=rt[:, 4:4 + NTC], s1=1.0 / D,
            op0=ALU.mult, sem="sVE")
        vq_ = adv("sVE", 1)
        rec("vector", _wait, sem="sVE", val=vq_)
        rec("vector", _tt, out=rt[:, 8:8 + NTC], in0=rt[:, 0:NTC], in1=rt[:, 0:NTC],
            op=ALU.mult, sem="sVE")
        vmm2 = adv("sVE", 1)
        rec("vector", _wait, sem="sVE", val=vmm2)
        rec("vector", _tt, out=rt[:, 8:8 + NTC], in0=rt[:, 4:4 + NTC], in1=rt[:, 8:8 + NTC],
            op=ALU.subtract, sem="sVE")
        vvar = adv("sVE", 1)
        rec("scalar", _wait, sem="sVE", val=vvar)
        rec("scalar", _act, out=rt[:, 8:8 + NTC], in_=rt[:, 8:8 + NTC], func=AF.Ln, bias=1e-5,
            sem="sSC")
        vln = adv("sSC", 1)
        rec("scalar", _wait, sem="sSC", val=vln)
        rec("scalar", _act, out=rt[:, 12:12 + NTC], in_=rt[:, 8:8 + NTC], func=AF.Exp, scale=-0.5,
            sem="sSC")
        vrs = adv("sSC", 1)
        rec("vector", _wait, sem="sSC", val=vrs)
        rec("vector", _tt, out=rt[:, 16:16 + NTC], in0=rt[:, 0:NTC], in1=rt[:, 12:12 + NTC],
            op=ALU.mult, sem="sVE")
        vmr = adv("sVE", 1)
        rec("vector", _wait, sem="sVE", val=vmr)
        rec("vector", _ts, out=rt[:, 16:16 + NTC], in0=rt[:, 16:16 + NTC], s1=-1.0,
            op0=ALU.mult, sem="sVE")
        vnm = adv("sVE", 1)
        for tc in range(NTC):
            rec("scalar", _wait, sem="sVE", val=vnm)
            if tc > 0:
                rec("scalar", _wait, sem="sPE", val=v_xn_tr[tc - 1])
            rec("scalar", _act, out=xn_tmp[:], in_=x_seg[:, tc, :], func=AF.Identity,
                scale=rt[:, 12 + tc:13 + tc], bias=rt[:, 16 + tc:17 + tc], sem="sSC")
            vxn = adv("sSC", 1)
            rec("tensor", _wait, sem="sSC", val=vxn)
            rec("tensor", _wait, sem="sVE", val=last["b2_free"])
            for g in range(CG):
                rec("tensor", _tr, out=bankbf(2, g * 128, 128),
                    in_=xn_tmp[:, g * 128:(g + 1) * 128], ident=identb[:],
                    sem="sPE" if g == CG - 1 else None)
            v_xn_tr[tc] = adv("sPE", 1)
            rec("vector", _wait, sem="sPE", val=v_xn_tr[tc])
            rec("vector", _cp,
                out=xn_fm[:, :, tc * 128:(tc + 1) * 128],
                in_=bankbf(2, 0, 1024).rearrange("p (g t) -> p g t", g=CG), sem="sVE")
            v_xnfm[tc] = adv("sVE", 1)
            last["b2_free"] = v_xnfm[tc]

        for tc in range(NTC):
            rec("tensor", _wait, sem="sVE", val=v_xnfm[tc])
            for cg in range(CG):
                rec("tensor", _mm, out=bank(3, tc * E_EXP, E_EXP),
                    lhsT=xn_fm[:, cg, tc * 128:(tc + 1) * 128],
                    rhs=gate_sb[:, cg, :], start=(cg == 0), stop=(cg == CG - 1),
                    sem="sPE" if cg == CG - 1 else None)
        v_lg = adv("sPE", NTC)
        lg3 = bank(3, 0, NTC * E_EXP).rearrange("p (a e) -> p a e", a=NTC)
        rec("vector", _wait, sem="sPE", val=v_lg)
        rec("vector", _red, out=rt[:, 10:10 + NTC], in_=lg3, op=ALU.max, sem="sVE")
        vm1 = adv("sVE", 1)
        rec("vector", _wait, sem="sVE", val=vm1)
        m1b = rt[:, 10:10 + NTC].unsqueeze(2).broadcast_to([128, NTC, E_EXP])
        rec("vector", _tt, out=rt3a[:], in0=lg3, in1=m1b, op=ALU.is_equal, sem="sVE")
        veq1 = adv("sVE", 1)
        rec("vector", _wait, sem="sVE", val=veq1)
        rec("vector", _stt, out=rt3b[:], in0=rt3a[:], scalar=-1e30, in1=lg3,
            op0=ALU.mult, op1=ALU.add, sem="sVE")
        vl2 = adv("sVE", 1)
        rec("vector", _wait, sem="sVE", val=vl2)
        rec("vector", _red, out=rt[:, 14:14 + NTC], in_=rt3b[:], op=ALU.max, sem="sVE")
        vm2 = adv("sVE", 1)
        rec("vector", _wait, sem="sVE", val=vm2)
        rec("vector", _tt, out=rt[:, 18:18 + NTC], in0=rt[:, 14:14 + NTC],
            in1=rt[:, 10:10 + NTC], op=ALU.subtract, sem="sVE")
        vd = adv("sVE", 1)
        m2b = rt[:, 14:14 + NTC].unsqueeze(2).broadcast_to([128, NTC, E_EXP])
        rec("vector", _tt, out=rt3c[:], in0=rt3b[:], in1=m2b, op=ALU.is_equal, sem="sVE")
        veq2 = adv("sVE", 1)
        rec("scalar", _wait, sem="sVE", val=vd)
        rec("scalar", _act, out=rt[:, 10:10 + NTC], in_=rt[:, 18:18 + NTC], func=AF.Exp, sem="sSC")
        vex = adv("sSC", 1)
        rec("vector", _wait, sem="sSC", val=vex)
        rec("vector", _wait, sem="sVE", val=veq2)
        rec("vector", _ts, out=rt[:, 14:14 + NTC], in0=rt[:, 10:10 + NTC], s1=1.0,
            op0=ALU.add, sem="sVE")
        vp1 = adv("sVE", 1)
        rec("vector", _wait, sem="sVE", val=vp1)
        rec("vector", _recip, out=rt[:, 18:18 + NTC], in_=rt[:, 14:14 + NTC], sem="sVE")
        vw1 = adv("sVE", 1)
        rec("vector", _wait, sem="sVE", val=vw1)
        rec("vector", _tt, out=rt[:, 14:14 + NTC], in0=rt[:, 18:18 + NTC],
            in1=rt[:, 10:10 + NTC], op=ALU.mult, sem="sVE")
        vw2 = adv("sVE", 1)
        rec("vector", _wait, sem="sVE", val=max(vw2, veq2))
        w1b = rt[:, 18:18 + NTC].unsqueeze(2).broadcast_to([128, NTC, E_EXP])
        w2b = rt[:, 14:14 + NTC].unsqueeze(2).broadcast_to([128, NTC, E_EXP])
        rec("vector", _tt, out=mask01[:], in0=rt3a[:], in1=rt3c[:], op=ALU.add, sem="sVE")
        adv("sVE", 1)
        rec("vector", _tt, out=rt3b[:], in0=rt3a[:], in1=w1b, op=ALU.mult, sem="sVE")
        vt1 = adv("sVE", 1)
        rec("vector", _wait, sem="sVE", val=vt1)
        rec("vector", _tt, out=rt3c[:], in0=rt3c[:], in1=w2b, op=ALU.mult, sem="sVE")
        vt2 = adv("sVE", 1)
        rec("vector", _wait, sem="sVE", val=vt2)
        rec("vector", _tt, out=wgt01[:], in0=rt3b[:], in1=rt3c[:], op=ALU.add, sem="sVE")
        v_route = adv("sVE", 1)

        rec("sync", _wait, sem="sG2", val=v_kvscr)
        rec("sync", _wait, sem="sPE", val=max(last["e_pe_done"], last["kv_user"]))
        rec("sync", _wait, sem="sOW", val=C["sOW"])
        if cfg["debug"]:
            rec("sync", _wait, sem="sOUT", val=C["sOUT"])
        rec("sync", _dma, out=kvw_sb[:],
            in_=qw_scr.rearrange("(cg p) o -> p cg o", p=128), sem="sQW")
        v_qw = adv("sQW", 16)
        v_qh = [0] * H
        v_qe = [0] * H
        v_fm = [0] * H
        v_qp = [0] * H
        v_ae = [0] * H

        def b_qhat(h):
            kb = 0 if h % 2 == 0 else 4
            rec("tensor", _wait, sem="sVE", val=v_xnfm[-1])
            rec("tensor", _wait, sem="sQW", val=v_qw)
            if h >= 2:
                rec("tensor", _wait, sem="sSC", val=v_qe[h - 2])
            for cg in range(CG):
                rec("tensor", _mm, out=bank(kb), lhsT=kvw_sb[:, cg, h * 128:(h + 1) * 128],
                    rhs=xn_fm[:, cg, :], start=(cg == 0), stop=(cg == CG - 1),
                    sem="sPE" if cg == CG - 1 else None)
            v_qh[h] = adv("sPE", 1)

        def b_ke(h):
            kb = 0 if h % 2 == 0 else 4
            rec("scalar", _wait, sem="sPE", val=v_qh[h])
            if h == 0:
                rec("scalar", _wait, sem="sOUT", val=C["sOUT"])
            rec("scalar", _scp, out=Khat2[:, h % 2, :], in_=bank(kb), sem="sSC")
            v_qe[h] = adv("sSC", 1)

        def b_fm(h):
            rec("tensor", _wait, sem="sSC", val=v_qe[h])
            if h >= 1:
                rec("tensor", _wait, sem="sSC", val=v_ae[h - 1])
                rec("tensor", _wait, sem="sVE", val=v_qp[h - 1])
            rec("tensor", _mm, out=bank(1), lhsT=fm_sb[:, 0, :], rhs=Khat2[:, h % 2, :],
                start=True, stop=True)
            rec("tensor", _mm, out=bank(5), lhsT=fm_sb[:, 1, :], rhs=Khat2[:, h % 2, :],
                start=True, stop=True, sem="sPE")
            v_fm[h] = adv("sPE", 1)

        def b_fme(h):
            rec("scalar", _wait, sem="sPE", val=v_fm[h])
            if h >= 1:
                rec("scalar", _wait, sem="sVE", val=v_qp[h - 1])
            rec("scalar", _mul, out=A1s[:], in_=bank(1), mul=S_ATTN, sem="sSC")
            v_ae[h] = adv("sSC", 1)
            rec("vector", _wait, sem="sSC", val=v_ae[h])
            rec("vector", _tt, out=Qp_fm[:, h, :], in0=A1s[:], in1=bank(5), op=ALU.mult, sem="sVE")
            v_qp[h] = adv("sVE", 1)

        v_h1 = []
        e1 = dict(next=0, w0=0)
        E1BANKS = [2, 3, 6, 7]

        def emit_e1_group():
            og_abs = e1["next"]
            if og_abs >= F1 // 128:
                return
            e1["next"] += 1
            if og_abs % 4 == 0:
                os_ = og_abs // 4
                rec("sync", _wait, sem="sPE", val=max(last["w0_user"], last["e_pe_done"]))
                rec("sync", _dma, out=w01_sb[:],
                    in_=w0_scr[:, os_ * 512:(os_ + 1) * 512].rearrange(
                        "(cg p) o -> p cg o", p=128), sem="sW0")
                e1["w0"] = adv("sW0", 16)
            bk = E1BANKS[og_abs % 4]
            rec("tensor", _wait, sem="sW0", val=e1["w0"])
            if og_abs == 0:
                rec("tensor", _wait, sem="sVE", val=v_route)
            if len(v_h1) >= 4:
                rec("tensor", _wait, sem="sSC", val=v_h1[-4])
            for cg in range(CG):
                rec("tensor", _mm, out=bank(bk),
                    lhsT=w01_sb[:, cg, (og_abs % 4) * 128:(og_abs % 4) * 128 + 128],
                    rhs=xn_fm[:, cg, :], start=(cg == 0), stop=(cg == CG - 1),
                    sem="sPE" if cg == CG - 1 else None)
            vm = adv("sPE", 1)
            last["w0_user"] = vm
            rec("scalar", _wait, sem="sPE", val=vm)
            rec("scalar", _act, out=H1[:, og_abs, :], in_=bank(bk), func=AF.Relu,
                scale=scol[:, 0:1], sem="sSC")
            v_h1.append(adv("sSC", 1))

        b_qhat(0); b_ke(0); b_qhat(1); b_ke(1)
        for h in range(H):
            b_fm(h)
            if h + 2 < H:
                b_qhat(h + 2); b_ke(h + 2)
            b_fme(h)
            for _ in range(4):
                emit_e1_group()
        while e1["next"] < F1 // 128:
            emit_e1_group()
        v_h1_done = v_h1[-1]
        v_qp_done = v_qp[-1]
        last["kv_user"] = max(last["kv_user"], v_fm[-1])

        fil = dict(ring=0, cons=[0, 0], vcons=[0, 0], next_h2=0, v_h2_last=0,
                   w1_user=last["w0_user"], slot=0, nslots=nexp * 10 + NTC)

        def maybe_fill():
            fil["slot"] += 1
            target = (F1 // 128) * fil["slot"] // fil["nslots"]
            while fil["next_h2"] < target:
                emit_h2_slice()

        def emit_h2_slice():
            o2g_abs = fil["next_h2"]
            if o2g_abs >= F1 // 128:
                return
            fil["next_h2"] += 1
            off = 3584
            rec("sync", _wait, sem="sPE", val=fil["w1_user"])
            rec("sync", _dma, out=w1v_sb[:],
                in_=w1_scr[:, :, o2g_abs * 128:(o2g_abs + 1) * 128].rearrange("g p o -> p g o"),
                sem="sW1A")
            v_w1 = adv("sW1A", 16)
            rec("tensor", _wait, sem="sW1A", val=v_w1)
            rec("tensor", _wait, sem="sSC", val=max(v_h1_done, fil["cons"][0]))
            rec("tensor", _wait, sem="sVE", val=fil["vcons"][0])
            for o1g in range(F1 // 128):
                rec("tensor", _mm, out=pflat(off, 512),
                    lhsT=w1v_sb[:, o1g, :],
                    rhs=H1[:, o1g, :], start=(o1g == 0), stop=(o1g == F1 // 128 - 1),
                    sem="sPE" if o1g == F1 // 128 - 1 else None)
            vm = adv("sPE", 1)
            fil["w1_user"] = vm
            rec("scalar", _wait, sem="sPE", val=vm)
            rec("scalar", _act, out=H2[:, o2g_abs, :], in_=pflat(off, 512), func=AF.Relu,
                scale=scol[:, 1:2], sem="sSC")
            fil["cons"][0] = adv("sSC", 1)
            fil["v_h2_last"] = fil["cons"][0]

        for ei in range(nexp):
            rec("sync", _wait, sem="sG2", val=v_kvscr)
            rec("sync", _wait, sem="sPE", val=max(last["kv_user"], last["e_pe_done"]))
            rec("sync", _dma, out=kvw_sb[:],
                in_=kv_scr[ei, 0].rearrange("(cg p) o -> p cg o", p=128), sem="sKV")
            v_kv = adv("sKV", 16)
            kw_v = kvw_sb
            vw_v = kvw_sb


            v_kh = [0] * H
            v_ke = [0] * H
            v_fmk = [0] * H
            v_kph = [0] * H
            v_aek = [0] * H
            rec("tensor", _wait, sem="sVE", val=max(v_qp_done, last["c5_ve_done"]))
            rec("tensor", _wait, sem="sSC", val=v_h1_done)

            def c_khat(h):
                kb = [0, 4, 6][h % 3]
                rec("tensor", _wait, sem="sKV", val=v_kv)
                if h >= 3:
                    rec("tensor", _wait, sem="sSC", val=v_ke[h - 3])
                for cg in range(CG):
                    rec("tensor", _mm, out=bank(kb), lhsT=kw_v[:, cg, h * 128:(h + 1) * 128],
                        rhs=xn_fm[:, cg, :], start=(cg == 0), stop=(cg == CG - 1),
                        sem="sPE" if cg == CG - 1 else None)
                v_kh[h] = adv("sPE", 1)

            def c_ke(h):
                kb = [0, 4, 6][h % 3]
                rec("scalar", _wait, sem="sPE", val=v_kh[h])
                rec("scalar", _scp, out=Khat2[:, h % 3, :], in_=bank(kb), sem="sSC")
                v_ke[h] = adv("sSC", 1)

            def c_fm(h):
                rec("tensor", _wait, sem="sSC", val=v_ke[h])
                if h >= 1:
                    rec("tensor", _wait, sem="sSC", val=v_aek[h - 1])
                    rec("tensor", _wait, sem="sVE", val=v_kph[h - 1])
                rec("tensor", _mm, out=bank(1), lhsT=fm_sb[:, 2, :], rhs=Khat2[:, h % 3, :],
                    start=True, stop=True)
                rec("tensor", _mm, out=bank(5), lhsT=fm_sb[:, 3, :], rhs=Khat2[:, h % 3, :],
                    start=True, stop=True)
                for tc in range(NTC):
                    rec("tensor", _mm, out=bank(2, tc * 128, 128),
                        lhsT=Khat2[:, h % 3, tc * 128:(tc + 1) * 128], rhs=fm_sb[:, 2, :],
                        start=True, stop=True)
                for tc in range(NTC):
                    rec("tensor", _mm, out=bank(3, tc * 128, 128),
                        lhsT=Khat2[:, h % 3, tc * 128:(tc + 1) * 128], rhs=fm_sb[:, 3, :],
                        start=True, stop=True, sem="sPE" if tc == NTC - 1 else None)
                v_fmk[h] = adv("sPE", 1)

            def c_fme(h):
                rec("scalar", _wait, sem="sPE", val=v_fmk[h])
                if h >= 1:
                    rec("scalar", _wait, sem="sVE", val=v_kph[h - 1])
                rec("scalar", _scp, out=A1s[:], in_=bank(1), sem="sSC")
                rec("scalar", _scp, out=A1s2[:], in_=bank(2), sem="sSC")
                adv("sSC", 1)
                v_aek[h] = adv("sSC", 1)
                rec("vector", _wait, sem="sSC", val=v_aek[h])
                rec("vector", _tt, out=Kp_fm[:, h, :], in0=A1s[:], in1=bank(5), op=ALU.mult,
                    sem="sVE")
                rec("vector", _tt, out=Kp_tm[:, :, h, :],
                    in0=A1s2[:].rearrange("p (a d) -> p a d", a=NTC),
                    in1=bank(3).rearrange("p (a d) -> p a d", a=NTC), op=ALU.mult, sem="sVE")
                adv("sVE", 1)
                v_kph[h] = adv("sVE", 1)

            c_khat(0); c_ke(0); c_khat(1); c_ke(1); c_khat(2); c_ke(2)
            for h in range(H):
                c_fm(h)
                if h + 3 < H:
                    c_khat(h + 3); c_ke(h + 3)
                c_fme(h)
                if h % 2 == 1:
                    maybe_fill()

            rec("sync", _wait, sem="sPE", val=v_kh[H - 1])
            rec("sync", _dma, out=kvw_sb[:],
                in_=kv_scr[ei, 1].rearrange("(cg p) o -> p cg o", p=128), sem="sKV")
            v_kv2 = adv("sKV", 16)

            rec("vector", _wait, sem="sVE", val=v_route)
            v_vp = []
            v_vp_pe = 0
            for tc in range(NTC):
                for vs in range(2):
                    bk = [0, 1, 5][(tc * 2 + vs) % 3]
                    if tc == 0 and vs == 0:
                        rec("tensor", _wait, sem="sKV", val=v_kv2)
                    if len(v_vp) < 3:
                        rec("tensor", _wait, sem="sVE", val=v_kph[-1])
                    if len(v_vp) >= 3:
                        rec("tensor", _wait, sem="sVE", val=v_vp[-3])
                    for cg in range(CG):
                        rec("tensor", _mm, out=bank(bk),
                            lhsT=xn_fm[:, cg, tc * 128:(tc + 1) * 128],
                            rhs=vw_v[:, cg, vs * 512:(vs + 1) * 512],
                            start=(cg == 0), stop=(cg == CG - 1),
                            sem="sPE" if cg == CG - 1 else None)
                    vvh = adv("sPE", 1)
                    v_vp_pe = vvh
                    rec("vector", _wait, sem="sPE", val=vvh)
                    rec("vector", _ts, out=Vp_tm[:, tc, vs * 4:(vs + 1) * 4, :],
                        in0=bank(bk).rearrange("p (a d) -> p a d", a=4),
                        s1=mask01[:, tc, ei:ei + 1], op0=ALU.mult, sem="sVE")
                    v_vp.append(adv("sVE", 1))
                if tc % 2 == 1:
                    maybe_fill()
            v_vp_done = v_vp[-1]
            last["kv_user"] = v_vp_pe

            has_state = not first_seg
            rec("tensor", _wait, sem="sVE", val=v_vp_done)
            rec("tensor", _wait, sem="sSC", val=C["sSC"])
            v_sbfcopy = C["sSC"]
            if not first_seg:
                rec("sync", _wait, sem="sSV", val=C["sSV"])
                rec("sync", _wait, sem="sSC", val=C["sSC"])
                rec("sync", _wait, sem="sVE", val=C["sVE"])
                rec("sync", _dma, out=S_mast[:], in_=st_scr[ei], sem="sST")
                v_strest = adv("sST", 16)
                rec("scalar", _wait, sem="sST", val=v_strest)
                rec("scalar", _wait, sem="sPE", val=v_vp_pe)
                rec("scalar", _scp, out=S_bf[:].rearrange("p h v -> p (h v)"),
                    in_=S_mast[:].rearrange("p h v -> p (h v)"), sem="sSC")
                v_sbfcopy = adv("sSC", 1)

            p_even = [0, 512, 1024, 896]
            p_odd = [1536, 2048, 1280, 2432]
            a_off = [0, 512, 896, 1152]
            slot_n = [512, 384, 256, 128]
            v_aT = [0] * H
            v_omm = [0] * H
            v_dtt = [0] * H
            v_oc = [0] * H
            v_oev = [0] * H
            v_dadd = [0] * H

            def c5_delta(j):
                off = 3584
                rec("tensor", _wait, sem="sSC", val=fil["cons"][0])
                rec("tensor", _wait, sem="sVE", val=fil["vcons"][0])
                for jc in range(NTC):
                    rec("tensor", _mm, out=pflat(off, 128),
                        lhsT=Kp_tm[:, jc, j, :], rhs=Vp_tm[:, jc, j, :],
                        start=(jc == 0), stop=(jc == NTC - 1),
                        sem="sPE" if jc == NTC - 1 else None)
                vm = adv("sPE", 1)
                rec("vector", _wait, sem="sPE", val=vm)
                if j == 0:
                    rec("vector", _wait, sem="sSC", val=v_sbfcopy)
                    if first_seg:
                        rec("vector", _wait, sem="sSV", val=C["sSV"])
                if first_seg:
                    rec("vector", _cp, out=S_mast[:, j, :], in_=pflat(off, 128), sem="sVE")
                else:
                    rec("vector", _stt, out=S_mast[:, j, :], in0=pflat(off, 128),
                        scalar=1.0, in1=S_mast[:, j, :], op0=ALU.mult, op1=ALU.add, sem="sVE")
                fil["vcons"][0] = adv("sVE", 1)
                v_dadd[j] = fil["vcons"][0]

            def c5_av(j):
                par = j % 2
                o_off = 2560 if par == 0 else 3072
                rec("tensor", _wait, sem="sVE", val=v_dtt[j])
                rec("tensor", _wait, sem="sSC", val=v_oc[j])
                if j >= 2:
                    rec("tensor", _wait, sem="sVE", val=v_oev[j - 2])
                if has_state and j == 0:
                    rec("tensor", _wait, sem="sSC", val=v_sbfcopy)
                for il in range(NTC):
                    nmm = il + 1 + (1 if has_state else 0)
                    k = 0
                    for jc in range(il + 1):
                        k += 1
                        rec("tensor", _mm, out=pflat(o_off + il * 128, 128),
                            lhsT=aT_big[:, par, a_off[jc] + (il - jc) * 128:
                                        a_off[jc] + (il - jc) * 128 + 128],
                            rhs=Vp_tm[:, jc, j, :], start=(k == 1), stop=(k == nmm),
                            sem="sPE" if (il == NTC - 1 and k == nmm) else None)
                    if has_state:
                        k += 1
                        rec("tensor", _mm, out=pflat(o_off + il * 128, 128),
                            lhsT=Qp_fm[:, j, il * 128:(il + 1) * 128], rhs=S_bf[:, j, :],
                            start=False, stop=True,
                            sem="sPE" if (il == NTC - 1) else None)
                v_omm[j] = adv("sPE", 1)
                rec("vector", _wait, sem="sPE", val=v_omm[j])
                for il in range(NTC):
                    wcol = wgt01[:, il, ei:ei + 1]
                    if ei == 0:
                        rec("vector", _ts, out=O_acc[:, il, j * 128:(j + 1) * 128],
                            in0=pflat(o_off + il * 128, 128), s1=wcol, op0=ALU.mult, sem="sVE")
                    else:
                        rec("vector", _stt, out=O_acc[:, il, j * 128:(j + 1) * 128],
                            in0=pflat(o_off + il * 128, 128), scalar=wcol,
                            in1=O_acc[:, il, j * 128:(j + 1) * 128],
                            op0=ALU.mult, op1=ALU.add, sem="sVE")
                adv("sVE", 3)
                v_oev[j] = adv("sVE", 1)

            for h in range(H):
                par = h % 2
                poff = p_even if par == 0 else p_odd
                if h >= 2:
                    rec("tensor", _wait, sem="sVE", val=v_dtt[h - 2])
                    rec("tensor", _wait, sem="sSC", val=v_oc[h - 2])
                for jc in (0, 1, 3):
                    rec("tensor", _mm, out=pflat(poff[jc], slot_n[jc]),
                        lhsT=Kp_fm[:, h, jc * 128:(jc + 1) * 128],
                        rhs=Qp_fm[:, h, jc * 128:512], start=True, stop=True)
                c5_delta(h)
                if h >= 1:
                    rec("tensor", _wait, sem="sVE", val=v_dtt[h - 1])
                    rec("tensor", _wait, sem="sSC", val=v_oc[h - 1])
                rec("tensor", _mm, out=pflat(poff[2], slot_n[2]),
                    lhsT=Kp_fm[:, h, 256:384], rhs=Qp_fm[:, h, 256:512],
                    start=True, stop=True, sem="sPE")
                v_aT[h] = adv("sPE", 1)
                rec("vector", _wait, sem="sPE", val=v_aT[h])
                for jc in range(NTC):
                    rec("vector", _tt, out=aT_big[:, par, a_off[jc]: a_off[jc] + 128],
                        in0=pflat(poff[jc], 128), in1=tri_s[:], op=ALU.mult,
                        sem="sVE")
                adv("sVE", 3)
                v_dtt[h] = adv("sVE", 1)
                rec("scalar", _wait, sem="sPE", val=v_aT[h])
                rec("scalar", _wait, sem="sVE", val=v_dtt[h])
                for jc, w in ((0, 384), (1, 256), (2, 128)):
                    rec("scalar", _scp,
                        out=aT_big[:, par, a_off[jc] + 128: a_off[jc] + 128 + w],
                        in_=pflat(poff[jc] + 128, w), sem="sSC")
                adv("sSC", 2)
                v_oc[h] = adv("sSC", 1)

                if h >= 1:
                    c5_av(h - 1)
                if h % 2 == 1:
                    maybe_fill()
            c5_av(H - 1)

            if not last_seg:
                rec("scalar", _wait, sem="sVE", val=v_dadd[H - 1])
                rec("scalar", _dma, out=st_scr[ei], in_=S_mast[:], sem="sSV")
                last["stsave"][ei] = adv("sSV", 16)
            last["c5_ve_done"] = C["sVE"]

        c_pe_done = C["sPE"]

        v_onrm_tr = [0] * NTC
        rec("vector", _wait, sem="sVE", val=v_oev[H - 1])
        vsq_d = 0
        for tc in range(NTC):
            o3 = O_acc[:, tc, :].rearrange("p (h v) -> p h v", h=H)
            if vsq_d:
                rec("vector", _wait, sem="sVE", val=C["sVE"])
            rec("vector", _tt, out=tmp_sq[:].rearrange("p (h v) -> p h v", h=H),
                in0=o3, in1=o3, op=ALU.mult, sem="sVE")
            vsq_d = adv("sVE", 1)
            rec("vector", _wait, sem="sVE", val=vsq_d)
            rec("vector", _red, out=rms_s[:, tc, :],
                in_=tmp_sq[:].rearrange("p (h v) -> p h v", h=H), op=ALU.add, sem="sVE")
            adv("sVE", 1)
        vred_d = C["sVE"]
        rec("scalar", _wait, sem="sVE", val=vred_d)
        rec("scalar", _act, out=rms_s[:].rearrange("p a h -> p (a h)"),
            in_=rms_s[:].rearrange("p a h -> p (a h)"), func=AF.Ln, scale=1.0 / DK,
            bias=1e-5, sem="sSC")
        vln_d = adv("sSC", 1)
        rec("scalar", _wait, sem="sSC", val=vln_d)
        rec("scalar", _act, out=rms_r[:].rearrange("p a h -> p (a h)"),
            in_=rms_s[:].rearrange("p a h -> p (a h)"), func=AF.Exp, scale=-0.5, sem="sSC")
        vrs_d = adv("sSC", 1)
        for tc in range(NTC):
            maybe_fill()
            o3 = O_acc[:, tc, :].rearrange("p (h v) -> p h v", h=H)
            rec("vector", _wait, sem="sSC", val=vrs_d)
            if tc > 0:
                rec("vector", _wait, sem="sPE", val=v_onrm_tr[tc - 1])
            rb = rms_r[:, tc, :].unsqueeze(2).broadcast_to([128, H, DK])
            rec("vector", _tt, out=Onrm[:].rearrange("p (h v) -> p h v", h=H),
                in0=o3, in1=rb, op=ALU.mult, sem="sVE")
            vON = adv("sVE", 1)
            rec("tensor", _wait, sem="sVE", val=vON)
            rec("tensor", _wait, sem="sVE", val=last["b2_free"])
            for g in range(CG):
                rec("tensor", _tr, out=bankbf(2, g * 128, 128),
                    in_=Onrm[:, g * 128:(g + 1) * 128], ident=identb[:],
                    sem="sPE" if g == CG - 1 else None)
            v_onrm_tr[tc] = adv("sPE", 1)
            rec("vector", _wait, sem="sPE", val=v_onrm_tr[tc])
            rec("vector", _cp, out=Onrm_fm[:, :, tc * 128:(tc + 1) * 128],
                in_=bankbf(2, 0, 1024).rearrange("p (g t) -> p g t", g=CG), sem="sVE")
            vev = adv("sVE", 1)
            last["b2_free"] = vev
        v_onrm_done = last["b2_free"]

        while fil["next_h2"] < F1 // 128:
            emit_h2_slice()
        v_h2_done = fil["v_h2_last"]

        rec("sync", _wait, sem="sPE", val=c_pe_done)
        rec("sync", _wait, sem="sOW", val=C["sOW"])
        rec("sync", _dma, out=owT_sb[:],
            in_=ow_scr.rearrange("(cg p) o -> p cg o", p=128), sem="sOW")
        v_ow = adv("sOW", 16)
        rec("tensor", _wait, sem="sSC", val=v_h2_done)
        rec("tensor", _wait, sem="sVE", val=v_onrm_done)
        rec("tensor", _wait, sem="sOW", val=v_ow)
        for tc in range(NTC):
            for g in range(CG):
                for ds in range(2):
                    rec("tensor", _mm, out=pflat(tc * 1024 + ds * 512, 512),
                        lhsT=Onrm_fm[:, g, tc * 128:(tc + 1) * 128],
                        rhs=owT_sb[:, g, ds * 512:(ds + 1) * 512],
                        start=(g == 0), stop=False,
                        sem="sPE" if (tc == NTC - 1 and g == CG - 1 and ds == 1) else None)
        v_attn_mm = adv("sPE", 1)
        for o2s in range(8):
            par = o2s % 2
            w2sem = "sW2" if par == 0 else "sW2B"
            rec("sync", _wait, sem="sPE", val=max(last["w2_user"][par], c_pe_done))
            rec("sync", _dma, out=w2_sb[:, par],
                in_=w2_scr[o2s * 512:(o2s + 1) * 512, :].rearrange("(s p) d -> p s d", p=128),
                sem=w2sem)
            v_w2 = adv(w2sem, 16)
            rec("tensor", _wait, sem=w2sem, val=v_w2)
            lastslice = (o2s == 7)
            for tc in range(NTC):
                for sub in range(4):
                    o2g = o2s * 4 + sub
                    for ds in range(2):
                        is_last = lastslice and sub == 3
                        rec("tensor", _mm,
                            out=pflat(tc * 1024 + ds * 512, 512),
                            lhsT=H2[:, o2g, tc * 128:(tc + 1) * 128],
                            rhs=w2_sb[:, par, sub, ds * 512:(ds + 1) * 512],
                            start=False, stop=is_last,
                            sem="sPE" if (tc == NTC - 1 and sub == 3 and ds == 1) else None)
            last["w2_user"][par] = adv("sPE", 1)
        v_e3 = C["sPE"]
        last["w2_user"] = [v_e3, v_e3]
        for tc in range(NTC):
            par = tc % 2
            rec("vector", _wait, sem="sPE", val=v_e3)
            if tc >= 2:
                rec("vector", _wait, sem="sOUT", val=C["sOUT"])
            rec("vector", _tt, out=out_sb[:, par, :], in0=pflat(tc * 1024, 1024),
                in1=x_seg[:, tc, :], op=ALU.add, sem="sVE")
            vev = adv("sVE", 1)
            rec("scalar", _wait, sem="sVE", val=vev)
            rec("scalar", _dma, out=out_d[b, t0 + tc * 128: t0 + (tc + 1) * 128, :],
                in_=out_sb[:, par, :], sem="sOUT")
            last["out_user"][par] = adv("sOUT", 16)
        last["e3_evict"] = C["sVE"]
        last["b2_free"] = C["sVE"]
        last["e_pe_done"] = v_e3

        if cfg["debug"] and seg == 0:
            rec("scalar", _wait, sem="sVE", val=C["sVE"])
            for name, srcap in [("xn_fm", xn_fm[:]), ("wgt01", wgt01[:]), ("mask01", mask01[:]),
                              ("Qp", Qp_fm[:]), ("Oacc", O_acc[:]), ("H1", H1[:]),
                              ("H2", H2[:]), ("Onrmfm", Onrm_fm[:])]:
                rec("scalar", _dma, out=dbg[name], in_=srcap, sem="sOUT")
                adv("sOUT", 16)

    rec("scalar", _wait, sem="sOUT", val=C["sOUT"])
    rec("scalar", _wait, sem="sSV", val=C["sSV"])

    chk = {k: 0 for k in SEMS}
    for _en, _fn, _kw in REC:
        s = _kw.get("sem")
        if s and _fn is not _wait:
            chk[s] += 16 if _fn is _dma else 1
    chk["sGP"] += 15
    assert chk == C, f"sem count mismatch: {chk} vs {C}"

    with es:
        block = nc.Block()
        with block:
            for eng_name in ["sync", "gpsimd", "tensor", "vector", "scalar"]:
                def body(e, _n=eng_name):
                    for en, fn, kw in REC:
                        if en == _n:
                            fn(e, **kw)
                getattr(block, eng_name)(body)
    return nc


def kernel(**inputs):
    cfg = dict(DEFAULT_CFG)
    x = np.ascontiguousarray(inputs["L0_attn_x"])
    n_cores = 8
    shared = make_shared_inputs(inputs)
    nc = build_program(cfg)
    in_maps = []
    for c in range(n_cores):
        m = dict(shared)
        m["x"] = np.ascontiguousarray(x[2 * c: 2 * c + 2])
        in_maps.append(m)
    res = run_bass_kernel_spmd(nc, in_maps, core_ids=list(range(n_cores)))
    return np.concatenate([r["out"] for r in res.results], axis=0)


def make_shared_inputs(inputs):
    return dict(
        gate_wT=np.ascontiguousarray(inputs["L0_attn_gate_w"].T),
        q_wT=np.ascontiguousarray(inputs["L0_attn_q_w"].T),
        k_wT=np.ascontiguousarray(np.transpose(inputs["L0_attn_k_w"], (0, 2, 1))),
        v_wT=np.ascontiguousarray(np.transpose(inputs["L0_attn_v_w"], (0, 2, 1))),
        fmq1T=np.ascontiguousarray(inputs["L0_attn_fmq1_w"].T),
        fmq2T=np.ascontiguousarray(inputs["L0_attn_fmq2_w"].T),
        fmk1T=np.ascontiguousarray(inputs["L0_attn_fmk1_w"].T),
        fmk2T=np.ascontiguousarray(inputs["L0_attn_fmk2_w"].T),
        norm_w=np.ascontiguousarray(np.asarray(inputs["L0_attn_norm_w"]).reshape(DK, 1)),
        o_wT=np.ascontiguousarray(inputs["L0_attn_o_w"].T),
        k0=np.ascontiguousarray(inputs["L0_ffn_k0"]).view(np.uint8),
        k1=np.ascontiguousarray(inputs["L0_ffn_k1"]).view(np.uint8),
        k2=np.ascontiguousarray(inputs["L0_ffn_k2"]).view(np.uint8),
        s0=np.asarray(inputs["L0_ffn_s0"], dtype=np.float32).reshape(1, 1),
        s1=np.asarray(inputs["L0_ffn_s1"], dtype=np.float32).reshape(1, 1),
        s2=np.asarray(inputs["L0_ffn_s2"], dtype=np.float32).reshape(1, 1),
    )


# version 11
# speedup vs baseline: 1.0583x; 1.0179x over previous
import numpy as np
import ml_dtypes
from contextlib import ExitStack
import concourse.bass as bass
import concourse.mybir as mybir
from concourse.bass_utils import run_bass_kernel_spmd

F32 = mybir.dt.float32
BF16 = mybir.dt.bfloat16
U8 = mybir.dt.uint8
AF = mybir.ActivationFunctionType
ALU = mybir.AluOpType
AX = mybir.AxisListType

D = 1024
E_EXP = 8
H = 8
DK = 128
SEG = 512
NTC = SEG // 128
CG = D // 128
S_ATTN = float(DK) ** -0.5
F1 = D * 4

DEFAULT_CFG = dict(nbatch=2, segs_per_batch=8, nexp=8, debug=False)


def build_program(cfg):
    nb = cfg["nbatch"]
    spb = cfg["segs_per_batch"]
    nexp = cfg["nexp"]
    seqlen = spb * SEG
    nseg = nb * spb

    nc = bass.Bass("TRN2", target_bir_lowering=False)

    x_d = nc.dram_tensor("x", [nb, seqlen, D], F32, kind="ExternalInput").ap()
    gate_d = nc.dram_tensor("gate_wT", [D, E_EXP], F32, kind="ExternalInput").ap()
    qwT_d = nc.dram_tensor("q_wT", [D, D], F32, kind="ExternalInput").ap()
    kwT_d = nc.dram_tensor("k_wT", [E_EXP, D, D], F32, kind="ExternalInput").ap()
    vwT_d = nc.dram_tensor("v_wT", [E_EXP, D, D], F32, kind="ExternalInput").ap()
    fm_d = [nc.dram_tensor(n, [DK, DK], F32, kind="ExternalInput").ap()
            for n in ["fmq1T", "fmq2T", "fmk1T", "fmk2T"]]
    nw_d = nc.dram_tensor("norm_w", [DK, 1], F32, kind="ExternalInput").ap()
    owT_d = nc.dram_tensor("o_wT", [D, D], F32, kind="ExternalInput").ap()
    k0_d = nc.dram_tensor("k0", [2 * D, F1], U8, kind="ExternalInput").ap()
    k1_d = nc.dram_tensor("k1", [F1, F1], U8, kind="ExternalInput").ap()
    k2_d = nc.dram_tensor("k2", [F1, D], U8, kind="ExternalInput").ap()
    s0_d = nc.dram_tensor("s0", [1, 1], F32, kind="ExternalInput").ap()
    s1_d = nc.dram_tensor("s1", [1, 1], F32, kind="ExternalInput").ap()
    s2_d = nc.dram_tensor("s2", [1, 1], F32, kind="ExternalInput").ap()
    out_d = nc.dram_tensor("out", [nb, seqlen, D], F32, kind="ExternalOutput").ap()

    kv_scr = nc.dram_tensor("kv_scr", [E_EXP, 2, D, D], BF16, kind="Internal").ap()
    w0_scr = nc.dram_tensor("w0_scr", [D, F1], BF16, kind="Internal").ap()
    w1_scr = nc.dram_tensor("w1_scr", [F1 // 128, 128, F1], BF16, kind="Internal").ap()
    w2_scr = nc.dram_tensor("w2_scr", [F1, D], BF16, kind="Internal").ap()
    st_scr = nc.dram_tensor("st_scr", [E_EXP, 128, H, DK], F32, kind="Internal").ap()
    qw_scr = nc.dram_tensor("qw_scr", [D, D], BF16, kind="Internal").ap()
    ow_scr = nc.dram_tensor("ow_scr", [D, D], BF16, kind="Internal").ap()

    dbg = {}
    if cfg["debug"]:
        for n, shape, dt in [
            ("xn_fm", [128, CG, SEG], BF16), ("wgt01", [128, NTC, E_EXP], F32),
            ("mask01", [128, NTC, E_EXP], F32), ("Qp", [128, H, SEG], BF16),
            ("Kp", [128, H, SEG], BF16), ("Kptm", [128, NTC, H, DK], BF16),
            ("Vp", [128, NTC, H, DK], BF16),
            ("Oacc", [128, NTC, D], F32), ("H1", [128, F1 // 128, SEG], BF16),
            ("H2", [128, F1 // 128, SEG], BF16), ("Onrmfm", [128, CG, SEG], BF16),
        ]:
            dbg[n] = nc.dram_tensor("dbg_" + n, shape, dt, kind="ExternalOutput").ap()

    es = ExitStack()
    SEMS = {}
    for s in ["sX", "sKV", "sQW", "sOW", "sST", "sW0", "sW1A", "sW1B", "sW2", "sW2B",
              "sP", "sOUT", "sSV", "sPS", "sG1", "sG2", "sPE", "sVE", "sSC", "sGP"]:
        SEMS[s] = es.enter_context(nc.semaphore(s))
    C = {k: 0 for k in SEMS}

    def adv(sem, k):
        C[sem] += k
        return C[sem]

    def sb(name, shape, dt):
        return es.enter_context(nc.sbuf_tensor(name, shape, dt))

    gate_sb = sb("gate_sb", [128, CG, E_EXP], BF16)
    fm_sb = sb("fm_sb", [128, 4, DK], BF16)
    identb = sb("identb", [128, 128], BF16)
    identf = sb("identf", [128, 128], F32)
    tri_s = sb("tri_s", [128, 128], F32)
    tri_b = sb("tri_b", [128, 128], BF16)
    onesr = sb("onesr", [1, 128], F32)
    scol = sb("scol", [128, 4], F32)
    nw_sb = sb("nw_sb", [128, 1], F32)
    ssc_sb = sb("ssc_sb", [1, 4], F32)
    S_mast = sb("S_mast", [128, H, DK], F32)
    S_bf = sb("S_bf", [128, H, DK], BF16)

    x_seg = sb("x_seg", [128, NTC, D], F32)
    xn_fm = sb("xn_fm", [128, CG, SEG], BF16)
    Qp_fm = sb("Qp_fm", [128, H, SEG], BF16)
    wgt01 = sb("wgt01", [128, NTC, E_EXP], F32)
    mask01 = sb("mask01", [128, NTC, E_EXP], F32)
    O_acc = sb("O_acc", [128, NTC, D], F32)
    Onrm_fm = sb("Onrm_fm", [128, CG, SEG], BF16)
    xn_tmp = sb("xn_tmp", [128, D], BF16)
    Onrm = xn_tmp
    tmp_sq = sb("tmp_sq", [128, D], F32)
    rt = sb("rt", [128, 24], F32)
    rt3a = sb("rt3a", [128, NTC, E_EXP], F32)
    rt3b = sb("rt3b", [128, NTC, E_EXP], F32)
    rt3c = sb("rt3c", [128, NTC, E_EXP], F32)
    rms_s = sb("rms_s", [128, NTC, H], F32)
    rms_r = sb("rms_r", [128, NTC, H], F32)
    cst = sb("cst", [128, 6], F32)
    _cvals = [1e-5, -0.5, 1.0 / DK, S_ATTN, 1.0, 0.0]
    for _ci, _cv in enumerate(_cvals):
        nc.const_aps.register(F32, _cv, cst[:, _ci:_ci + 1])

    ARENA_BYTES = 124928
    arena = sb("arena", [128, ARENA_BYTES // 2], BF16)

    def carve(off_bytes, shape, dt):
        nelem = 1
        for s in shape[1:]:
            nelem *= s
        esz = {BF16: 2, F32: 4, U8: 1}[dt]
        base = arena[:].bitcast(dt) if dt != BF16 else arena[:]
        v = base[:, off_bytes // esz: off_bytes // esz + nelem]
        if len(shape) > 2:
            dims = "".join(chr(ord('a') + i) for i in range(len(shape) - 1))
            v = v.rearrange(f"p ({' '.join(dims)}) -> p {' '.join(dims)}",
                            **{chr(ord('a') + i): shape[1 + i] for i in range(len(shape) - 1)})
        return v

    kvw_sb = carve(0, [128, CG, D], BF16)
    Khat2 = carve(16384, [128, 3, SEG], BF16)
    Kp_fm = carve(19456, [128, H, SEG], BF16)
    Kp_tm = carve(27648, [128, NTC, H, DK], BF16)
    Vp_tm = carve(35840, [128, NTC, H, DK], BF16)
    aT_big = carve(44032, [128, 2, 1280], BF16)
    A1s = carve(49152, [128, SEG], BF16)
    A1s2 = carve(50176, [128, SEG], BF16)
    H1 = carve(51200, [128, F1 // 128, SEG], BF16)
    H2 = carve(83968, [128, F1 // 128, SEG], BF16)
    w01_sb = carve(116736, [128, CG, 512], BF16)
    w1v_sb = carve(116736, [128, F1 // 128, 128], BF16)
    owT_sb = carve(0, [128, CG, D], BF16)
    out_sb = carve(16384, [128, 2, D], F32)
    w2_sb = carve(27648, [128, 2, 4, D], BF16)
    pstg_a = carve(0, [128, F1], U8)
    pstg_b = carve(4096, [128, F1], U8)
    pstg_f = carve(8192, [128, 512], F32)
    pstg_o = carve(10240, [128, F1], BF16)

    pALL = es.enter_context(nc.psum_tensor("pALL", [128, 4096], F32))

    def bank(i, lo=0, n=512):
        return pALL[:, i * 512 + lo: i * 512 + lo + n]

    def pflat(lo, n):
        return pALL[:, lo: lo + n]

    def bankbf(i, lo=0, n=1024):
        return pALL[:].bitcast(BF16)[:, i * 1024 + lo: i * 1024 + lo + n]

    def pbf(off, n):
        return pALL[:].bitcast(BF16)[:, off: off + n]

    REC = []

    def rec(eng, fn, **kw):
        REC.append((eng, fn, kw))

    def _dma(e, out=None, in_=None, sem=None):
        e.dma_start(out=out, in_=in_).then_inc(SEMS[sem], 16)

    def _wait(e, sem=None, val=0):
        e.wait_ge(SEMS[sem], val)

    def _mm(e, out=None, lhsT=None, rhs=None, start=True, stop=True, sem=None):
        m = e.matmul(out, lhsT, rhs, start=start, stop=stop)
        if sem:
            m.then_inc(SEMS[sem], 1)

    def _tr(e, out=None, in_=None, ident=None, sem=None):
        m = e.transpose(out, in_, ident)
        if sem:
            m.then_inc(SEMS[sem], 1)

    def _cp(e, out=None, in_=None, sem=None):
        m = e.tensor_copy(out=out, in_=in_)
        if sem:
            m.then_inc(SEMS[sem], 1)

    def _scp(e, out=None, in_=None, sem=None):
        m = e.copy(out=out, in_=in_)
        if sem:
            m.then_inc(SEMS[sem], 1)

    def _tt(e, out=None, in0=None, in1=None, op=None, sem=None):
        m = e.tensor_tensor(out=out, in0=in0, in1=in1, op=op)
        if sem:
            m.then_inc(SEMS[sem], 1)

    def _ts(e, out=None, in0=None, s1=None, s2=None, op0=None, op1=None, sem=None):
        kw = {}
        if op1 is not None:
            kw["op1"] = op1
        m = e.tensor_scalar(out=out, in0=in0, scalar1=s1, scalar2=s2, op0=op0, **kw)
        if sem:
            m.then_inc(SEMS[sem], 1)

    def _stt(e, out=None, in0=None, scalar=None, in1=None, op0=None, op1=None, sem=None):
        m = e.scalar_tensor_tensor(out=out, in0=in0, scalar=scalar, in1=in1, op0=op0, op1=op1)
        if sem:
            m.then_inc(SEMS[sem], 1)

    def _red(e, out=None, in_=None, op=None, sem=None):
        m = e.tensor_reduce(out=out, in_=in_, axis=AX.X, op=op)
        if sem:
            m.then_inc(SEMS[sem], 1)

    def _act(e, out=None, in_=None, func=None, bias=0.0, scale=1.0, sem=None):
        m = e.activation(out=out, in_=in_, func=func, bias=bias, scale=scale)
        if sem:
            m.then_inc(SEMS[sem], 1)

    def _mul(e, out=None, in_=None, mul=None, sem=None):
        m = e.mul(out=out, in_=in_, mul=mul)
        if sem:
            m.then_inc(SEMS[sem], 1)

    def _recip(e, out=None, in_=None, sem=None):
        m = e.reciprocal(out=out, in_=in_)
        if sem:
            m.then_inc(SEMS[sem], 1)

    def prep_consts(e):
        e.memset(identb[:], 0.0).then_inc(SEMS["sGP"], 1)
        e.memset(identf[:], 0.0).then_inc(SEMS["sGP"], 1)
        e.memset(tri_s[:], 1.0).then_inc(SEMS["sGP"], 1)
        e.memset(tri_b[:], 1.0).then_inc(SEMS["sGP"], 1)
        e.memset(onesr[:], 1.0).then_inc(SEMS["sGP"], 1)
        for _ci, _cv in enumerate([1e-5, -0.5, 1.0 / DK, S_ATTN, 1.0, 0.0]):
            e.memset(cst[:, _ci:_ci + 1], _cv).then_inc(SEMS["sGP"], 1)
        e.wait_ge(SEMS["sGP"], 11)
        e.affine_select(out=identb[:], in_=identb[:], compare_op=ALU.not_equal,
                        fill=1.0, base=0, pattern=[[-1, 128]],
                        channel_multiplier=1).then_inc(SEMS["sGP"], 1)
        e.affine_select(out=identf[:], in_=identf[:], compare_op=ALU.not_equal,
                        fill=1.0, base=0, pattern=[[-1, 128]],
                        channel_multiplier=1).then_inc(SEMS["sGP"], 1)
        e.affine_select(out=tri_s[:], in_=tri_s[:], compare_op=ALU.is_ge,
                        fill=0.0, base=0, pattern=[[1, 128]],
                        channel_multiplier=-1).then_inc(SEMS["sGP"], 1)
        e.affine_select(out=tri_b[:], in_=tri_b[:], compare_op=ALU.is_ge,
                        fill=0.0, base=0, pattern=[[1, 128]],
                        channel_multiplier=-1).then_inc(SEMS["sGP"], 1)
    rec("gpsimd", prep_consts)
    v_consts = adv("sGP", 15)

    small_loads = [
        (gate_sb[:], gate_d.rearrange("(cg p) e -> p cg e", p=128)),
    ]
    for i in range(4):
        small_loads.append((fm_sb[:, i, :], fm_d[i]))
    for dst, srcap in small_loads:
        rec("gpsimd", _dma, out=dst, in_=srcap, sem="sG1")
    v_small = adv("sG1", 16 * len(small_loads))

    rec("sync", _dma, out=nw_sb[:], in_=nw_d, sem="sP")
    rec("sync", _dma, out=ssc_sb[:, 0:1], in_=s0_d, sem="sP")
    rec("sync", _dma, out=ssc_sb[:, 1:2], in_=s1_d, sem="sP")
    rec("sync", _dma, out=ssc_sb[:, 2:3], in_=s2_d, sem="sP")
    v_scal = adv("sP", 64)
    rec("sync", _wait, sem="sP", val=v_scal)

    for e_i in range(nexp):
        rec("gpsimd", _dma, out=kv_scr[e_i, 0], in_=kwT_d[e_i], sem="sG2")
        rec("gpsimd", _dma, out=kv_scr[e_i, 1], in_=vwT_d[e_i], sem="sG2")
    rec("gpsimd", _dma, out=qw_scr, in_=qwT_d, sem="sG2")
    v_kvscr = adv("sG2", 16 * (2 * nexp + 1))

    rec("tensor", _wait, sem="sP", val=v_scal)
    rec("tensor", _wait, sem="sGP", val=v_consts)
    rec("tensor", _mm, out=bank(3, 0, 3), lhsT=onesr[:], rhs=ssc_sb[:, 0:3],
        start=True, stop=True, sem="sPE")
    v_scolmm = adv("sPE", 1)
    rec("vector", _wait, sem="sPE", val=v_scolmm)
    rec("vector", _ts, out=scol[:, 0:3], in0=bank(3, 0, 3), s1=2.0, op0=ALU.mult, sem="sVE")
    v1 = adv("sVE", 1)
    rec("vector", _wait, sem="sVE", val=v1)
    rec("vector", _ts, out=scol[:, 2:3], in0=bank(3, 2, 1), s1=4.0, op0=ALU.mult, sem="sVE")
    rec("vector", _ts, out=scol[:, 3:4], in0=bank(3, 2, 1), s1=2.0, op0=ALU.mult, sem="sVE")
    adv("sVE", 2)
    v_scol = C["sVE"]

    rec("vector", _wait, sem="sP", val=v_scal)
    for cg in range(CG):
        for os_ in range(2):
            srcap = owT_d.rearrange("(cg p) o -> cg p o", p=128)[cg][:, os_ * 512:(os_ + 1) * 512]
            rec("sync", _dma, out=pstg_f[:], in_=srcap, sem="sP")
            vload = adv("sP", 16)
            rec("vector", _wait, sem="sP", val=vload)
            rec("vector", _ts, out=pstg_o[:, 0:512], in0=pstg_f[:],
                s1=nw_sb[:, 0:1], op0=ALU.mult, sem="sVE")
            vconv = adv("sVE", 1)
            rec("scalar", _wait, sem="sVE", val=vconv)
            rec("scalar", _dma, out=ow_scr[cg * 128:(cg + 1) * 128, os_ * 512:(os_ + 1) * 512],
                in_=pstg_o[:, 0:512], sem="sPS")
            vstore = adv("sPS", 16)
            rec("sync", _wait, sem="sPS", val=vstore)

    for cg in range(CG):
        rec("sync", _dma, out=pstg_a[:], in_=k0_d[cg * 128:(cg + 1) * 128, :], sem="sP")
        rec("sync", _dma, out=pstg_b[:], in_=k0_d[D + cg * 128: D + (cg + 1) * 128, :], sem="sP")
        vload = adv("sP", 32)
        rec("vector", _wait, sem="sP", val=vload)
        rec("vector", _tt, out=pstg_o[:], in0=pstg_a[:], in1=pstg_b[:], op=ALU.subtract, sem="sVE")
        vconv = adv("sVE", 1)
        rec("scalar", _wait, sem="sVE", val=vconv)
        rec("scalar", _dma, out=w0_scr[cg * 128:(cg + 1) * 128, :], in_=pstg_o[:], sem="sPS")
        vstore = adv("sPS", 16)
        rec("sync", _wait, sem="sPS", val=vstore)

    for g in range(F1 // 128):
        rec("sync", _dma, out=pstg_a[:], in_=k1_d[g * 128:(g + 1) * 128, :], sem="sP")
        vload = adv("sP", 16)
        rec("vector", _wait, sem="sP", val=vload)
        rec("vector", _ts, out=pstg_o[:], in0=pstg_a[:], s1=2.0, s2=-1.0,
            op0=ALU.mult, op1=ALU.add, sem="sVE")
        vconv = adv("sVE", 1)
        rec("scalar", _wait, sem="sVE", val=vconv)
        rec("scalar", _dma, out=w1_scr[g], in_=pstg_o[:], sem="sPS")
        vstore = adv("sPS", 16)
        rec("sync", _wait, sem="sPS", val=vstore)

    rec("vector", _wait, sem="sVE", val=v_scol)
    for g in range(F1 // 128):
        rec("sync", _dma, out=pstg_a[:, 0:D], in_=k2_d[g * 128:(g + 1) * 128, :], sem="sP")
        vload = adv("sP", 16)
        rec("vector", _wait, sem="sP", val=vload)
        rec("vector", _ts, out=pstg_o[:, 0:D], in0=pstg_a[:, 0:D], s1=scol[:, 2:3],
            s2=scol[:, 3:4], op0=ALU.mult, op1=ALU.subtract, sem="sVE")
        vconv = adv("sVE", 1)
        rec("scalar", _wait, sem="sVE", val=vconv)
        rec("scalar", _dma, out=w2_scr[g * 128:(g + 1) * 128, :], in_=pstg_o[:, 0:D], sem="sPS")
        vstore = adv("sPS", 16)
        rec("sync", _wait, sem="sPS", val=vstore)

    v_prep_sS = C["sPS"]
    v_prep_sVE = C["sVE"]

    for eng in ["tensor", "vector", "scalar"]:
        rec(eng, _wait, sem="sG1", val=v_small)
        rec(eng, _wait, sem="sG2", val=v_kvscr)
        rec(eng, _wait, sem="sPS", val=v_prep_sS)
        rec(eng, _wait, sem="sVE", val=v_prep_sVE)
        rec(eng, _wait, sem="sGP", val=v_consts)

    last = dict(
        kv_user=0, e3_evict=0, b2_free=0, e_pe_done=0,
        w0_user=0, w1_user=[0, 0], w2_user=[0, 0], out_user=[0, 0],
        stsave=[0] * E_EXP, c5_ve_done=0,
    )

    for seg in range(nseg):
        b = seg // spb
        sib = seg % spb
        t0 = sib * SEG
        first_seg = (sib == 0)
        last_seg = (sib == spb - 1)

        rec("sync", _wait, sem="sVE", val=last["e3_evict"])
        rec("sync", _wait, sem="sX", val=C["sX"])
        if cfg["debug"] and seg > 0:
            for eng in ["vector", "scalar", "tensor"]:
                rec(eng, _wait, sem="sOUT", val=C["sOUT"])
        for tc in range(NTC):
            rec("sync", _dma, out=x_seg[:, tc, :],
                in_=x_d[b, t0 + tc * 128: t0 + (tc + 1) * 128, :], sem="sX")
            adv("sX", 16)
        xw_total = C["sX"]
        xw = [xw_total] * NTC

        v_xn_tr = [0] * NTC
        v_xnfm = [0] * NTC
        xall = x_seg[:].rearrange("p a d -> p (a d)")
        tmp_sq4 = O_acc[:].rearrange("p a d -> p (a d)")
        rec("vector", _wait, sem="sX", val=xw[-1])
        rec("vector", _wait, sem="sVE", val=last["e3_evict"])
        rec("vector", _tt, out=tmp_sq4, in0=xall, in1=xall, op=ALU.mult, sem="sVE")
        vsq = adv("sVE", 1)
        rec("vector", _red, out=rt[:, 0:NTC], in_=x_seg[:], op=ALU.add, sem="sVE")
        adv("sVE", 1)
        rec("vector", _wait, sem="sVE", val=vsq)
        rec("vector", _red, out=rt[:, 4:4 + NTC],
            in_=tmp_sq4.rearrange("p (a d) -> p a d", a=NTC), op=ALU.add, sem="sVE")
        vred = adv("sVE", 1)
        rec("vector", _wait, sem="sVE", val=vred)
        rec("vector", _ts, out=rt[:, 0:NTC], in0=rt[:, 0:NTC], s1=1.0 / D, op0=ALU.mult, sem="sVE")
        vm_ = adv("sVE", 1)
        rec("vector", _wait, sem="sVE", val=vm_)
        rec("vector", _ts, out=rt[:, 4:4 + NTC], in0=rt[:, 4:4 + NTC], s1=1.0 / D,
            op0=ALU.mult, sem="sVE")
        vq_ = adv("sVE", 1)
        rec("vector", _wait, sem="sVE", val=vq_)
        rec("vector", _tt, out=rt[:, 8:8 + NTC], in0=rt[:, 0:NTC], in1=rt[:, 0:NTC],
            op=ALU.mult, sem="sVE")
        vmm2 = adv("sVE", 1)
        rec("vector", _wait, sem="sVE", val=vmm2)
        rec("vector", _tt, out=rt[:, 8:8 + NTC], in0=rt[:, 4:4 + NTC], in1=rt[:, 8:8 + NTC],
            op=ALU.subtract, sem="sVE")
        vvar = adv("sVE", 1)
        rec("scalar", _wait, sem="sVE", val=vvar)
        rec("scalar", _act, out=rt[:, 8:8 + NTC], in_=rt[:, 8:8 + NTC], func=AF.Ln, bias=1e-5,
            sem="sSC")
        vln = adv("sSC", 1)
        rec("scalar", _wait, sem="sSC", val=vln)
        rec("scalar", _act, out=rt[:, 12:12 + NTC], in_=rt[:, 8:8 + NTC], func=AF.Exp, scale=-0.5,
            sem="sSC")
        vrs = adv("sSC", 1)
        rec("vector", _wait, sem="sSC", val=vrs)
        rec("vector", _tt, out=rt[:, 16:16 + NTC], in0=rt[:, 0:NTC], in1=rt[:, 12:12 + NTC],
            op=ALU.mult, sem="sVE")
        vmr = adv("sVE", 1)
        rec("vector", _wait, sem="sVE", val=vmr)
        rec("vector", _ts, out=rt[:, 16:16 + NTC], in0=rt[:, 16:16 + NTC], s1=-1.0,
            op0=ALU.mult, sem="sVE")
        vnm = adv("sVE", 1)
        for tc in range(NTC):
            rec("scalar", _wait, sem="sVE", val=vnm)
            if tc > 0:
                rec("scalar", _wait, sem="sPE", val=v_xn_tr[tc - 1])
            rec("scalar", _act, out=xn_tmp[:], in_=x_seg[:, tc, :], func=AF.Identity,
                scale=rt[:, 12 + tc:13 + tc], bias=rt[:, 16 + tc:17 + tc], sem="sSC")
            vxn = adv("sSC", 1)
            rec("tensor", _wait, sem="sSC", val=vxn)
            rec("tensor", _wait, sem="sVE", val=last["b2_free"])
            for g in range(CG):
                rec("tensor", _tr, out=bankbf(2, g * 128, 128),
                    in_=xn_tmp[:, g * 128:(g + 1) * 128], ident=identb[:],
                    sem="sPE" if g == CG - 1 else None)
            v_xn_tr[tc] = adv("sPE", 1)
            rec("vector", _wait, sem="sPE", val=v_xn_tr[tc])
            rec("vector", _cp,
                out=xn_fm[:, :, tc * 128:(tc + 1) * 128],
                in_=bankbf(2, 0, 1024).rearrange("p (g t) -> p g t", g=CG), sem="sVE")
            v_xnfm[tc] = adv("sVE", 1)
            last["b2_free"] = v_xnfm[tc]

        for tc in range(NTC):
            rec("tensor", _wait, sem="sVE", val=v_xnfm[tc])
            for cg in range(CG):
                rec("tensor", _mm, out=bank(3, tc * E_EXP, E_EXP),
                    lhsT=xn_fm[:, cg, tc * 128:(tc + 1) * 128],
                    rhs=gate_sb[:, cg, :], start=(cg == 0), stop=(cg == CG - 1),
                    sem="sPE" if cg == CG - 1 else None)
        v_lg = adv("sPE", NTC)
        lg3 = bank(3, 0, NTC * E_EXP).rearrange("p (a e) -> p a e", a=NTC)
        rec("vector", _wait, sem="sPE", val=v_lg)
        rec("vector", _red, out=rt[:, 10:10 + NTC], in_=lg3, op=ALU.max, sem="sVE")
        vm1 = adv("sVE", 1)
        rec("vector", _wait, sem="sVE", val=vm1)
        m1b = rt[:, 10:10 + NTC].unsqueeze(2).broadcast_to([128, NTC, E_EXP])
        rec("vector", _tt, out=rt3a[:], in0=lg3, in1=m1b, op=ALU.is_equal, sem="sVE")
        veq1 = adv("sVE", 1)
        rec("vector", _wait, sem="sVE", val=veq1)
        rec("vector", _stt, out=rt3b[:], in0=rt3a[:], scalar=-1e30, in1=lg3,
            op0=ALU.mult, op1=ALU.add, sem="sVE")
        vl2 = adv("sVE", 1)
        rec("vector", _wait, sem="sVE", val=vl2)
        rec("vector", _red, out=rt[:, 14:14 + NTC], in_=rt3b[:], op=ALU.max, sem="sVE")
        vm2 = adv("sVE", 1)
        rec("vector", _wait, sem="sVE", val=vm2)
        rec("vector", _tt, out=rt[:, 18:18 + NTC], in0=rt[:, 14:14 + NTC],
            in1=rt[:, 10:10 + NTC], op=ALU.subtract, sem="sVE")
        vd = adv("sVE", 1)
        m2b = rt[:, 14:14 + NTC].unsqueeze(2).broadcast_to([128, NTC, E_EXP])
        rec("vector", _tt, out=rt3c[:], in0=rt3b[:], in1=m2b, op=ALU.is_equal, sem="sVE")
        veq2 = adv("sVE", 1)
        rec("scalar", _wait, sem="sVE", val=vd)
        rec("scalar", _act, out=rt[:, 10:10 + NTC], in_=rt[:, 18:18 + NTC], func=AF.Exp, sem="sSC")
        vex = adv("sSC", 1)
        rec("vector", _wait, sem="sSC", val=vex)
        rec("vector", _wait, sem="sVE", val=veq2)
        rec("vector", _ts, out=rt[:, 14:14 + NTC], in0=rt[:, 10:10 + NTC], s1=1.0,
            op0=ALU.add, sem="sVE")
        vp1 = adv("sVE", 1)
        rec("vector", _wait, sem="sVE", val=vp1)
        rec("vector", _recip, out=rt[:, 18:18 + NTC], in_=rt[:, 14:14 + NTC], sem="sVE")
        vw1 = adv("sVE", 1)
        rec("vector", _wait, sem="sVE", val=vw1)
        rec("vector", _tt, out=rt[:, 14:14 + NTC], in0=rt[:, 18:18 + NTC],
            in1=rt[:, 10:10 + NTC], op=ALU.mult, sem="sVE")
        vw2 = adv("sVE", 1)
        rec("vector", _wait, sem="sVE", val=max(vw2, veq2))
        w1b = rt[:, 18:18 + NTC].unsqueeze(2).broadcast_to([128, NTC, E_EXP])
        w2b = rt[:, 14:14 + NTC].unsqueeze(2).broadcast_to([128, NTC, E_EXP])
        rec("vector", _tt, out=mask01[:], in0=rt3a[:], in1=rt3c[:], op=ALU.add, sem="sVE")
        adv("sVE", 1)
        rec("vector", _tt, out=rt3b[:], in0=rt3a[:], in1=w1b, op=ALU.mult, sem="sVE")
        vt1 = adv("sVE", 1)
        rec("vector", _wait, sem="sVE", val=vt1)
        rec("vector", _tt, out=rt3c[:], in0=rt3c[:], in1=w2b, op=ALU.mult, sem="sVE")
        vt2 = adv("sVE", 1)
        rec("vector", _wait, sem="sVE", val=vt2)
        rec("vector", _tt, out=wgt01[:], in0=rt3b[:], in1=rt3c[:], op=ALU.add, sem="sVE")
        v_route = adv("sVE", 1)

        rec("sync", _wait, sem="sG2", val=v_kvscr)
        rec("sync", _wait, sem="sPE", val=max(last["e_pe_done"], last["kv_user"]))
        rec("sync", _wait, sem="sOW", val=C["sOW"])
        if cfg["debug"]:
            rec("sync", _wait, sem="sOUT", val=C["sOUT"])
        rec("sync", _dma, out=kvw_sb[:],
            in_=qw_scr.rearrange("(cg p) o -> p cg o", p=128), sem="sQW")
        v_qw = adv("sQW", 16)
        v_qh = [0] * H
        v_qe = [0] * H
        v_fm = [0] * H
        v_qp = [0] * H
        v_ae = [0] * H

        def b_qhat(h):
            kb = 0 if h % 2 == 0 else 4
            rec("tensor", _wait, sem="sVE", val=v_xnfm[-1])
            rec("tensor", _wait, sem="sQW", val=v_qw)
            if h >= 2:
                rec("tensor", _wait, sem="sSC", val=v_qe[h - 2])
            for cg in range(CG):
                rec("tensor", _mm, out=bank(kb), lhsT=kvw_sb[:, cg, h * 128:(h + 1) * 128],
                    rhs=xn_fm[:, cg, :], start=(cg == 0), stop=(cg == CG - 1),
                    sem="sPE" if cg == CG - 1 else None)
            v_qh[h] = adv("sPE", 1)

        def b_ke(h):
            kb = 0 if h % 2 == 0 else 4
            rec("scalar", _wait, sem="sPE", val=v_qh[h])
            if h == 0:
                rec("scalar", _wait, sem="sOUT", val=C["sOUT"])
            rec("scalar", _scp, out=Khat2[:, h % 2, :], in_=bank(kb), sem="sSC")
            v_qe[h] = adv("sSC", 1)

        def b_fm(h):
            rec("tensor", _wait, sem="sSC", val=v_qe[h])
            if h >= 1:
                rec("tensor", _wait, sem="sSC", val=v_ae[h - 1])
                rec("tensor", _wait, sem="sVE", val=v_qp[h - 1])
            rec("tensor", _mm, out=bank(1), lhsT=fm_sb[:, 0, :], rhs=Khat2[:, h % 2, :],
                start=True, stop=True)
            rec("tensor", _mm, out=bank(5), lhsT=fm_sb[:, 1, :], rhs=Khat2[:, h % 2, :],
                start=True, stop=True, sem="sPE")
            v_fm[h] = adv("sPE", 1)

        def b_fme(h):
            rec("scalar", _wait, sem="sPE", val=v_fm[h])
            if h >= 1:
                rec("scalar", _wait, sem="sVE", val=v_qp[h - 1])
            rec("scalar", _mul, out=A1s[:], in_=bank(1), mul=S_ATTN, sem="sSC")
            v_ae[h] = adv("sSC", 1)
            rec("vector", _wait, sem="sSC", val=v_ae[h])
            rec("vector", _tt, out=Qp_fm[:, h, :], in0=A1s[:], in1=bank(5), op=ALU.mult, sem="sVE")
            v_qp[h] = adv("sVE", 1)

        v_h1 = []
        e1 = dict(next=0, w0=0)
        E1BANKS = [2, 3, 6, 7]

        def emit_e1_group():
            og_abs = e1["next"]
            if og_abs >= F1 // 128:
                return
            e1["next"] += 1
            if og_abs % 4 == 0:
                os_ = og_abs // 4
                rec("sync", _wait, sem="sPE", val=max(last["w0_user"], last["e_pe_done"]))
                rec("sync", _dma, out=w01_sb[:],
                    in_=w0_scr[:, os_ * 512:(os_ + 1) * 512].rearrange(
                        "(cg p) o -> p cg o", p=128), sem="sW0")
                e1["w0"] = adv("sW0", 16)
            bk = E1BANKS[og_abs % 4]
            rec("tensor", _wait, sem="sW0", val=e1["w0"])
            if og_abs == 0:
                rec("tensor", _wait, sem="sVE", val=v_route)
            if len(v_h1) >= 4:
                rec("tensor", _wait, sem="sSC", val=v_h1[-4])
            for cg in range(CG):
                rec("tensor", _mm, out=bank(bk),
                    lhsT=w01_sb[:, cg, (og_abs % 4) * 128:(og_abs % 4) * 128 + 128],
                    rhs=xn_fm[:, cg, :], start=(cg == 0), stop=(cg == CG - 1),
                    sem="sPE" if cg == CG - 1 else None)
            vm = adv("sPE", 1)
            last["w0_user"] = vm
            rec("scalar", _wait, sem="sPE", val=vm)
            rec("scalar", _act, out=H1[:, og_abs, :], in_=bank(bk), func=AF.Relu,
                scale=scol[:, 0:1], sem="sSC")
            v_h1.append(adv("sSC", 1))

        b_qhat(0); b_ke(0); b_qhat(1); b_ke(1)
        for h in range(H):
            b_fm(h)
            if h + 2 < H:
                b_qhat(h + 2); b_ke(h + 2)
            b_fme(h)
            for _ in range(4):
                emit_e1_group()
        while e1["next"] < F1 // 128:
            emit_e1_group()
        v_h1_done = v_h1[-1]
        v_qp_done = v_qp[-1]
        last["kv_user"] = max(last["kv_user"], v_fm[-1])

        fil = dict(ring=0, cons=[0, 0], vcons=[0, 0], next_h2=0, v_h2_last=0,
                   w1_user=last["w0_user"], slot=0, nslots=nexp * 18 + NTC)

        def maybe_fill():
            fil["slot"] += 1
            target = (F1 // 128) * fil["slot"] // fil["nslots"]
            while fil["next_h2"] < target:
                emit_h2_slice()

        def emit_h2_slice():
            o2g_abs = fil["next_h2"]
            if o2g_abs >= F1 // 128:
                return
            fil["next_h2"] += 1
            off = 3584
            rec("sync", _wait, sem="sPE", val=fil["w1_user"])
            rec("sync", _dma, out=w1v_sb[:],
                in_=w1_scr[:, :, o2g_abs * 128:(o2g_abs + 1) * 128].rearrange("g p o -> p g o"),
                sem="sW1A")
            v_w1 = adv("sW1A", 16)
            rec("tensor", _wait, sem="sW1A", val=v_w1)
            rec("tensor", _wait, sem="sSC", val=max(v_h1_done, fil["cons"][0]))
            rec("tensor", _wait, sem="sVE", val=fil["vcons"][0])
            for o1g in range(F1 // 128):
                rec("tensor", _mm, out=pflat(off, 512),
                    lhsT=w1v_sb[:, o1g, :],
                    rhs=H1[:, o1g, :], start=(o1g == 0), stop=(o1g == F1 // 128 - 1),
                    sem="sPE" if o1g == F1 // 128 - 1 else None)
            vm = adv("sPE", 1)
            fil["w1_user"] = vm
            rec("scalar", _wait, sem="sPE", val=vm)
            rec("scalar", _act, out=H2[:, o2g_abs, :], in_=pflat(off, 512), func=AF.Relu,
                scale=scol[:, 1:2], sem="sSC")
            fil["cons"][0] = adv("sSC", 1)
            fil["v_h2_last"] = fil["cons"][0]

        for ei in range(nexp):
            rec("sync", _wait, sem="sG2", val=v_kvscr)
            rec("sync", _wait, sem="sPE", val=max(last["kv_user"], last["e_pe_done"]))
            rec("sync", _dma, out=kvw_sb[:],
                in_=kv_scr[ei, 0].rearrange("(cg p) o -> p cg o", p=128), sem="sKV")
            v_kv = adv("sKV", 16)
            kw_v = kvw_sb
            vw_v = kvw_sb


            v_kh = [0] * H
            v_ke = [0] * H
            v_fmk = [0] * H
            v_kph = [0] * H
            v_aek = [0] * H
            rec("tensor", _wait, sem="sVE", val=max(v_qp_done, last["c5_ve_done"]))
            rec("tensor", _wait, sem="sSC", val=v_h1_done)

            def c_khat(h):
                kb = [0, 4, 6][h % 3]
                rec("tensor", _wait, sem="sKV", val=v_kv)
                if h >= 3:
                    rec("tensor", _wait, sem="sSC", val=v_ke[h - 3])
                for cg in range(CG):
                    rec("tensor", _mm, out=bank(kb), lhsT=kw_v[:, cg, h * 128:(h + 1) * 128],
                        rhs=xn_fm[:, cg, :], start=(cg == 0), stop=(cg == CG - 1),
                        sem="sPE" if cg == CG - 1 else None)
                v_kh[h] = adv("sPE", 1)

            def c_ke(h):
                kb = [0, 4, 6][h % 3]
                rec("scalar", _wait, sem="sPE", val=v_kh[h])
                rec("scalar", _scp, out=Khat2[:, h % 3, :], in_=bank(kb), sem="sSC")
                v_ke[h] = adv("sSC", 1)

            def c_fm(h):
                rec("tensor", _wait, sem="sSC", val=v_ke[h])
                if h >= 1:
                    rec("tensor", _wait, sem="sSC", val=v_aek[h - 1])
                    rec("tensor", _wait, sem="sVE", val=v_kph[h - 1])
                rec("tensor", _mm, out=bank(1), lhsT=fm_sb[:, 2, :], rhs=Khat2[:, h % 3, :],
                    start=True, stop=True)
                rec("tensor", _mm, out=bank(5), lhsT=fm_sb[:, 3, :], rhs=Khat2[:, h % 3, :],
                    start=True, stop=True)
                for tc in range(NTC):
                    rec("tensor", _mm, out=bank(2, tc * 128, 128),
                        lhsT=Khat2[:, h % 3, tc * 128:(tc + 1) * 128], rhs=fm_sb[:, 2, :],
                        start=True, stop=True)
                for tc in range(NTC):
                    rec("tensor", _mm, out=bank(3, tc * 128, 128),
                        lhsT=Khat2[:, h % 3, tc * 128:(tc + 1) * 128], rhs=fm_sb[:, 3, :],
                        start=True, stop=True, sem="sPE" if tc == NTC - 1 else None)
                v_fmk[h] = adv("sPE", 1)

            def c_fme(h):
                rec("scalar", _wait, sem="sPE", val=v_fmk[h])
                if h >= 1:
                    rec("scalar", _wait, sem="sVE", val=v_kph[h - 1])
                rec("scalar", _scp, out=A1s[:], in_=bank(1), sem="sSC")
                rec("scalar", _scp, out=A1s2[:], in_=bank(2), sem="sSC")
                adv("sSC", 1)
                v_aek[h] = adv("sSC", 1)
                rec("vector", _wait, sem="sSC", val=v_aek[h])
                rec("vector", _tt, out=Kp_fm[:, h, :], in0=A1s[:], in1=bank(5), op=ALU.mult,
                    sem="sVE")
                rec("vector", _tt, out=Kp_tm[:, :, h, :],
                    in0=A1s2[:].rearrange("p (a d) -> p a d", a=NTC),
                    in1=bank(3).rearrange("p (a d) -> p a d", a=NTC), op=ALU.mult, sem="sVE")
                adv("sVE", 1)
                v_kph[h] = adv("sVE", 1)

            c_khat(0); c_ke(0); c_khat(1); c_ke(1); c_khat(2); c_ke(2)
            for h in range(H):
                c_fm(h)
                if h + 3 < H:
                    c_khat(h + 3); c_ke(h + 3)
                c_fme(h)
                maybe_fill()

            rec("sync", _wait, sem="sPE", val=v_kh[H - 1])
            rec("sync", _dma, out=kvw_sb[:],
                in_=kv_scr[ei, 1].rearrange("(cg p) o -> p cg o", p=128), sem="sKV")
            v_kv2 = adv("sKV", 16)

            rec("vector", _wait, sem="sVE", val=v_route)
            v_vp = []
            v_vp_pe = 0
            for tc in range(NTC):
                for vs in range(2):
                    bk = [0, 1, 5][(tc * 2 + vs) % 3]
                    if tc == 0 and vs == 0:
                        rec("tensor", _wait, sem="sKV", val=v_kv2)
                    if len(v_vp) < 3:
                        rec("tensor", _wait, sem="sVE", val=v_kph[-1])
                    if len(v_vp) >= 3:
                        rec("tensor", _wait, sem="sVE", val=v_vp[-3])
                    for cg in range(CG):
                        rec("tensor", _mm, out=bank(bk),
                            lhsT=xn_fm[:, cg, tc * 128:(tc + 1) * 128],
                            rhs=vw_v[:, cg, vs * 512:(vs + 1) * 512],
                            start=(cg == 0), stop=(cg == CG - 1),
                            sem="sPE" if cg == CG - 1 else None)
                    vvh = adv("sPE", 1)
                    v_vp_pe = vvh
                    rec("vector", _wait, sem="sPE", val=vvh)
                    rec("vector", _ts, out=Vp_tm[:, tc, vs * 4:(vs + 1) * 4, :],
                        in0=bank(bk).rearrange("p (a d) -> p a d", a=4),
                        s1=mask01[:, tc, ei:ei + 1], op0=ALU.mult, sem="sVE")
                    v_vp.append(adv("sVE", 1))
                if tc % 2 == 1:
                    maybe_fill()
            v_vp_done = v_vp[-1]
            last["kv_user"] = v_vp_pe

            has_state = not first_seg
            rec("tensor", _wait, sem="sVE", val=v_vp_done)
            rec("tensor", _wait, sem="sSC", val=C["sSC"])
            v_sbfcopy = C["sSC"]
            if not first_seg:
                rec("sync", _wait, sem="sSV", val=C["sSV"])
                rec("sync", _wait, sem="sSC", val=C["sSC"])
                rec("sync", _wait, sem="sVE", val=C["sVE"])
                rec("sync", _dma, out=S_mast[:], in_=st_scr[ei], sem="sST")
                v_strest = adv("sST", 16)
                rec("scalar", _wait, sem="sST", val=v_strest)
                rec("scalar", _wait, sem="sPE", val=v_vp_pe)
                rec("scalar", _scp, out=S_bf[:].rearrange("p h v -> p (h v)"),
                    in_=S_mast[:].rearrange("p h v -> p (h v)"), sem="sSC")
                v_sbfcopy = adv("sSC", 1)

            p_even = [0, 512, 1024, 896]
            p_odd = [1536, 2048, 1280, 2432]
            a_off = [0, 512, 896, 1152]
            slot_n = [512, 384, 256, 128]
            v_aT = [0] * H
            v_omm = [0] * H
            v_dtt = [0] * H
            v_oc = [0] * H
            v_oev = [0] * H
            v_dadd = [0] * H

            def c5_delta(j):
                off = 3584
                rec("tensor", _wait, sem="sSC", val=fil["cons"][0])
                rec("tensor", _wait, sem="sVE", val=fil["vcons"][0])
                for jc in range(NTC):
                    rec("tensor", _mm, out=pflat(off, 128),
                        lhsT=Kp_tm[:, jc, j, :], rhs=Vp_tm[:, jc, j, :],
                        start=(jc == 0), stop=(jc == NTC - 1),
                        sem="sPE" if jc == NTC - 1 else None)
                vm = adv("sPE", 1)
                rec("vector", _wait, sem="sPE", val=vm)
                if j == 0:
                    rec("vector", _wait, sem="sSC", val=v_sbfcopy)
                    if first_seg:
                        rec("vector", _wait, sem="sSV", val=C["sSV"])
                if first_seg:
                    rec("vector", _cp, out=S_mast[:, j, :], in_=pflat(off, 128), sem="sVE")
                else:
                    rec("vector", _stt, out=S_mast[:, j, :], in0=pflat(off, 128),
                        scalar=1.0, in1=S_mast[:, j, :], op0=ALU.mult, op1=ALU.add, sem="sVE")
                fil["vcons"][0] = adv("sVE", 1)
                v_dadd[j] = fil["vcons"][0]

            def c5_av(j):
                par = j % 2
                o_off = 2560 if par == 0 else 3072
                rec("tensor", _wait, sem="sVE", val=v_dtt[j])
                rec("tensor", _wait, sem="sSC", val=v_oc[j])
                if j >= 2:
                    rec("tensor", _wait, sem="sVE", val=v_oev[j - 2])
                if has_state and j == 0:
                    rec("tensor", _wait, sem="sSC", val=v_sbfcopy)
                for il in range(NTC):
                    nmm = il + 1 + (1 if has_state else 0)
                    k = 0
                    for jc in range(il + 1):
                        k += 1
                        rec("tensor", _mm, out=pflat(o_off + il * 128, 128),
                            lhsT=aT_big[:, par, a_off[jc] + (il - jc) * 128:
                                        a_off[jc] + (il - jc) * 128 + 128],
                            rhs=Vp_tm[:, jc, j, :], start=(k == 1), stop=(k == nmm),
                            sem="sPE" if (il == NTC - 1 and k == nmm) else None)
                    if has_state:
                        k += 1
                        rec("tensor", _mm, out=pflat(o_off + il * 128, 128),
                            lhsT=Qp_fm[:, j, il * 128:(il + 1) * 128], rhs=S_bf[:, j, :],
                            start=False, stop=True,
                            sem="sPE" if (il == NTC - 1) else None)
                v_omm[j] = adv("sPE", 1)
                rec("vector", _wait, sem="sPE", val=v_omm[j])
                for il in range(NTC):
                    wcol = wgt01[:, il, ei:ei + 1]
                    if ei == 0:
                        rec("vector", _ts, out=O_acc[:, il, j * 128:(j + 1) * 128],
                            in0=pflat(o_off + il * 128, 128), s1=wcol, op0=ALU.mult, sem="sVE")
                    else:
                        rec("vector", _stt, out=O_acc[:, il, j * 128:(j + 1) * 128],
                            in0=pflat(o_off + il * 128, 128), scalar=wcol,
                            in1=O_acc[:, il, j * 128:(j + 1) * 128],
                            op0=ALU.mult, op1=ALU.add, sem="sVE")
                adv("sVE", 3)
                v_oev[j] = adv("sVE", 1)

            for h in range(H):
                par = h % 2
                poff = p_even if par == 0 else p_odd
                if h >= 2:
                    rec("tensor", _wait, sem="sVE", val=v_dtt[h - 2])
                    rec("tensor", _wait, sem="sSC", val=v_oc[h - 2])
                for jc in (0, 1, 3):
                    rec("tensor", _mm, out=pflat(poff[jc], slot_n[jc]),
                        lhsT=Kp_fm[:, h, jc * 128:(jc + 1) * 128],
                        rhs=Qp_fm[:, h, jc * 128:512], start=True, stop=True)
                c5_delta(h)
                if h >= 1:
                    rec("tensor", _wait, sem="sVE", val=v_dtt[h - 1])
                    rec("tensor", _wait, sem="sSC", val=v_oc[h - 1])
                rec("tensor", _mm, out=pflat(poff[2], slot_n[2]),
                    lhsT=Kp_fm[:, h, 256:384], rhs=Qp_fm[:, h, 256:512],
                    start=True, stop=True, sem="sPE")
                v_aT[h] = adv("sPE", 1)
                rec("vector", _wait, sem="sPE", val=v_aT[h])
                for jc in range(NTC):
                    rec("vector", _tt, out=aT_big[:, par, a_off[jc]: a_off[jc] + 128],
                        in0=pflat(poff[jc], 128), in1=tri_s[:], op=ALU.mult,
                        sem="sVE")
                adv("sVE", 3)
                v_dtt[h] = adv("sVE", 1)
                rec("scalar", _wait, sem="sPE", val=v_aT[h])
                rec("scalar", _wait, sem="sVE", val=v_dtt[h])
                for jc, w in ((0, 384), (1, 256), (2, 128)):
                    rec("scalar", _scp,
                        out=aT_big[:, par, a_off[jc] + 128: a_off[jc] + 128 + w],
                        in_=pflat(poff[jc] + 128, w), sem="sSC")
                adv("sSC", 2)
                v_oc[h] = adv("sSC", 1)

                if h >= 1:
                    c5_av(h - 1)
                maybe_fill()
            c5_av(H - 1)

            if not last_seg:
                rec("scalar", _wait, sem="sVE", val=v_dadd[H - 1])
                rec("scalar", _dma, out=st_scr[ei], in_=S_mast[:], sem="sSV")
                last["stsave"][ei] = adv("sSV", 16)
            last["c5_ve_done"] = C["sVE"]

        c_pe_done = C["sPE"]

        v_onrm_tr = [0] * NTC
        rec("vector", _wait, sem="sVE", val=v_oev[H - 1])
        vsq_d = 0
        for tc in range(NTC):
            o3 = O_acc[:, tc, :].rearrange("p (h v) -> p h v", h=H)
            if vsq_d:
                rec("vector", _wait, sem="sVE", val=C["sVE"])
            rec("vector", _tt, out=tmp_sq[:].rearrange("p (h v) -> p h v", h=H),
                in0=o3, in1=o3, op=ALU.mult, sem="sVE")
            vsq_d = adv("sVE", 1)
            rec("vector", _wait, sem="sVE", val=vsq_d)
            rec("vector", _red, out=rms_s[:, tc, :],
                in_=tmp_sq[:].rearrange("p (h v) -> p h v", h=H), op=ALU.add, sem="sVE")
            adv("sVE", 1)
        vred_d = C["sVE"]
        rec("scalar", _wait, sem="sVE", val=vred_d)
        rec("scalar", _act, out=rms_s[:].rearrange("p a h -> p (a h)"),
            in_=rms_s[:].rearrange("p a h -> p (a h)"), func=AF.Ln, scale=1.0 / DK,
            bias=1e-5, sem="sSC")
        vln_d = adv("sSC", 1)
        rec("scalar", _wait, sem="sSC", val=vln_d)
        rec("scalar", _act, out=rms_r[:].rearrange("p a h -> p (a h)"),
            in_=rms_s[:].rearrange("p a h -> p (a h)"), func=AF.Exp, scale=-0.5, sem="sSC")
        vrs_d = adv("sSC", 1)
        for tc in range(NTC):
            maybe_fill()
            o3 = O_acc[:, tc, :].rearrange("p (h v) -> p h v", h=H)
            rec("vector", _wait, sem="sSC", val=vrs_d)
            if tc > 0:
                rec("vector", _wait, sem="sPE", val=v_onrm_tr[tc - 1])
            rb = rms_r[:, tc, :].unsqueeze(2).broadcast_to([128, H, DK])
            rec("vector", _tt, out=Onrm[:].rearrange("p (h v) -> p h v", h=H),
                in0=o3, in1=rb, op=ALU.mult, sem="sVE")
            vON = adv("sVE", 1)
            rec("tensor", _wait, sem="sVE", val=vON)
            rec("tensor", _wait, sem="sVE", val=last["b2_free"])
            for g in range(CG):
                rec("tensor", _tr, out=bankbf(2, g * 128, 128),
                    in_=Onrm[:, g * 128:(g + 1) * 128], ident=identb[:],
                    sem="sPE" if g == CG - 1 else None)
            v_onrm_tr[tc] = adv("sPE", 1)
            rec("vector", _wait, sem="sPE", val=v_onrm_tr[tc])
            rec("vector", _cp, out=Onrm_fm[:, :, tc * 128:(tc + 1) * 128],
                in_=bankbf(2, 0, 1024).rearrange("p (g t) -> p g t", g=CG), sem="sVE")
            vev = adv("sVE", 1)
            last["b2_free"] = vev
        v_onrm_done = last["b2_free"]

        while fil["next_h2"] < F1 // 128:
            emit_h2_slice()
        v_h2_done = fil["v_h2_last"]

        rec("sync", _wait, sem="sPE", val=c_pe_done)
        rec("sync", _wait, sem="sOW", val=C["sOW"])
        rec("sync", _dma, out=owT_sb[:],
            in_=ow_scr.rearrange("(cg p) o -> p cg o", p=128), sem="sOW")
        v_ow = adv("sOW", 16)
        rec("tensor", _wait, sem="sSC", val=v_h2_done)
        rec("tensor", _wait, sem="sVE", val=v_onrm_done)
        rec("tensor", _wait, sem="sOW", val=v_ow)
        for tc in range(NTC):
            for g in range(CG):
                for ds in range(2):
                    rec("tensor", _mm, out=pflat(tc * 1024 + ds * 512, 512),
                        lhsT=Onrm_fm[:, g, tc * 128:(tc + 1) * 128],
                        rhs=owT_sb[:, g, ds * 512:(ds + 1) * 512],
                        start=(g == 0), stop=False,
                        sem="sPE" if (tc == NTC - 1 and g == CG - 1 and ds == 1) else None)
        v_attn_mm = adv("sPE", 1)
        for o2s in range(8):
            par = o2s % 2
            w2sem = "sW2" if par == 0 else "sW2B"
            rec("sync", _wait, sem="sPE", val=max(last["w2_user"][par], c_pe_done))
            rec("sync", _dma, out=w2_sb[:, par],
                in_=w2_scr[o2s * 512:(o2s + 1) * 512, :].rearrange("(s p) d -> p s d", p=128),
                sem=w2sem)
            v_w2 = adv(w2sem, 16)
            rec("tensor", _wait, sem=w2sem, val=v_w2)
            lastslice = (o2s == 7)
            for tc in range(NTC):
                for sub in range(4):
                    o2g = o2s * 4 + sub
                    for ds in range(2):
                        is_last = lastslice and sub == 3
                        rec("tensor", _mm,
                            out=pflat(tc * 1024 + ds * 512, 512),
                            lhsT=H2[:, o2g, tc * 128:(tc + 1) * 128],
                            rhs=w2_sb[:, par, sub, ds * 512:(ds + 1) * 512],
                            start=False, stop=is_last,
                            sem="sPE" if (tc == NTC - 1 and sub == 3 and ds == 1) else None)
            last["w2_user"][par] = adv("sPE", 1)
        v_e3 = C["sPE"]
        last["w2_user"] = [v_e3, v_e3]
        for tc in range(NTC):
            par = tc % 2
            rec("vector", _wait, sem="sPE", val=v_e3)
            if tc >= 2:
                rec("vector", _wait, sem="sOUT", val=C["sOUT"])
            rec("vector", _tt, out=out_sb[:, par, :], in0=pflat(tc * 1024, 1024),
                in1=x_seg[:, tc, :], op=ALU.add, sem="sVE")
            vev = adv("sVE", 1)
            rec("scalar", _wait, sem="sVE", val=vev)
            rec("scalar", _dma, out=out_d[b, t0 + tc * 128: t0 + (tc + 1) * 128, :],
                in_=out_sb[:, par, :], sem="sOUT")
            last["out_user"][par] = adv("sOUT", 16)
        last["e3_evict"] = C["sVE"]
        last["b2_free"] = C["sVE"]
        last["e_pe_done"] = v_e3

        if cfg["debug"] and seg == 0:
            rec("scalar", _wait, sem="sVE", val=C["sVE"])
            for name, srcap in [("xn_fm", xn_fm[:]), ("wgt01", wgt01[:]), ("mask01", mask01[:]),
                              ("Qp", Qp_fm[:]), ("Oacc", O_acc[:]), ("H1", H1[:]),
                              ("H2", H2[:]), ("Onrmfm", Onrm_fm[:])]:
                rec("scalar", _dma, out=dbg[name], in_=srcap, sem="sOUT")
                adv("sOUT", 16)

    rec("scalar", _wait, sem="sOUT", val=C["sOUT"])
    rec("scalar", _wait, sem="sSV", val=C["sSV"])

    chk = {k: 0 for k in SEMS}
    for _en, _fn, _kw in REC:
        s = _kw.get("sem")
        if s and _fn is not _wait:
            chk[s] += 16 if _fn is _dma else 1
    chk["sGP"] += 15
    assert chk == C, f"sem count mismatch: {chk} vs {C}"

    with es:
        block = nc.Block()
        with block:
            for eng_name in ["sync", "gpsimd", "tensor", "vector", "scalar"]:
                def body(e, _n=eng_name):
                    for en, fn, kw in REC:
                        if en == _n:
                            fn(e, **kw)
                getattr(block, eng_name)(body)
    return nc


def kernel(**inputs):
    cfg = dict(DEFAULT_CFG)
    x = np.ascontiguousarray(inputs["L0_attn_x"])
    n_cores = 8
    shared = make_shared_inputs(inputs)
    nc = build_program(cfg)
    in_maps = []
    for c in range(n_cores):
        m = dict(shared)
        m["x"] = np.ascontiguousarray(x[2 * c: 2 * c + 2])
        in_maps.append(m)
    res = run_bass_kernel_spmd(nc, in_maps, core_ids=list(range(n_cores)))
    return np.concatenate([r["out"] for r in res.results], axis=0)


def make_shared_inputs(inputs):
    return dict(
        gate_wT=np.ascontiguousarray(inputs["L0_attn_gate_w"].T),
        q_wT=np.ascontiguousarray(inputs["L0_attn_q_w"].T),
        k_wT=np.ascontiguousarray(np.transpose(inputs["L0_attn_k_w"], (0, 2, 1))),
        v_wT=np.ascontiguousarray(np.transpose(inputs["L0_attn_v_w"], (0, 2, 1))),
        fmq1T=np.ascontiguousarray(inputs["L0_attn_fmq1_w"].T),
        fmq2T=np.ascontiguousarray(inputs["L0_attn_fmq2_w"].T),
        fmk1T=np.ascontiguousarray(inputs["L0_attn_fmk1_w"].T),
        fmk2T=np.ascontiguousarray(inputs["L0_attn_fmk2_w"].T),
        norm_w=np.ascontiguousarray(np.asarray(inputs["L0_attn_norm_w"]).reshape(DK, 1)),
        o_wT=np.ascontiguousarray(inputs["L0_attn_o_w"].T),
        k0=np.ascontiguousarray(inputs["L0_ffn_k0"]).view(np.uint8),
        k1=np.ascontiguousarray(inputs["L0_ffn_k1"]).view(np.uint8),
        k2=np.ascontiguousarray(inputs["L0_ffn_k2"]).view(np.uint8),
        s0=np.asarray(inputs["L0_ffn_s0"], dtype=np.float32).reshape(1, 1),
        s1=np.asarray(inputs["L0_ffn_s1"], dtype=np.float32).reshape(1, 1),
        s2=np.asarray(inputs["L0_ffn_s2"], dtype=np.float32).reshape(1, 1),
    )


# version 12
# speedup vs baseline: 1.1283x; 1.0661x over previous
import numpy as np
import ml_dtypes
from contextlib import ExitStack
import concourse.bass as bass
import concourse.mybir as mybir
from concourse.bass_utils import run_bass_kernel_spmd

F32 = mybir.dt.float32
BF16 = mybir.dt.bfloat16
U8 = mybir.dt.uint8
AF = mybir.ActivationFunctionType
ALU = mybir.AluOpType
AX = mybir.AxisListType

D = 1024
E_EXP = 8
H = 8
DK = 128
SEG = 512
NTC = SEG // 128
CG = D // 128
S_ATTN = float(DK) ** -0.5
F1 = D * 4

DEFAULT_CFG = dict(nbatch=2, segs_per_batch=8, nexp=8, debug=False)


def build_program(cfg):
    nb = cfg["nbatch"]
    spb = cfg["segs_per_batch"]
    nexp = cfg["nexp"]
    seqlen = spb * SEG
    nseg = nb * spb

    nc = bass.Bass("TRN2", target_bir_lowering=False)

    x_d = nc.dram_tensor("x", [nb, seqlen, D], F32, kind="ExternalInput").ap()
    gate_d = nc.dram_tensor("gate_wT", [D, E_EXP], F32, kind="ExternalInput").ap()
    qwT_d = nc.dram_tensor("q_wT", [D, D], F32, kind="ExternalInput").ap()
    kwT_d = nc.dram_tensor("k_wT", [E_EXP, D, D], F32, kind="ExternalInput").ap()
    vwT_d = nc.dram_tensor("v_wT", [E_EXP, D, D], F32, kind="ExternalInput").ap()
    fm_d = [nc.dram_tensor(n, [DK, DK], F32, kind="ExternalInput").ap()
            for n in ["fmq1T", "fmq2T", "fmk1T", "fmk2T"]]
    nw_d = nc.dram_tensor("norm_w", [DK, 1], F32, kind="ExternalInput").ap()
    owT_d = nc.dram_tensor("o_wT", [D, D], F32, kind="ExternalInput").ap()
    k0_d = nc.dram_tensor("k0", [2 * D, F1], U8, kind="ExternalInput").ap()
    k1_d = nc.dram_tensor("k1", [F1, F1], U8, kind="ExternalInput").ap()
    k2_d = nc.dram_tensor("k2", [F1, D], U8, kind="ExternalInput").ap()
    s0_d = nc.dram_tensor("s0", [1, 1], F32, kind="ExternalInput").ap()
    s1_d = nc.dram_tensor("s1", [1, 1], F32, kind="ExternalInput").ap()
    s2_d = nc.dram_tensor("s2", [1, 1], F32, kind="ExternalInput").ap()
    out_d = nc.dram_tensor("out", [nb, seqlen, D], F32, kind="ExternalOutput").ap()

    kv_scr = nc.dram_tensor("kv_scr", [E_EXP, 2, D, D], BF16, kind="Internal").ap()
    w0_scr = nc.dram_tensor("w0_scr", [D, F1], BF16, kind="Internal").ap()
    w1_scr = nc.dram_tensor("w1_scr", [F1 // 128, 128, F1], BF16, kind="Internal").ap()
    w2_scr = nc.dram_tensor("w2_scr", [F1, D], BF16, kind="Internal").ap()
    st_scr = nc.dram_tensor("st_scr", [E_EXP, 128, H, DK], F32, kind="Internal").ap()
    qw_scr = nc.dram_tensor("qw_scr", [D, D], BF16, kind="Internal").ap()
    ow_scr = nc.dram_tensor("ow_scr", [D, D], BF16, kind="Internal").ap()

    dbg = {}
    if cfg["debug"]:
        for n, shape, dt in [
            ("xn_fm", [128, CG, SEG], BF16), ("wgt01", [128, NTC, E_EXP], F32),
            ("mask01", [128, NTC, E_EXP], F32), ("Qp", [128, H, SEG], BF16),
            ("Kp", [128, H, SEG], BF16), ("Kptm", [128, NTC, H, DK], BF16),
            ("Vp", [128, NTC, H, DK], BF16),
            ("Oacc", [128, NTC, D], F32), ("H1", [128, F1 // 128, SEG], BF16),
            ("H2", [128, F1 // 128, SEG], BF16), ("Onrmfm", [128, CG, SEG], BF16),
        ]:
            dbg[n] = nc.dram_tensor("dbg_" + n, shape, dt, kind="ExternalOutput").ap()

    es = ExitStack()
    SEMS = {}
    for s in ["sX", "sKV", "sQW", "sOW", "sST", "sW0", "sW1A", "sW1B", "sW2", "sW2B",
              "sP", "sOUT", "sSV", "sPS", "sG1", "sG2", "sPE", "sVE", "sSC", "sGP"]:
        SEMS[s] = es.enter_context(nc.semaphore(s))
    C = {k: 0 for k in SEMS}

    def adv(sem, k):
        C[sem] += k
        return C[sem]

    def sb(name, shape, dt):
        return es.enter_context(nc.sbuf_tensor(name, shape, dt))

    gate_sb = sb("gate_sb", [128, CG, E_EXP], BF16)
    fm_sb = sb("fm_sb", [128, 4, DK], BF16)
    identb = sb("identb", [128, 128], BF16)
    identf = sb("identf", [128, 128], F32)
    tri_s = sb("tri_s", [128, 128], F32)
    tri_b = sb("tri_b", [128, 128], BF16)
    onesr = sb("onesr", [1, 128], F32)
    scol = sb("scol", [128, 4], F32)
    nw_sb = sb("nw_sb", [128, 1], F32)
    ssc_sb = sb("ssc_sb", [1, 4], F32)
    S_mast = sb("S_mast", [128, H, DK], F32)
    S_bf = sb("S_bf", [128, H, DK], BF16)

    x_seg = sb("x_seg", [128, NTC, D], F32)
    xn_fm = sb("xn_fm", [128, CG, SEG], BF16)
    Qp_fm = sb("Qp_fm", [128, H, SEG], BF16)
    wgt01 = sb("wgt01", [128, NTC, E_EXP], F32)
    mask01 = sb("mask01", [128, NTC, E_EXP], F32)
    O_acc = sb("O_acc", [128, NTC, D], F32)
    Onrm_fm = sb("Onrm_fm", [128, CG, SEG], BF16)
    xn_tmp = sb("xn_tmp", [128, D], BF16)
    Onrm = xn_tmp
    tmp_sq = sb("tmp_sq", [128, D], F32)
    rt = sb("rt", [128, 24], F32)
    rt3a = sb("rt3a", [128, NTC, E_EXP], F32)
    rt3b = sb("rt3b", [128, NTC, E_EXP], F32)
    rt3c = sb("rt3c", [128, NTC, E_EXP], F32)
    rms_s = sb("rms_s", [128, NTC, H], F32)
    rms_r = sb("rms_r", [128, NTC, H], F32)
    cst = sb("cst", [128, 6], F32)
    _cvals = [1e-5, -0.5, 1.0 / DK, S_ATTN, 1.0, 0.0]
    for _ci, _cv in enumerate(_cvals):
        nc.const_aps.register(F32, _cv, cst[:, _ci:_ci + 1])

    ARENA_BYTES = 124928
    arena = sb("arena", [128, ARENA_BYTES // 2], BF16)

    def carve(off_bytes, shape, dt):
        nelem = 1
        for s in shape[1:]:
            nelem *= s
        esz = {BF16: 2, F32: 4, U8: 1}[dt]
        base = arena[:].bitcast(dt) if dt != BF16 else arena[:]
        v = base[:, off_bytes // esz: off_bytes // esz + nelem]
        if len(shape) > 2:
            dims = "".join(chr(ord('a') + i) for i in range(len(shape) - 1))
            v = v.rearrange(f"p ({' '.join(dims)}) -> p {' '.join(dims)}",
                            **{chr(ord('a') + i): shape[1 + i] for i in range(len(shape) - 1)})
        return v

    kvw_sb = carve(0, [128, CG, D], BF16)
    Khat2 = carve(16384, [128, 3, SEG], BF16)
    Kp_fm = carve(19456, [128, H, SEG], BF16)
    Kp_tm = carve(27648, [128, NTC, H, DK], BF16)
    Vp_tm = carve(35840, [128, NTC, H, DK], BF16)
    aT_big = carve(44032, [128, 2, 1280], BF16)
    A1s = carve(49152, [128, SEG], BF16)
    A1s2 = carve(50176, [128, SEG], BF16)
    H1 = carve(51200, [128, F1 // 128, SEG], BF16)
    H2 = carve(83968, [128, F1 // 128, SEG], BF16)
    w01_sb = carve(116736, [128, CG, 512], BF16)
    w1v_sb = carve(116736, [128, F1 // 128, 128], BF16)
    owT_sb = carve(0, [128, CG, D], BF16)
    out_sb = carve(16384, [128, 2, D], F32)
    w2_sb = carve(27648, [128, 2, 4, D], BF16)
    pstg_a = carve(0, [128, F1], U8)
    pstg_b = carve(4096, [128, F1], U8)
    pstg_f = carve(8192, [128, 512], F32)
    pstg_o = carve(10240, [128, F1], BF16)

    pALL = es.enter_context(nc.psum_tensor("pALL", [128, 4096], F32))

    def bank(i, lo=0, n=512):
        return pALL[:, i * 512 + lo: i * 512 + lo + n]

    def pflat(lo, n):
        return pALL[:, lo: lo + n]

    def bankbf(i, lo=0, n=1024):
        return pALL[:].bitcast(BF16)[:, i * 1024 + lo: i * 1024 + lo + n]

    def pbf(off, n):
        return pALL[:].bitcast(BF16)[:, off: off + n]

    REC = []

    def rec(eng, fn, **kw):
        REC.append((eng, fn, kw))

    def _dma(e, out=None, in_=None, sem=None):
        e.dma_start(out=out, in_=in_).then_inc(SEMS[sem], 16)

    def _wait(e, sem=None, val=0):
        e.wait_ge(SEMS[sem], val)

    def _mm(e, out=None, lhsT=None, rhs=None, start=True, stop=True, sem=None):
        m = e.matmul(out, lhsT, rhs, start=start, stop=stop)
        if sem:
            m.then_inc(SEMS[sem], 1)

    def _tr(e, out=None, in_=None, ident=None, sem=None):
        m = e.transpose(out, in_, ident)
        if sem:
            m.then_inc(SEMS[sem], 1)

    def _cp(e, out=None, in_=None, sem=None):
        m = e.tensor_copy(out=out, in_=in_)
        if sem:
            m.then_inc(SEMS[sem], 1)

    def _scp(e, out=None, in_=None, sem=None):
        m = e.copy(out=out, in_=in_)
        if sem:
            m.then_inc(SEMS[sem], 1)

    def _tt(e, out=None, in0=None, in1=None, op=None, sem=None):
        m = e.tensor_tensor(out=out, in0=in0, in1=in1, op=op)
        if sem:
            m.then_inc(SEMS[sem], 1)

    def _ts(e, out=None, in0=None, s1=None, s2=None, op0=None, op1=None, sem=None):
        kw = {}
        if op1 is not None:
            kw["op1"] = op1
        m = e.tensor_scalar(out=out, in0=in0, scalar1=s1, scalar2=s2, op0=op0, **kw)
        if sem:
            m.then_inc(SEMS[sem], 1)

    def _stt(e, out=None, in0=None, scalar=None, in1=None, op0=None, op1=None, sem=None):
        m = e.scalar_tensor_tensor(out=out, in0=in0, scalar=scalar, in1=in1, op0=op0, op1=op1)
        if sem:
            m.then_inc(SEMS[sem], 1)

    def _red(e, out=None, in_=None, op=None, sem=None):
        m = e.tensor_reduce(out=out, in_=in_, axis=AX.X, op=op)
        if sem:
            m.then_inc(SEMS[sem], 1)

    def _act(e, out=None, in_=None, func=None, bias=0.0, scale=1.0, sem=None):
        m = e.activation(out=out, in_=in_, func=func, bias=bias, scale=scale)
        if sem:
            m.then_inc(SEMS[sem], 1)

    def _mul(e, out=None, in_=None, mul=None, sem=None):
        m = e.mul(out=out, in_=in_, mul=mul)
        if sem:
            m.then_inc(SEMS[sem], 1)

    def _recip(e, out=None, in_=None, sem=None):
        m = e.reciprocal(out=out, in_=in_)
        if sem:
            m.then_inc(SEMS[sem], 1)

    def prep_consts(e):
        e.memset(identb[:], 0.0).then_inc(SEMS["sGP"], 1)
        e.memset(identf[:], 0.0).then_inc(SEMS["sGP"], 1)
        e.memset(tri_s[:], 1.0).then_inc(SEMS["sGP"], 1)
        e.memset(tri_b[:], 1.0).then_inc(SEMS["sGP"], 1)
        e.memset(onesr[:], 1.0).then_inc(SEMS["sGP"], 1)
        for _ci, _cv in enumerate([1e-5, -0.5, 1.0 / DK, S_ATTN, 1.0, 0.0]):
            e.memset(cst[:, _ci:_ci + 1], _cv).then_inc(SEMS["sGP"], 1)
        e.wait_ge(SEMS["sGP"], 11)
        e.affine_select(out=identb[:], in_=identb[:], compare_op=ALU.not_equal,
                        fill=1.0, base=0, pattern=[[-1, 128]],
                        channel_multiplier=1).then_inc(SEMS["sGP"], 1)
        e.affine_select(out=identf[:], in_=identf[:], compare_op=ALU.not_equal,
                        fill=1.0, base=0, pattern=[[-1, 128]],
                        channel_multiplier=1).then_inc(SEMS["sGP"], 1)
        e.affine_select(out=tri_s[:], in_=tri_s[:], compare_op=ALU.is_ge,
                        fill=0.0, base=0, pattern=[[1, 128]],
                        channel_multiplier=-1).then_inc(SEMS["sGP"], 1)
        e.affine_select(out=tri_b[:], in_=tri_b[:], compare_op=ALU.is_ge,
                        fill=0.0, base=0, pattern=[[1, 128]],
                        channel_multiplier=-1).then_inc(SEMS["sGP"], 1)
    rec("gpsimd", prep_consts)
    v_consts = adv("sGP", 15)

    small_loads = [
        (gate_sb[:], gate_d.rearrange("(cg p) e -> p cg e", p=128)),
    ]
    for i in range(4):
        small_loads.append((fm_sb[:, i, :], fm_d[i]))
    for dst, srcap in small_loads:
        rec("gpsimd", _dma, out=dst, in_=srcap, sem="sG1")
    v_small = adv("sG1", 16 * len(small_loads))

    rec("sync", _dma, out=nw_sb[:], in_=nw_d, sem="sP")
    rec("sync", _dma, out=ssc_sb[:, 0:1], in_=s0_d, sem="sP")
    rec("sync", _dma, out=ssc_sb[:, 1:2], in_=s1_d, sem="sP")
    rec("sync", _dma, out=ssc_sb[:, 2:3], in_=s2_d, sem="sP")
    v_scal = adv("sP", 64)
    rec("sync", _wait, sem="sP", val=v_scal)

    for e_i in range(nexp):
        rec("gpsimd", _dma, out=kv_scr[e_i, 0], in_=kwT_d[e_i], sem="sG2")
        rec("gpsimd", _dma, out=kv_scr[e_i, 1], in_=vwT_d[e_i], sem="sG2")
    rec("gpsimd", _dma, out=qw_scr, in_=qwT_d, sem="sG2")
    v_kvscr = adv("sG2", 16 * (2 * nexp + 1))

    rec("tensor", _wait, sem="sP", val=v_scal)
    rec("tensor", _wait, sem="sGP", val=v_consts)
    rec("tensor", _mm, out=bank(3, 0, 3), lhsT=onesr[:], rhs=ssc_sb[:, 0:3],
        start=True, stop=True, sem="sPE")
    v_scolmm = adv("sPE", 1)
    rec("vector", _wait, sem="sPE", val=v_scolmm)
    rec("vector", _ts, out=scol[:, 0:3], in0=bank(3, 0, 3), s1=2.0, op0=ALU.mult, sem="sVE")
    v1 = adv("sVE", 1)
    rec("vector", _wait, sem="sVE", val=v1)
    rec("vector", _ts, out=scol[:, 2:3], in0=bank(3, 2, 1), s1=4.0, op0=ALU.mult, sem="sVE")
    rec("vector", _ts, out=scol[:, 3:4], in0=bank(3, 2, 1), s1=2.0, op0=ALU.mult, sem="sVE")
    adv("sVE", 2)
    v_scol = C["sVE"]

    rec("vector", _wait, sem="sP", val=v_scal)
    for cg in range(CG):
        for os_ in range(2):
            srcap = owT_d.rearrange("(cg p) o -> cg p o", p=128)[cg][:, os_ * 512:(os_ + 1) * 512]
            rec("sync", _dma, out=pstg_f[:], in_=srcap, sem="sP")
            vload = adv("sP", 16)
            rec("vector", _wait, sem="sP", val=vload)
            rec("vector", _ts, out=pstg_o[:, 0:512], in0=pstg_f[:],
                s1=nw_sb[:, 0:1], op0=ALU.mult, sem="sVE")
            vconv = adv("sVE", 1)
            rec("scalar", _wait, sem="sVE", val=vconv)
            rec("scalar", _dma, out=ow_scr[cg * 128:(cg + 1) * 128, os_ * 512:(os_ + 1) * 512],
                in_=pstg_o[:, 0:512], sem="sPS")
            vstore = adv("sPS", 16)
            rec("sync", _wait, sem="sPS", val=vstore)

    for cg in range(CG):
        rec("sync", _dma, out=pstg_a[:], in_=k0_d[cg * 128:(cg + 1) * 128, :], sem="sP")
        rec("sync", _dma, out=pstg_b[:], in_=k0_d[D + cg * 128: D + (cg + 1) * 128, :], sem="sP")
        vload = adv("sP", 32)
        rec("vector", _wait, sem="sP", val=vload)
        rec("vector", _tt, out=pstg_o[:], in0=pstg_a[:], in1=pstg_b[:], op=ALU.subtract, sem="sVE")
        vconv = adv("sVE", 1)
        rec("scalar", _wait, sem="sVE", val=vconv)
        rec("scalar", _dma, out=w0_scr[cg * 128:(cg + 1) * 128, :], in_=pstg_o[:], sem="sPS")
        vstore = adv("sPS", 16)
        rec("sync", _wait, sem="sPS", val=vstore)

    for g in range(F1 // 128):
        rec("sync", _dma, out=pstg_a[:], in_=k1_d[g * 128:(g + 1) * 128, :], sem="sP")
        vload = adv("sP", 16)
        rec("vector", _wait, sem="sP", val=vload)
        rec("vector", _ts, out=pstg_o[:], in0=pstg_a[:], s1=2.0, s2=-1.0,
            op0=ALU.mult, op1=ALU.add, sem="sVE")
        vconv = adv("sVE", 1)
        rec("scalar", _wait, sem="sVE", val=vconv)
        rec("scalar", _dma, out=w1_scr[g], in_=pstg_o[:], sem="sPS")
        vstore = adv("sPS", 16)
        rec("sync", _wait, sem="sPS", val=vstore)

    rec("vector", _wait, sem="sVE", val=v_scol)
    for g in range(F1 // 128):
        rec("sync", _dma, out=pstg_a[:, 0:D], in_=k2_d[g * 128:(g + 1) * 128, :], sem="sP")
        vload = adv("sP", 16)
        rec("vector", _wait, sem="sP", val=vload)
        rec("vector", _ts, out=pstg_o[:, 0:D], in0=pstg_a[:, 0:D], s1=scol[:, 2:3],
            s2=scol[:, 3:4], op0=ALU.mult, op1=ALU.subtract, sem="sVE")
        vconv = adv("sVE", 1)
        rec("scalar", _wait, sem="sVE", val=vconv)
        rec("scalar", _dma, out=w2_scr[g * 128:(g + 1) * 128, :], in_=pstg_o[:, 0:D], sem="sPS")
        vstore = adv("sPS", 16)
        rec("sync", _wait, sem="sPS", val=vstore)

    v_prep_sS = C["sPS"]
    v_prep_sVE = C["sVE"]

    for eng in ["tensor", "vector", "scalar"]:
        rec(eng, _wait, sem="sG1", val=v_small)
        rec(eng, _wait, sem="sG2", val=v_kvscr)
        rec(eng, _wait, sem="sPS", val=v_prep_sS)
        rec(eng, _wait, sem="sVE", val=v_prep_sVE)
        rec(eng, _wait, sem="sGP", val=v_consts)

    last = dict(
        kv_user=0, e3_evict=0, b2_free=0, e_pe_done=0,
        w0_user=0, w1_user=[0, 0], w2_user=[0, 0], out_user=[0, 0],
        stsave=[0] * E_EXP, c5_ve_done=0,
    )

    for seg in range(nseg):
        b = seg // spb
        sib = seg % spb
        t0 = sib * SEG
        first_seg = (sib == 0)
        last_seg = (sib == spb - 1)

        rec("sync", _wait, sem="sVE", val=last["e3_evict"])
        rec("sync", _wait, sem="sX", val=C["sX"])
        if cfg["debug"] and seg > 0:
            for eng in ["vector", "scalar", "tensor"]:
                rec(eng, _wait, sem="sOUT", val=C["sOUT"])
        for tc in range(NTC):
            rec("sync", _dma, out=x_seg[:, tc, :],
                in_=x_d[b, t0 + tc * 128: t0 + (tc + 1) * 128, :], sem="sX")
            adv("sX", 16)
        xw_total = C["sX"]
        xw = [xw_total] * NTC

        v_xn_tr = [0] * NTC
        v_xnfm = [0] * NTC
        xall = x_seg[:].rearrange("p a d -> p (a d)")
        tmp_sq4 = O_acc[:].rearrange("p a d -> p (a d)")
        rec("vector", _wait, sem="sX", val=xw[-1])
        rec("vector", _wait, sem="sVE", val=last["e3_evict"])
        rec("vector", _tt, out=tmp_sq4, in0=xall, in1=xall, op=ALU.mult, sem="sVE")
        vsq = adv("sVE", 1)
        rec("vector", _red, out=rt[:, 0:NTC], in_=x_seg[:], op=ALU.add, sem="sVE")
        adv("sVE", 1)
        rec("vector", _wait, sem="sVE", val=vsq)
        rec("vector", _red, out=rt[:, 4:4 + NTC],
            in_=tmp_sq4.rearrange("p (a d) -> p a d", a=NTC), op=ALU.add, sem="sVE")
        vred = adv("sVE", 1)
        rec("vector", _wait, sem="sVE", val=vred)
        rec("vector", _ts, out=rt[:, 0:NTC], in0=rt[:, 0:NTC], s1=1.0 / D, op0=ALU.mult, sem="sVE")
        vm_ = adv("sVE", 1)
        rec("vector", _wait, sem="sVE", val=vm_)
        rec("vector", _ts, out=rt[:, 4:4 + NTC], in0=rt[:, 4:4 + NTC], s1=1.0 / D,
            op0=ALU.mult, sem="sVE")
        vq_ = adv("sVE", 1)
        rec("vector", _wait, sem="sVE", val=vq_)
        rec("vector", _tt, out=rt[:, 8:8 + NTC], in0=rt[:, 0:NTC], in1=rt[:, 0:NTC],
            op=ALU.mult, sem="sVE")
        vmm2 = adv("sVE", 1)
        rec("vector", _wait, sem="sVE", val=vmm2)
        rec("vector", _tt, out=rt[:, 8:8 + NTC], in0=rt[:, 4:4 + NTC], in1=rt[:, 8:8 + NTC],
            op=ALU.subtract, sem="sVE")
        vvar = adv("sVE", 1)
        rec("scalar", _wait, sem="sVE", val=vvar)
        rec("scalar", _act, out=rt[:, 8:8 + NTC], in_=rt[:, 8:8 + NTC], func=AF.Ln, bias=1e-5,
            sem="sSC")
        vln = adv("sSC", 1)
        rec("scalar", _wait, sem="sSC", val=vln)
        rec("scalar", _act, out=rt[:, 12:12 + NTC], in_=rt[:, 8:8 + NTC], func=AF.Exp, scale=-0.5,
            sem="sSC")
        vrs = adv("sSC", 1)
        rec("vector", _wait, sem="sSC", val=vrs)
        rec("vector", _tt, out=rt[:, 16:16 + NTC], in0=rt[:, 0:NTC], in1=rt[:, 12:12 + NTC],
            op=ALU.mult, sem="sVE")
        vmr = adv("sVE", 1)
        rec("vector", _wait, sem="sVE", val=vmr)
        rec("vector", _ts, out=rt[:, 16:16 + NTC], in0=rt[:, 16:16 + NTC], s1=-1.0,
            op0=ALU.mult, sem="sVE")
        vnm = adv("sVE", 1)
        for tc in range(NTC):
            rec("scalar", _wait, sem="sVE", val=vnm)
            if tc > 0:
                rec("scalar", _wait, sem="sPE", val=v_xn_tr[tc - 1])
            rec("scalar", _act, out=xn_tmp[:], in_=x_seg[:, tc, :], func=AF.Identity,
                scale=rt[:, 12 + tc:13 + tc], bias=rt[:, 16 + tc:17 + tc], sem="sSC")
            vxn = adv("sSC", 1)
            rec("tensor", _wait, sem="sSC", val=vxn)
            rec("tensor", _wait, sem="sVE", val=last["b2_free"])
            for g in range(CG):
                rec("tensor", _tr, out=bankbf(2, g * 128, 128),
                    in_=xn_tmp[:, g * 128:(g + 1) * 128], ident=identb[:],
                    sem="sPE" if g == CG - 1 else None)
            v_xn_tr[tc] = adv("sPE", 1)
            rec("vector", _wait, sem="sPE", val=v_xn_tr[tc])
            rec("vector", _cp,
                out=xn_fm[:, :, tc * 128:(tc + 1) * 128],
                in_=bankbf(2, 0, 1024).rearrange("p (g t) -> p g t", g=CG), sem="sVE")
            v_xnfm[tc] = adv("sVE", 1)
            last["b2_free"] = v_xnfm[tc]

        for tc in range(NTC):
            rec("tensor", _wait, sem="sVE", val=v_xnfm[tc])
            for cg in range(CG):
                rec("tensor", _mm, out=bank(3, tc * E_EXP, E_EXP),
                    lhsT=xn_fm[:, cg, tc * 128:(tc + 1) * 128],
                    rhs=gate_sb[:, cg, :], start=(cg == 0), stop=(cg == CG - 1),
                    sem="sPE" if cg == CG - 1 else None)
        v_lg = adv("sPE", NTC)
        lg3 = bank(3, 0, NTC * E_EXP).rearrange("p (a e) -> p a e", a=NTC)
        rec("vector", _wait, sem="sPE", val=v_lg)
        rec("vector", _red, out=rt[:, 10:10 + NTC], in_=lg3, op=ALU.max, sem="sVE")
        vm1 = adv("sVE", 1)
        rec("vector", _wait, sem="sVE", val=vm1)
        m1b = rt[:, 10:10 + NTC].unsqueeze(2).broadcast_to([128, NTC, E_EXP])
        rec("vector", _tt, out=rt3a[:], in0=lg3, in1=m1b, op=ALU.is_equal, sem="sVE")
        veq1 = adv("sVE", 1)
        rec("vector", _wait, sem="sVE", val=veq1)
        rec("vector", _stt, out=rt3b[:], in0=rt3a[:], scalar=-1e30, in1=lg3,
            op0=ALU.mult, op1=ALU.add, sem="sVE")
        vl2 = adv("sVE", 1)
        rec("vector", _wait, sem="sVE", val=vl2)
        rec("vector", _red, out=rt[:, 14:14 + NTC], in_=rt3b[:], op=ALU.max, sem="sVE")
        vm2 = adv("sVE", 1)
        rec("vector", _wait, sem="sVE", val=vm2)
        rec("vector", _tt, out=rt[:, 18:18 + NTC], in0=rt[:, 14:14 + NTC],
            in1=rt[:, 10:10 + NTC], op=ALU.subtract, sem="sVE")
        vd = adv("sVE", 1)
        m2b = rt[:, 14:14 + NTC].unsqueeze(2).broadcast_to([128, NTC, E_EXP])
        rec("vector", _tt, out=rt3c[:], in0=rt3b[:], in1=m2b, op=ALU.is_equal, sem="sVE")
        veq2 = adv("sVE", 1)
        rec("scalar", _wait, sem="sVE", val=vd)
        rec("scalar", _act, out=rt[:, 10:10 + NTC], in_=rt[:, 18:18 + NTC], func=AF.Exp, sem="sSC")
        vex = adv("sSC", 1)
        rec("vector", _wait, sem="sSC", val=vex)
        rec("vector", _wait, sem="sVE", val=veq2)
        rec("vector", _ts, out=rt[:, 14:14 + NTC], in0=rt[:, 10:10 + NTC], s1=1.0,
            op0=ALU.add, sem="sVE")
        vp1 = adv("sVE", 1)
        rec("vector", _wait, sem="sVE", val=vp1)
        rec("vector", _recip, out=rt[:, 18:18 + NTC], in_=rt[:, 14:14 + NTC], sem="sVE")
        vw1 = adv("sVE", 1)
        rec("vector", _wait, sem="sVE", val=vw1)
        rec("vector", _tt, out=rt[:, 14:14 + NTC], in0=rt[:, 18:18 + NTC],
            in1=rt[:, 10:10 + NTC], op=ALU.mult, sem="sVE")
        vw2 = adv("sVE", 1)
        rec("vector", _wait, sem="sVE", val=max(vw2, veq2))
        w1b = rt[:, 18:18 + NTC].unsqueeze(2).broadcast_to([128, NTC, E_EXP])
        w2b = rt[:, 14:14 + NTC].unsqueeze(2).broadcast_to([128, NTC, E_EXP])
        rec("vector", _tt, out=mask01[:], in0=rt3a[:], in1=rt3c[:], op=ALU.add, sem="sVE")
        adv("sVE", 1)
        rec("vector", _tt, out=rt3b[:], in0=rt3a[:], in1=w1b, op=ALU.mult, sem="sVE")
        vt1 = adv("sVE", 1)
        rec("vector", _wait, sem="sVE", val=vt1)
        rec("vector", _tt, out=rt3c[:], in0=rt3c[:], in1=w2b, op=ALU.mult, sem="sVE")
        vt2 = adv("sVE", 1)
        rec("vector", _wait, sem="sVE", val=vt2)
        rec("vector", _tt, out=wgt01[:], in0=rt3b[:], in1=rt3c[:], op=ALU.add, sem="sVE")
        v_route = adv("sVE", 1)

        rec("sync", _wait, sem="sG2", val=v_kvscr)
        rec("sync", _wait, sem="sPE", val=max(last["e_pe_done"], last["kv_user"]))
        rec("sync", _wait, sem="sOW", val=C["sOW"])
        if cfg["debug"]:
            rec("sync", _wait, sem="sOUT", val=C["sOUT"])
        rec("sync", _dma, out=kvw_sb[:],
            in_=qw_scr.rearrange("(cg p) o -> p cg o", p=128), sem="sQW")
        v_qw = adv("sQW", 16)
        v_qh = [0] * H
        v_qe = [0] * H
        v_fm = [0] * H
        v_qp = [0] * H
        v_ae = [0] * H

        def b_qhat(h):
            kb = 0 if h % 2 == 0 else 4
            rec("tensor", _wait, sem="sVE", val=v_xnfm[-1])
            rec("tensor", _wait, sem="sQW", val=v_qw)
            if h >= 2:
                rec("tensor", _wait, sem="sSC", val=v_qe[h - 2])
            for cg in range(CG):
                rec("tensor", _mm, out=bank(kb), lhsT=kvw_sb[:, cg, h * 128:(h + 1) * 128],
                    rhs=xn_fm[:, cg, :], start=(cg == 0), stop=(cg == CG - 1),
                    sem="sPE" if cg == CG - 1 else None)
            v_qh[h] = adv("sPE", 1)

        def b_ke(h):
            kb = 0 if h % 2 == 0 else 4
            rec("scalar", _wait, sem="sPE", val=v_qh[h])
            if h == 0:
                rec("scalar", _wait, sem="sOUT", val=C["sOUT"])
            rec("scalar", _scp, out=Khat2[:, h % 2, :], in_=bank(kb), sem="sSC")
            v_qe[h] = adv("sSC", 1)

        def b_fm(h):
            rec("tensor", _wait, sem="sSC", val=v_qe[h])
            if h >= 1:
                rec("tensor", _wait, sem="sSC", val=v_ae[h - 1])
                rec("tensor", _wait, sem="sVE", val=v_qp[h - 1])
            rec("tensor", _mm, out=bank(1), lhsT=fm_sb[:, 0, :], rhs=Khat2[:, h % 2, :],
                start=True, stop=True)
            rec("tensor", _mm, out=bank(5), lhsT=fm_sb[:, 1, :], rhs=Khat2[:, h % 2, :],
                start=True, stop=True, sem="sPE")
            v_fm[h] = adv("sPE", 1)

        def b_fme(h):
            rec("scalar", _wait, sem="sPE", val=v_fm[h])
            if h >= 1:
                rec("scalar", _wait, sem="sVE", val=v_qp[h - 1])
            rec("scalar", _mul, out=A1s[:], in_=bank(1), mul=S_ATTN, sem="sSC")
            v_ae[h] = adv("sSC", 1)
            rec("vector", _wait, sem="sSC", val=v_ae[h])
            rec("vector", _tt, out=Qp_fm[:, h, :], in0=A1s[:], in1=bank(5), op=ALU.mult, sem="sVE")
            v_qp[h] = adv("sVE", 1)

        v_h1 = []
        e1 = dict(next=0, w0=0)
        E1BANKS = [2, 3, 6, 7]

        def emit_e1_group():
            og_abs = e1["next"]
            if og_abs >= F1 // 128:
                return
            e1["next"] += 1
            if og_abs % 4 == 0:
                os_ = og_abs // 4
                rec("sync", _wait, sem="sPE", val=max(last["w0_user"], last["e_pe_done"]))
                rec("sync", _dma, out=w01_sb[:],
                    in_=w0_scr[:, os_ * 512:(os_ + 1) * 512].rearrange(
                        "(cg p) o -> p cg o", p=128), sem="sW0")
                e1["w0"] = adv("sW0", 16)
            bk = E1BANKS[og_abs % 4]
            rec("tensor", _wait, sem="sW0", val=e1["w0"])
            if og_abs == 0:
                rec("tensor", _wait, sem="sVE", val=v_route)
            if len(v_h1) >= 4:
                rec("tensor", _wait, sem="sSC", val=v_h1[-4])
            for cg in range(CG):
                rec("tensor", _mm, out=bank(bk),
                    lhsT=w01_sb[:, cg, (og_abs % 4) * 128:(og_abs % 4) * 128 + 128],
                    rhs=xn_fm[:, cg, :], start=(cg == 0), stop=(cg == CG - 1),
                    sem="sPE" if cg == CG - 1 else None)
            vm = adv("sPE", 1)
            last["w0_user"] = vm
            rec("scalar", _wait, sem="sPE", val=vm)
            rec("scalar", _act, out=H1[:, og_abs, :], in_=bank(bk), func=AF.Relu,
                scale=scol[:, 0:1], sem="sSC")
            v_h1.append(adv("sSC", 1))

        b_qhat(0); b_ke(0); b_qhat(1); b_ke(1)
        for h in range(H):
            b_fm(h)
            if h + 2 < H:
                b_qhat(h + 2)
            b_fme(h)
            if h + 2 < H:
                b_ke(h + 2)
            for _ in range(4):
                emit_e1_group()
        while e1["next"] < F1 // 128:
            emit_e1_group()
        v_h1_done = v_h1[-1]
        v_qp_done = v_qp[-1]
        last["kv_user"] = max(last["kv_user"], v_fm[-1])

        fil = dict(ring=0, cons=[0, 0], vcons=[0, 0], next_h2=0, v_h2_last=0,
                   w1_user=last["w0_user"], slot=0, nslots=nexp * 10 + NTC)

        def maybe_fill():
            fil["slot"] += 1
            target = (F1 // 128) * fil["slot"] // fil["nslots"]
            while fil["next_h2"] < target:
                emit_h2_slice()

        def emit_h2_slice():
            o2g_abs = fil["next_h2"]
            if o2g_abs >= F1 // 128:
                return
            fil["next_h2"] += 1
            off = 3584
            rec("sync", _wait, sem="sPE", val=fil["w1_user"])
            rec("sync", _dma, out=w1v_sb[:],
                in_=w1_scr[:, :, o2g_abs * 128:(o2g_abs + 1) * 128].rearrange("g p o -> p g o"),
                sem="sW1A")
            v_w1 = adv("sW1A", 16)
            rec("tensor", _wait, sem="sW1A", val=v_w1)
            rec("tensor", _wait, sem="sSC", val=max(v_h1_done, fil["cons"][0]))
            rec("tensor", _wait, sem="sVE", val=fil["vcons"][0])
            for o1g in range(F1 // 128):
                rec("tensor", _mm, out=pflat(off, 512),
                    lhsT=w1v_sb[:, o1g, :],
                    rhs=H1[:, o1g, :], start=(o1g == 0), stop=(o1g == F1 // 128 - 1),
                    sem="sPE" if o1g == F1 // 128 - 1 else None)
            vm = adv("sPE", 1)
            fil["w1_user"] = vm
            rec("scalar", _wait, sem="sPE", val=vm)
            rec("scalar", _act, out=H2[:, o2g_abs, :], in_=pflat(off, 512), func=AF.Relu,
                scale=scol[:, 1:2], sem="sSC")
            fil["cons"][0] = adv("sSC", 1)
            fil["v_h2_last"] = fil["cons"][0]

        for ei in range(nexp):
            rec("sync", _wait, sem="sG2", val=v_kvscr)
            rec("sync", _wait, sem="sPE", val=max(last["kv_user"], last["e_pe_done"]))
            rec("sync", _dma, out=kvw_sb[:],
                in_=kv_scr[ei, 0].rearrange("(cg p) o -> p cg o", p=128), sem="sKV")
            v_kv = adv("sKV", 16)
            kw_v = kvw_sb
            vw_v = kvw_sb


            v_kh = [0] * H
            v_ke = [0] * H
            v_fmk = [0] * H
            v_kph = [0] * H
            v_aek = [0] * H
            rec("tensor", _wait, sem="sVE", val=max(v_qp_done, last["c5_ve_done"]))
            rec("tensor", _wait, sem="sSC", val=v_h1_done)

            def c_khat(h):
                kb = [0, 4, 6][h % 3]
                rec("tensor", _wait, sem="sKV", val=v_kv)
                if h >= 3:
                    rec("tensor", _wait, sem="sSC", val=v_ke[h - 3])
                for cg in range(CG):
                    rec("tensor", _mm, out=bank(kb), lhsT=kw_v[:, cg, h * 128:(h + 1) * 128],
                        rhs=xn_fm[:, cg, :], start=(cg == 0), stop=(cg == CG - 1),
                        sem="sPE" if cg == CG - 1 else None)
                v_kh[h] = adv("sPE", 1)

            def c_ke(h):
                kb = [0, 4, 6][h % 3]
                rec("scalar", _wait, sem="sPE", val=v_kh[h])
                rec("scalar", _scp, out=Khat2[:, h % 3, :], in_=bank(kb), sem="sSC")
                v_ke[h] = adv("sSC", 1)

            def c_fm(h):
                rec("tensor", _wait, sem="sSC", val=v_ke[h])
                if h >= 1:
                    rec("tensor", _wait, sem="sSC", val=v_aek[h - 1])
                    rec("tensor", _wait, sem="sVE", val=v_kph[h - 1])
                rec("tensor", _mm, out=bank(1), lhsT=fm_sb[:, 2, :], rhs=Khat2[:, h % 3, :],
                    start=True, stop=True)
                rec("tensor", _mm, out=bank(5), lhsT=fm_sb[:, 3, :], rhs=Khat2[:, h % 3, :],
                    start=True, stop=True)
                for tc in range(NTC):
                    rec("tensor", _mm, out=bank(2, tc * 128, 128),
                        lhsT=Khat2[:, h % 3, tc * 128:(tc + 1) * 128], rhs=fm_sb[:, 2, :],
                        start=True, stop=True)
                for tc in range(NTC):
                    rec("tensor", _mm, out=bank(3, tc * 128, 128),
                        lhsT=Khat2[:, h % 3, tc * 128:(tc + 1) * 128], rhs=fm_sb[:, 3, :],
                        start=True, stop=True, sem="sPE" if tc == NTC - 1 else None)
                v_fmk[h] = adv("sPE", 1)

            def c_fme(h):
                rec("scalar", _wait, sem="sPE", val=v_fmk[h])
                if h >= 1:
                    rec("scalar", _wait, sem="sVE", val=v_kph[h - 1])
                rec("scalar", _scp, out=A1s[:], in_=bank(1), sem="sSC")
                rec("scalar", _scp, out=A1s2[:], in_=bank(2), sem="sSC")
                adv("sSC", 1)
                v_aek[h] = adv("sSC", 1)
                rec("vector", _wait, sem="sSC", val=v_aek[h])
                rec("vector", _tt, out=Kp_fm[:, h, :], in0=A1s[:], in1=bank(5), op=ALU.mult,
                    sem="sVE")
                rec("vector", _tt, out=Kp_tm[:, :, h, :],
                    in0=A1s2[:].rearrange("p (a d) -> p a d", a=NTC),
                    in1=bank(3).rearrange("p (a d) -> p a d", a=NTC), op=ALU.mult, sem="sVE")
                adv("sVE", 1)
                v_kph[h] = adv("sVE", 1)

            c_khat(0); c_ke(0); c_khat(1); c_ke(1); c_khat(2); c_ke(2)
            for h in range(H):
                c_fm(h)
                if h + 3 < H:
                    c_khat(h + 3)
                c_fme(h)
                if h + 3 < H:
                    c_ke(h + 3)
                if h % 2 == 1:
                    maybe_fill()

            rec("sync", _wait, sem="sPE", val=v_kh[H - 1])
            rec("sync", _dma, out=kvw_sb[:],
                in_=kv_scr[ei, 1].rearrange("(cg p) o -> p cg o", p=128), sem="sKV")
            v_kv2 = adv("sKV", 16)

            rec("vector", _wait, sem="sVE", val=v_route)
            v_vp = []
            v_vp_pe = 0
            for tc in range(NTC):
                for vs in range(2):
                    bk = [0, 1, 5][(tc * 2 + vs) % 3]
                    if tc == 0 and vs == 0:
                        rec("tensor", _wait, sem="sKV", val=v_kv2)
                    if len(v_vp) < 3:
                        rec("tensor", _wait, sem="sVE", val=v_kph[-1])
                    if len(v_vp) >= 3:
                        rec("tensor", _wait, sem="sVE", val=v_vp[-3])
                    for cg in range(CG):
                        rec("tensor", _mm, out=bank(bk),
                            lhsT=xn_fm[:, cg, tc * 128:(tc + 1) * 128],
                            rhs=vw_v[:, cg, vs * 512:(vs + 1) * 512],
                            start=(cg == 0), stop=(cg == CG - 1),
                            sem="sPE" if cg == CG - 1 else None)
                    vvh = adv("sPE", 1)
                    v_vp_pe = vvh
                    rec("vector", _wait, sem="sPE", val=vvh)
                    rec("vector", _ts, out=Vp_tm[:, tc, vs * 4:(vs + 1) * 4, :],
                        in0=bank(bk).rearrange("p (a d) -> p a d", a=4),
                        s1=mask01[:, tc, ei:ei + 1], op0=ALU.mult, sem="sVE")
                    v_vp.append(adv("sVE", 1))
                if tc % 2 == 1:
                    maybe_fill()
            v_vp_done = v_vp[-1]
            last["kv_user"] = v_vp_pe

            has_state = not first_seg
            rec("tensor", _wait, sem="sVE", val=v_vp_done)
            rec("tensor", _wait, sem="sSC", val=C["sSC"])
            v_sbfcopy = C["sSC"]
            if not first_seg:
                rec("sync", _wait, sem="sSV", val=C["sSV"])
                rec("sync", _wait, sem="sSC", val=C["sSC"])
                rec("sync", _wait, sem="sVE", val=C["sVE"])
                rec("sync", _dma, out=S_mast[:], in_=st_scr[ei], sem="sST")
                v_strest = adv("sST", 16)
                rec("scalar", _wait, sem="sST", val=v_strest)
                rec("scalar", _wait, sem="sPE", val=v_vp_pe)
                rec("scalar", _scp, out=S_bf[:].rearrange("p h v -> p (h v)"),
                    in_=S_mast[:].rearrange("p h v -> p (h v)"), sem="sSC")
                v_sbfcopy = adv("sSC", 1)

            p_even = [0, 512, 1024, 896]
            p_odd = [1536, 2048, 1280, 2432]
            a_off = [0, 512, 896, 1152]
            slot_n = [512, 384, 256, 128]
            v_aT = [0] * H
            v_omm = [0] * H
            v_dtt = [0] * H
            v_oc = [0] * H
            v_oev = [0] * H
            v_dadd = [0] * H

            def c5_delta(j):
                off = 3584
                rec("tensor", _wait, sem="sSC", val=fil["cons"][0])
                rec("tensor", _wait, sem="sVE", val=fil["vcons"][0])
                for jc in range(NTC):
                    rec("tensor", _mm, out=pflat(off, 128),
                        lhsT=Kp_tm[:, jc, j, :], rhs=Vp_tm[:, jc, j, :],
                        start=(jc == 0), stop=(jc == NTC - 1),
                        sem="sPE" if jc == NTC - 1 else None)
                vm = adv("sPE", 1)
                rec("vector", _wait, sem="sPE", val=vm)
                if j == 0:
                    rec("vector", _wait, sem="sSC", val=v_sbfcopy)
                    if first_seg:
                        rec("vector", _wait, sem="sSV", val=C["sSV"])
                if first_seg:
                    rec("vector", _cp, out=S_mast[:, j, :], in_=pflat(off, 128), sem="sVE")
                else:
                    rec("vector", _stt, out=S_mast[:, j, :], in0=pflat(off, 128),
                        scalar=1.0, in1=S_mast[:, j, :], op0=ALU.mult, op1=ALU.add, sem="sVE")
                fil["vcons"][0] = adv("sVE", 1)
                v_dadd[j] = fil["vcons"][0]

            def c5_av(j):
                par = j % 2
                o_off = 2560 if par == 0 else 3072
                rec("tensor", _wait, sem="sVE", val=v_dtt[j])
                rec("tensor", _wait, sem="sSC", val=v_oc[j])
                if j >= 2:
                    rec("tensor", _wait, sem="sVE", val=v_oev[j - 2])
                if has_state and j == 0:
                    rec("tensor", _wait, sem="sSC", val=v_sbfcopy)
                for il in range(NTC):
                    nmm = il + 1 + (1 if has_state else 0)
                    k = 0
                    for jc in range(il + 1):
                        k += 1
                        rec("tensor", _mm, out=pflat(o_off + il * 128, 128),
                            lhsT=aT_big[:, par, a_off[jc] + (il - jc) * 128:
                                        a_off[jc] + (il - jc) * 128 + 128],
                            rhs=Vp_tm[:, jc, j, :], start=(k == 1), stop=(k == nmm),
                            sem="sPE" if (il == NTC - 1 and k == nmm) else None)
                    if has_state:
                        k += 1
                        rec("tensor", _mm, out=pflat(o_off + il * 128, 128),
                            lhsT=Qp_fm[:, j, il * 128:(il + 1) * 128], rhs=S_bf[:, j, :],
                            start=False, stop=True,
                            sem="sPE" if (il == NTC - 1) else None)
                v_omm[j] = adv("sPE", 1)
                rec("vector", _wait, sem="sPE", val=v_omm[j])
                for il in range(NTC):
                    wcol = wgt01[:, il, ei:ei + 1]
                    if ei == 0:
                        rec("vector", _ts, out=O_acc[:, il, j * 128:(j + 1) * 128],
                            in0=pflat(o_off + il * 128, 128), s1=wcol, op0=ALU.mult, sem="sVE")
                    else:
                        rec("vector", _stt, out=O_acc[:, il, j * 128:(j + 1) * 128],
                            in0=pflat(o_off + il * 128, 128), scalar=wcol,
                            in1=O_acc[:, il, j * 128:(j + 1) * 128],
                            op0=ALU.mult, op1=ALU.add, sem="sVE")
                adv("sVE", 3)
                v_oev[j] = adv("sVE", 1)

            for h in range(H):
                par = h % 2
                poff = p_even if par == 0 else p_odd
                if h >= 2:
                    rec("tensor", _wait, sem="sVE", val=v_dtt[h - 2])
                    rec("tensor", _wait, sem="sSC", val=v_oc[h - 2])
                for jc in (0, 1, 3):
                    rec("tensor", _mm, out=pflat(poff[jc], slot_n[jc]),
                        lhsT=Kp_fm[:, h, jc * 128:(jc + 1) * 128],
                        rhs=Qp_fm[:, h, jc * 128:512], start=True, stop=True)
                c5_delta(h)
                if h >= 1:
                    rec("tensor", _wait, sem="sVE", val=v_dtt[h - 1])
                    rec("tensor", _wait, sem="sSC", val=v_oc[h - 1])
                rec("tensor", _mm, out=pflat(poff[2], slot_n[2]),
                    lhsT=Kp_fm[:, h, 256:384], rhs=Qp_fm[:, h, 256:512],
                    start=True, stop=True, sem="sPE")
                v_aT[h] = adv("sPE", 1)
                rec("vector", _wait, sem="sPE", val=v_aT[h])
                for jc in range(NTC):
                    rec("vector", _tt, out=aT_big[:, par, a_off[jc]: a_off[jc] + 128],
                        in0=pflat(poff[jc], 128), in1=tri_s[:], op=ALU.mult,
                        sem="sVE")
                adv("sVE", 3)
                v_dtt[h] = adv("sVE", 1)
                rec("scalar", _wait, sem="sPE", val=v_aT[h])
                rec("scalar", _wait, sem="sVE", val=v_dtt[h])
                for jc, w in ((0, 384), (1, 256), (2, 128)):
                    rec("scalar", _scp,
                        out=aT_big[:, par, a_off[jc] + 128: a_off[jc] + 128 + w],
                        in_=pflat(poff[jc] + 128, w), sem="sSC")
                adv("sSC", 2)
                v_oc[h] = adv("sSC", 1)

                if h >= 1:
                    c5_av(h - 1)
                if h % 2 == 1:
                    maybe_fill()
            c5_av(H - 1)

            if not last_seg:
                rec("scalar", _wait, sem="sVE", val=v_dadd[H - 1])
                rec("scalar", _dma, out=st_scr[ei], in_=S_mast[:], sem="sSV")
                last["stsave"][ei] = adv("sSV", 16)
            last["c5_ve_done"] = C["sVE"]

        c_pe_done = C["sPE"]

        v_onrm_tr = [0] * NTC
        rec("vector", _wait, sem="sVE", val=v_oev[H - 1])
        vsq_d = 0
        for tc in range(NTC):
            o3 = O_acc[:, tc, :].rearrange("p (h v) -> p h v", h=H)
            if vsq_d:
                rec("vector", _wait, sem="sVE", val=C["sVE"])
            rec("vector", _tt, out=tmp_sq[:].rearrange("p (h v) -> p h v", h=H),
                in0=o3, in1=o3, op=ALU.mult, sem="sVE")
            vsq_d = adv("sVE", 1)
            rec("vector", _wait, sem="sVE", val=vsq_d)
            rec("vector", _red, out=rms_s[:, tc, :],
                in_=tmp_sq[:].rearrange("p (h v) -> p h v", h=H), op=ALU.add, sem="sVE")
            adv("sVE", 1)
        vred_d = C["sVE"]
        rec("scalar", _wait, sem="sVE", val=vred_d)
        rec("scalar", _act, out=rms_s[:].rearrange("p a h -> p (a h)"),
            in_=rms_s[:].rearrange("p a h -> p (a h)"), func=AF.Ln, scale=1.0 / DK,
            bias=1e-5, sem="sSC")
        vln_d = adv("sSC", 1)
        rec("scalar", _wait, sem="sSC", val=vln_d)
        rec("scalar", _act, out=rms_r[:].rearrange("p a h -> p (a h)"),
            in_=rms_s[:].rearrange("p a h -> p (a h)"), func=AF.Exp, scale=-0.5, sem="sSC")
        vrs_d = adv("sSC", 1)
        for tc in range(NTC):
            maybe_fill()
            o3 = O_acc[:, tc, :].rearrange("p (h v) -> p h v", h=H)
            rec("vector", _wait, sem="sSC", val=vrs_d)
            if tc > 0:
                rec("vector", _wait, sem="sPE", val=v_onrm_tr[tc - 1])
            rb = rms_r[:, tc, :].unsqueeze(2).broadcast_to([128, H, DK])
            rec("vector", _tt, out=Onrm[:].rearrange("p (h v) -> p h v", h=H),
                in0=o3, in1=rb, op=ALU.mult, sem="sVE")
            vON = adv("sVE", 1)
            rec("tensor", _wait, sem="sVE", val=vON)
            rec("tensor", _wait, sem="sVE", val=last["b2_free"])
            for g in range(CG):
                rec("tensor", _tr, out=bankbf(2, g * 128, 128),
                    in_=Onrm[:, g * 128:(g + 1) * 128], ident=identb[:],
                    sem="sPE" if g == CG - 1 else None)
            v_onrm_tr[tc] = adv("sPE", 1)
            rec("vector", _wait, sem="sPE", val=v_onrm_tr[tc])
            rec("vector", _cp, out=Onrm_fm[:, :, tc * 128:(tc + 1) * 128],
                in_=bankbf(2, 0, 1024).rearrange("p (g t) -> p g t", g=CG), sem="sVE")
            vev = adv("sVE", 1)
            last["b2_free"] = vev
        v_onrm_done = last["b2_free"]

        while fil["next_h2"] < F1 // 128:
            emit_h2_slice()
        v_h2_done = fil["v_h2_last"]

        rec("sync", _wait, sem="sPE", val=c_pe_done)
        rec("sync", _wait, sem="sOW", val=C["sOW"])
        rec("sync", _dma, out=owT_sb[:],
            in_=ow_scr.rearrange("(cg p) o -> p cg o", p=128), sem="sOW")
        v_ow = adv("sOW", 16)
        rec("tensor", _wait, sem="sSC", val=v_h2_done)
        rec("tensor", _wait, sem="sVE", val=v_onrm_done)
        rec("tensor", _wait, sem="sOW", val=v_ow)
        for tc in range(NTC):
            for g in range(CG):
                for ds in range(2):
                    rec("tensor", _mm, out=pflat(tc * 1024 + ds * 512, 512),
                        lhsT=Onrm_fm[:, g, tc * 128:(tc + 1) * 128],
                        rhs=owT_sb[:, g, ds * 512:(ds + 1) * 512],
                        start=(g == 0), stop=False,
                        sem="sPE" if (tc == NTC - 1 and g == CG - 1 and ds == 1) else None)
        v_attn_mm = adv("sPE", 1)
        for o2s in range(8):
            par = o2s % 2
            w2sem = "sW2" if par == 0 else "sW2B"
            rec("sync", _wait, sem="sPE", val=max(last["w2_user"][par], c_pe_done))
            rec("sync", _dma, out=w2_sb[:, par],
                in_=w2_scr[o2s * 512:(o2s + 1) * 512, :].rearrange("(s p) d -> p s d", p=128),
                sem=w2sem)
            v_w2 = adv(w2sem, 16)
            rec("tensor", _wait, sem=w2sem, val=v_w2)
            lastslice = (o2s == 7)
            for tc in range(NTC):
                for sub in range(4):
                    o2g = o2s * 4 + sub
                    for ds in range(2):
                        is_last = lastslice and sub == 3
                        rec("tensor", _mm,
                            out=pflat(tc * 1024 + ds * 512, 512),
                            lhsT=H2[:, o2g, tc * 128:(tc + 1) * 128],
                            rhs=w2_sb[:, par, sub, ds * 512:(ds + 1) * 512],
                            start=False, stop=is_last,
                            sem="sPE" if (tc == NTC - 1 and sub == 3 and ds == 1) else None)
            last["w2_user"][par] = adv("sPE", 1)
        v_e3 = C["sPE"]
        last["w2_user"] = [v_e3, v_e3]
        for tc in range(NTC):
            par = tc % 2
            rec("vector", _wait, sem="sPE", val=v_e3)
            if tc >= 2:
                rec("vector", _wait, sem="sOUT", val=C["sOUT"])
            rec("vector", _tt, out=out_sb[:, par, :], in0=pflat(tc * 1024, 1024),
                in1=x_seg[:, tc, :], op=ALU.add, sem="sVE")
            vev = adv("sVE", 1)
            rec("scalar", _wait, sem="sVE", val=vev)
            rec("scalar", _dma, out=out_d[b, t0 + tc * 128: t0 + (tc + 1) * 128, :],
                in_=out_sb[:, par, :], sem="sOUT")
            last["out_user"][par] = adv("sOUT", 16)
        last["e3_evict"] = C["sVE"]
        last["b2_free"] = C["sVE"]
        last["e_pe_done"] = v_e3

        if cfg["debug"] and seg == 0:
            rec("scalar", _wait, sem="sVE", val=C["sVE"])
            for name, srcap in [("xn_fm", xn_fm[:]), ("wgt01", wgt01[:]), ("mask01", mask01[:]),
                              ("Qp", Qp_fm[:]), ("Oacc", O_acc[:]), ("H1", H1[:]),
                              ("H2", H2[:]), ("Onrmfm", Onrm_fm[:])]:
                rec("scalar", _dma, out=dbg[name], in_=srcap, sem="sOUT")
                adv("sOUT", 16)

    rec("scalar", _wait, sem="sOUT", val=C["sOUT"])
    rec("scalar", _wait, sem="sSV", val=C["sSV"])

    chk = {k: 0 for k in SEMS}
    for _en, _fn, _kw in REC:
        s = _kw.get("sem")
        if s and _fn is not _wait:
            chk[s] += 16 if _fn is _dma else 1
    chk["sGP"] += 15
    assert chk == C, f"sem count mismatch: {chk} vs {C}"

    with es:
        block = nc.Block()
        with block:
            for eng_name in ["sync", "gpsimd", "tensor", "vector", "scalar"]:
                def body(e, _n=eng_name):
                    for en, fn, kw in REC:
                        if en == _n:
                            fn(e, **kw)
                getattr(block, eng_name)(body)
    return nc


def kernel(**inputs):
    cfg = dict(DEFAULT_CFG)
    x = np.ascontiguousarray(inputs["L0_attn_x"])
    n_cores = 8
    shared = make_shared_inputs(inputs)
    nc = build_program(cfg)
    in_maps = []
    for c in range(n_cores):
        m = dict(shared)
        m["x"] = np.ascontiguousarray(x[2 * c: 2 * c + 2])
        in_maps.append(m)
    res = run_bass_kernel_spmd(nc, in_maps, core_ids=list(range(n_cores)))
    return np.concatenate([r["out"] for r in res.results], axis=0)


def make_shared_inputs(inputs):
    return dict(
        gate_wT=np.ascontiguousarray(inputs["L0_attn_gate_w"].T),
        q_wT=np.ascontiguousarray(inputs["L0_attn_q_w"].T),
        k_wT=np.ascontiguousarray(np.transpose(inputs["L0_attn_k_w"], (0, 2, 1))),
        v_wT=np.ascontiguousarray(np.transpose(inputs["L0_attn_v_w"], (0, 2, 1))),
        fmq1T=np.ascontiguousarray(inputs["L0_attn_fmq1_w"].T),
        fmq2T=np.ascontiguousarray(inputs["L0_attn_fmq2_w"].T),
        fmk1T=np.ascontiguousarray(inputs["L0_attn_fmk1_w"].T),
        fmk2T=np.ascontiguousarray(inputs["L0_attn_fmk2_w"].T),
        norm_w=np.ascontiguousarray(np.asarray(inputs["L0_attn_norm_w"]).reshape(DK, 1)),
        o_wT=np.ascontiguousarray(inputs["L0_attn_o_w"].T),
        k0=np.ascontiguousarray(inputs["L0_ffn_k0"]).view(np.uint8),
        k1=np.ascontiguousarray(inputs["L0_ffn_k1"]).view(np.uint8),
        k2=np.ascontiguousarray(inputs["L0_ffn_k2"]).view(np.uint8),
        s0=np.asarray(inputs["L0_ffn_s0"], dtype=np.float32).reshape(1, 1),
        s1=np.asarray(inputs["L0_ffn_s1"], dtype=np.float32).reshape(1, 1),
        s2=np.asarray(inputs["L0_ffn_s2"], dtype=np.float32).reshape(1, 1),
    )


# version 13 (final)
# speedup vs baseline: 1.1712x; 1.0380x over previous
import numpy as np
import ml_dtypes
from contextlib import ExitStack
import concourse.bass as bass
import concourse.mybir as mybir
from concourse.bass_utils import run_bass_kernel_spmd

F32 = mybir.dt.float32
BF16 = mybir.dt.bfloat16
U8 = mybir.dt.uint8
AF = mybir.ActivationFunctionType
ALU = mybir.AluOpType
AX = mybir.AxisListType

D = 1024
E_EXP = 8
H = 8
DK = 128
SEG = 512
NTC = SEG // 128
CG = D // 128
S_ATTN = float(DK) ** -0.5
F1 = D * 4

DEFAULT_CFG = dict(nbatch=2, segs_per_batch=8, nexp=8, debug=False)


def build_program(cfg):
    nb = cfg["nbatch"]
    spb = cfg["segs_per_batch"]
    nexp = cfg["nexp"]
    seqlen = spb * SEG
    nseg = nb * spb

    nc = bass.Bass("TRN2", target_bir_lowering=False)

    x_d = nc.dram_tensor("x", [nb, seqlen, D], F32, kind="ExternalInput").ap()
    gate_d = nc.dram_tensor("gate_wT", [D, E_EXP], F32, kind="ExternalInput").ap()
    qwT_d = nc.dram_tensor("q_wT", [D, D], F32, kind="ExternalInput").ap()
    kwT_d = nc.dram_tensor("k_wT", [E_EXP, D, D], F32, kind="ExternalInput").ap()
    vwT_d = nc.dram_tensor("v_wT", [E_EXP, D, D], F32, kind="ExternalInput").ap()
    fm_d = [nc.dram_tensor(n, [DK, DK], F32, kind="ExternalInput").ap()
            for n in ["fmq1T", "fmq2T", "fmk1T", "fmk2T"]]
    nw_d = nc.dram_tensor("norm_w", [DK, 1], F32, kind="ExternalInput").ap()
    owT_d = nc.dram_tensor("o_wT", [D, D], F32, kind="ExternalInput").ap()
    k0_d = nc.dram_tensor("k0", [2 * D, F1], U8, kind="ExternalInput").ap()
    k1_d = nc.dram_tensor("k1", [F1, F1], U8, kind="ExternalInput").ap()
    k2_d = nc.dram_tensor("k2", [F1, D], U8, kind="ExternalInput").ap()
    s0_d = nc.dram_tensor("s0", [1, 1], F32, kind="ExternalInput").ap()
    s1_d = nc.dram_tensor("s1", [1, 1], F32, kind="ExternalInput").ap()
    s2_d = nc.dram_tensor("s2", [1, 1], F32, kind="ExternalInput").ap()
    out_d = nc.dram_tensor("out", [nb, seqlen, D], F32, kind="ExternalOutput").ap()

    kv_scr = nc.dram_tensor("kv_scr", [E_EXP, 2, D, D], BF16, kind="Internal").ap()
    w0_scr = nc.dram_tensor("w0_scr", [D, F1], BF16, kind="Internal").ap()
    w1_scr = nc.dram_tensor("w1_scr", [F1 // 128, 128, F1], BF16, kind="Internal").ap()
    w2_scr = nc.dram_tensor("w2_scr", [F1, D], BF16, kind="Internal").ap()
    st_scr = nc.dram_tensor("st_scr", [E_EXP, 128, H, DK], F32, kind="Internal").ap()
    qw_scr = nc.dram_tensor("qw_scr", [D, D], BF16, kind="Internal").ap()
    ow_scr = nc.dram_tensor("ow_scr", [D, D], BF16, kind="Internal").ap()

    dbg = {}
    if cfg["debug"]:
        for n, shape, dt in [
            ("xn_fm", [128, CG, SEG], BF16), ("wgt01", [128, NTC, E_EXP], F32),
            ("mask01", [128, NTC, E_EXP], F32), ("Qp", [128, H, SEG], BF16),
            ("Kp", [128, H, SEG], BF16), ("Kptm", [128, NTC, H, DK], BF16),
            ("Vp", [128, NTC, H, DK], BF16),
            ("Oacc", [128, NTC, D], F32), ("H1", [128, F1 // 128, SEG], BF16),
            ("H2", [128, F1 // 128, SEG], BF16), ("Onrmfm", [128, CG, SEG], BF16),
        ]:
            dbg[n] = nc.dram_tensor("dbg_" + n, shape, dt, kind="ExternalOutput").ap()

    es = ExitStack()
    SEMS = {}
    for s in ["sX", "sKV", "sQW", "sOW", "sST", "sW0", "sW1A", "sW1B", "sW2", "sW2B",
              "sP", "sOUT", "sSV", "sPS", "sG1", "sG2", "sPE", "sVE", "sSC", "sGP"]:
        SEMS[s] = es.enter_context(nc.semaphore(s))
    C = {k: 0 for k in SEMS}

    def adv(sem, k):
        C[sem] += k
        return C[sem]

    def sb(name, shape, dt):
        return es.enter_context(nc.sbuf_tensor(name, shape, dt))

    gate_sb = sb("gate_sb", [128, CG, E_EXP], BF16)
    fm_sb = sb("fm_sb", [128, 4, DK], BF16)
    identb = sb("identb", [128, 128], BF16)
    identf = sb("identf", [128, 128], F32)
    tri_s = sb("tri_s", [128, 128], F32)
    tri_b = sb("tri_b", [128, 128], BF16)
    onesr = sb("onesr", [1, 128], F32)
    scol = sb("scol", [128, 4], F32)
    nw_sb = sb("nw_sb", [128, 1], F32)
    ssc_sb = sb("ssc_sb", [1, 4], F32)
    S_mast = sb("S_mast", [128, H, DK], F32)
    S_bf = sb("S_bf", [128, H, DK], BF16)

    x_seg = sb("x_seg", [128, NTC, D], F32)
    xn_fm = sb("xn_fm", [128, CG, SEG], BF16)
    Qp_fm = sb("Qp_fm", [128, H, SEG], BF16)
    wgt01 = sb("wgt01", [128, NTC, E_EXP], F32)
    mask01 = sb("mask01", [128, NTC, E_EXP], F32)
    O_acc = sb("O_acc", [128, NTC, D], F32)
    Onrm_fm = sb("Onrm_fm", [128, CG, SEG], BF16)
    xn_tmp = sb("xn_tmp", [128, 2, D], BF16)
    Onrm = xn_tmp
    tmp_sq = sb("tmp_sq", [128, D], F32)
    rt = sb("rt", [128, 24], F32)
    rt3a = sb("rt3a", [128, NTC, E_EXP], F32)
    rt3b = sb("rt3b", [128, NTC, E_EXP], F32)
    rt3c = sb("rt3c", [128, NTC, E_EXP], F32)
    rms_s = sb("rms_s", [128, NTC, H], F32)
    rms_r = sb("rms_r", [128, NTC, H], F32)
    cst = sb("cst", [128, 6], F32)
    _cvals = [1e-5, -0.5, 1.0 / DK, S_ATTN, 1.0, 0.0]
    for _ci, _cv in enumerate(_cvals):
        nc.const_aps.register(F32, _cv, cst[:, _ci:_ci + 1])

    ARENA_BYTES = 124928
    arena = sb("arena", [128, ARENA_BYTES // 2], BF16)

    def carve(off_bytes, shape, dt):
        nelem = 1
        for s in shape[1:]:
            nelem *= s
        esz = {BF16: 2, F32: 4, U8: 1}[dt]
        base = arena[:].bitcast(dt) if dt != BF16 else arena[:]
        v = base[:, off_bytes // esz: off_bytes // esz + nelem]
        if len(shape) > 2:
            dims = "".join(chr(ord('a') + i) for i in range(len(shape) - 1))
            v = v.rearrange(f"p ({' '.join(dims)}) -> p {' '.join(dims)}",
                            **{chr(ord('a') + i): shape[1 + i] for i in range(len(shape) - 1)})
        return v

    kvw_sb = carve(0, [128, CG, D], BF16)
    Khat2 = carve(16384, [128, 3, SEG], BF16)
    Kp_fm = carve(19456, [128, H, SEG], BF16)
    Kp_tm = carve(27648, [128, NTC, H, DK], BF16)
    Vp_tm = carve(35840, [128, NTC, H, DK], BF16)
    aT_big = carve(44032, [128, 2, 1280], BF16)
    A1s = carve(49152, [128, SEG], BF16)
    A1s2 = carve(50176, [128, SEG], BF16)
    H1 = carve(51200, [128, F1 // 128, SEG], BF16)
    H2 = carve(83968, [128, F1 // 128, SEG], BF16)
    w01_sb = carve(116736, [128, CG, 512], BF16)
    w1v_sb = carve(116736, [128, F1 // 128, 128], BF16)
    owT_sb = carve(0, [128, CG, D], BF16)
    out_sb = carve(16384, [128, 2, D], F32)
    w2_sb = carve(27648, [128, 2, 4, D], BF16)
    pstg_a = carve(0, [128, F1], U8)
    pstg_b = carve(4096, [128, F1], U8)
    pstg_f = carve(8192, [128, 512], F32)
    pstg_o = carve(10240, [128, F1], BF16)

    pALL = es.enter_context(nc.psum_tensor("pALL", [128, 4096], F32))

    def bank(i, lo=0, n=512):
        return pALL[:, i * 512 + lo: i * 512 + lo + n]

    def pflat(lo, n):
        return pALL[:, lo: lo + n]

    def bankbf(i, lo=0, n=1024):
        return pALL[:].bitcast(BF16)[:, i * 1024 + lo: i * 1024 + lo + n]

    def pbf(off, n):
        return pALL[:].bitcast(BF16)[:, off: off + n]

    REC = []

    def rec(eng, fn, **kw):
        REC.append((eng, fn, kw))

    def _dma(e, out=None, in_=None, sem=None):
        e.dma_start(out=out, in_=in_).then_inc(SEMS[sem], 16)

    def _wait(e, sem=None, val=0):
        e.wait_ge(SEMS[sem], val)

    def _mm(e, out=None, lhsT=None, rhs=None, start=True, stop=True, sem=None):
        m = e.matmul(out, lhsT, rhs, start=start, stop=stop)
        if sem:
            m.then_inc(SEMS[sem], 1)

    def _tr(e, out=None, in_=None, ident=None, sem=None):
        m = e.transpose(out, in_, ident)
        if sem:
            m.then_inc(SEMS[sem], 1)

    def _cp(e, out=None, in_=None, sem=None):
        m = e.tensor_copy(out=out, in_=in_)
        if sem:
            m.then_inc(SEMS[sem], 1)

    def _scp(e, out=None, in_=None, sem=None):
        m = e.copy(out=out, in_=in_)
        if sem:
            m.then_inc(SEMS[sem], 1)

    def _tt(e, out=None, in0=None, in1=None, op=None, sem=None):
        m = e.tensor_tensor(out=out, in0=in0, in1=in1, op=op)
        if sem:
            m.then_inc(SEMS[sem], 1)

    def _ts(e, out=None, in0=None, s1=None, s2=None, op0=None, op1=None, sem=None):
        kw = {}
        if op1 is not None:
            kw["op1"] = op1
        m = e.tensor_scalar(out=out, in0=in0, scalar1=s1, scalar2=s2, op0=op0, **kw)
        if sem:
            m.then_inc(SEMS[sem], 1)

    def _stt(e, out=None, in0=None, scalar=None, in1=None, op0=None, op1=None, sem=None):
        m = e.scalar_tensor_tensor(out=out, in0=in0, scalar=scalar, in1=in1, op0=op0, op1=op1)
        if sem:
            m.then_inc(SEMS[sem], 1)

    def _red(e, out=None, in_=None, op=None, sem=None):
        m = e.tensor_reduce(out=out, in_=in_, axis=AX.X, op=op)
        if sem:
            m.then_inc(SEMS[sem], 1)

    def _act(e, out=None, in_=None, func=None, bias=0.0, scale=1.0, sem=None):
        m = e.activation(out=out, in_=in_, func=func, bias=bias, scale=scale)
        if sem:
            m.then_inc(SEMS[sem], 1)

    def _mul(e, out=None, in_=None, mul=None, sem=None):
        m = e.mul(out=out, in_=in_, mul=mul)
        if sem:
            m.then_inc(SEMS[sem], 1)

    def _recip(e, out=None, in_=None, sem=None):
        m = e.reciprocal(out=out, in_=in_)
        if sem:
            m.then_inc(SEMS[sem], 1)

    def prep_consts(e):
        e.memset(identb[:], 0.0).then_inc(SEMS["sGP"], 1)
        e.memset(identf[:], 0.0).then_inc(SEMS["sGP"], 1)
        e.memset(tri_s[:], 1.0).then_inc(SEMS["sGP"], 1)
        e.memset(tri_b[:], 1.0).then_inc(SEMS["sGP"], 1)
        e.memset(onesr[:], 1.0).then_inc(SEMS["sGP"], 1)
        for _ci, _cv in enumerate([1e-5, -0.5, 1.0 / DK, S_ATTN, 1.0, 0.0]):
            e.memset(cst[:, _ci:_ci + 1], _cv).then_inc(SEMS["sGP"], 1)
        e.wait_ge(SEMS["sGP"], 11)
        e.affine_select(out=identb[:], in_=identb[:], compare_op=ALU.not_equal,
                        fill=1.0, base=0, pattern=[[-1, 128]],
                        channel_multiplier=1).then_inc(SEMS["sGP"], 1)
        e.affine_select(out=identf[:], in_=identf[:], compare_op=ALU.not_equal,
                        fill=1.0, base=0, pattern=[[-1, 128]],
                        channel_multiplier=1).then_inc(SEMS["sGP"], 1)
        e.affine_select(out=tri_s[:], in_=tri_s[:], compare_op=ALU.is_ge,
                        fill=0.0, base=0, pattern=[[1, 128]],
                        channel_multiplier=-1).then_inc(SEMS["sGP"], 1)
        e.affine_select(out=tri_b[:], in_=tri_b[:], compare_op=ALU.is_ge,
                        fill=0.0, base=0, pattern=[[1, 128]],
                        channel_multiplier=-1).then_inc(SEMS["sGP"], 1)
    rec("gpsimd", prep_consts)
    v_consts = adv("sGP", 15)

    small_loads = [
        (gate_sb[:], gate_d.rearrange("(cg p) e -> p cg e", p=128)),
    ]
    for i in range(4):
        small_loads.append((fm_sb[:, i, :], fm_d[i]))
    for dst, srcap in small_loads:
        rec("gpsimd", _dma, out=dst, in_=srcap, sem="sG1")
    v_small = adv("sG1", 16 * len(small_loads))

    rec("sync", _dma, out=nw_sb[:], in_=nw_d, sem="sP")
    rec("sync", _dma, out=ssc_sb[:, 0:1], in_=s0_d, sem="sP")
    rec("sync", _dma, out=ssc_sb[:, 1:2], in_=s1_d, sem="sP")
    rec("sync", _dma, out=ssc_sb[:, 2:3], in_=s2_d, sem="sP")
    v_scal = adv("sP", 64)
    rec("sync", _wait, sem="sP", val=v_scal)

    for e_i in range(nexp):
        rec("gpsimd", _dma, out=kv_scr[e_i, 0], in_=kwT_d[e_i], sem="sG2")
        rec("gpsimd", _dma, out=kv_scr[e_i, 1], in_=vwT_d[e_i], sem="sG2")
    rec("gpsimd", _dma, out=qw_scr, in_=qwT_d, sem="sG2")
    v_kvscr = adv("sG2", 16 * (2 * nexp + 1))

    rec("tensor", _wait, sem="sP", val=v_scal)
    rec("tensor", _wait, sem="sGP", val=v_consts)
    rec("tensor", _mm, out=bank(3, 0, 3), lhsT=onesr[:], rhs=ssc_sb[:, 0:3],
        start=True, stop=True, sem="sPE")
    v_scolmm = adv("sPE", 1)
    rec("vector", _wait, sem="sPE", val=v_scolmm)
    rec("vector", _ts, out=scol[:, 0:3], in0=bank(3, 0, 3), s1=2.0, op0=ALU.mult, sem="sVE")
    v1 = adv("sVE", 1)
    rec("vector", _wait, sem="sVE", val=v1)
    rec("vector", _ts, out=scol[:, 2:3], in0=bank(3, 2, 1), s1=4.0, op0=ALU.mult, sem="sVE")
    rec("vector", _ts, out=scol[:, 3:4], in0=bank(3, 2, 1), s1=2.0, op0=ALU.mult, sem="sVE")
    adv("sVE", 2)
    v_scol = C["sVE"]

    rec("vector", _wait, sem="sP", val=v_scal)
    for cg in range(CG):
        for os_ in range(2):
            srcap = owT_d.rearrange("(cg p) o -> cg p o", p=128)[cg][:, os_ * 512:(os_ + 1) * 512]
            rec("sync", _dma, out=pstg_f[:], in_=srcap, sem="sP")
            vload = adv("sP", 16)
            rec("vector", _wait, sem="sP", val=vload)
            rec("vector", _ts, out=pstg_o[:, 0:512], in0=pstg_f[:],
                s1=nw_sb[:, 0:1], op0=ALU.mult, sem="sVE")
            vconv = adv("sVE", 1)
            rec("scalar", _wait, sem="sVE", val=vconv)
            rec("scalar", _dma, out=ow_scr[cg * 128:(cg + 1) * 128, os_ * 512:(os_ + 1) * 512],
                in_=pstg_o[:, 0:512], sem="sPS")
            vstore = adv("sPS", 16)
            rec("sync", _wait, sem="sPS", val=vstore)

    for cg in range(CG):
        rec("sync", _dma, out=pstg_a[:], in_=k0_d[cg * 128:(cg + 1) * 128, :], sem="sP")
        rec("sync", _dma, out=pstg_b[:], in_=k0_d[D + cg * 128: D + (cg + 1) * 128, :], sem="sP")
        vload = adv("sP", 32)
        rec("vector", _wait, sem="sP", val=vload)
        rec("vector", _tt, out=pstg_o[:], in0=pstg_a[:], in1=pstg_b[:], op=ALU.subtract, sem="sVE")
        vconv = adv("sVE", 1)
        rec("scalar", _wait, sem="sVE", val=vconv)
        rec("scalar", _dma, out=w0_scr[cg * 128:(cg + 1) * 128, :], in_=pstg_o[:], sem="sPS")
        vstore = adv("sPS", 16)
        rec("sync", _wait, sem="sPS", val=vstore)

    for g in range(F1 // 128):
        rec("sync", _dma, out=pstg_a[:], in_=k1_d[g * 128:(g + 1) * 128, :], sem="sP")
        vload = adv("sP", 16)
        rec("vector", _wait, sem="sP", val=vload)
        rec("vector", _ts, out=pstg_o[:], in0=pstg_a[:], s1=2.0, s2=-1.0,
            op0=ALU.mult, op1=ALU.add, sem="sVE")
        vconv = adv("sVE", 1)
        rec("scalar", _wait, sem="sVE", val=vconv)
        rec("scalar", _dma, out=w1_scr[g], in_=pstg_o[:], sem="sPS")
        vstore = adv("sPS", 16)
        rec("sync", _wait, sem="sPS", val=vstore)

    rec("vector", _wait, sem="sVE", val=v_scol)
    for g in range(F1 // 128):
        rec("sync", _dma, out=pstg_a[:, 0:D], in_=k2_d[g * 128:(g + 1) * 128, :], sem="sP")
        vload = adv("sP", 16)
        rec("vector", _wait, sem="sP", val=vload)
        rec("vector", _ts, out=pstg_o[:, 0:D], in0=pstg_a[:, 0:D], s1=scol[:, 2:3],
            s2=scol[:, 3:4], op0=ALU.mult, op1=ALU.subtract, sem="sVE")
        vconv = adv("sVE", 1)
        rec("scalar", _wait, sem="sVE", val=vconv)
        rec("scalar", _dma, out=w2_scr[g * 128:(g + 1) * 128, :], in_=pstg_o[:, 0:D], sem="sPS")
        vstore = adv("sPS", 16)
        rec("sync", _wait, sem="sPS", val=vstore)

    v_prep_sS = C["sPS"]
    v_prep_sVE = C["sVE"]

    for eng in ["tensor", "vector", "scalar"]:
        rec(eng, _wait, sem="sG1", val=v_small)
        rec(eng, _wait, sem="sG2", val=v_kvscr)
        rec(eng, _wait, sem="sPS", val=v_prep_sS)
        rec(eng, _wait, sem="sVE", val=v_prep_sVE)
        rec(eng, _wait, sem="sGP", val=v_consts)

    last = dict(
        kv_user=0, e3_evict=0, b2_free=0, e_pe_done=0,
        w0_user=0, w1_user=[0, 0], w2_user=[0, 0], out_user=[0, 0],
        stsave=[0] * E_EXP, c5_ve_done=0,
    )

    for seg in range(nseg):
        b = seg // spb
        sib = seg % spb
        t0 = sib * SEG
        first_seg = (sib == 0)
        last_seg = (sib == spb - 1)

        rec("sync", _wait, sem="sVE", val=last["e3_evict"])
        rec("sync", _wait, sem="sX", val=C["sX"])
        if cfg["debug"] and seg > 0:
            for eng in ["vector", "scalar", "tensor"]:
                rec(eng, _wait, sem="sOUT", val=C["sOUT"])
        for tc in range(NTC):
            rec("sync", _dma, out=x_seg[:, tc, :],
                in_=x_d[b, t0 + tc * 128: t0 + (tc + 1) * 128, :], sem="sX")
            adv("sX", 16)
        xw_total = C["sX"]
        xw = [xw_total] * NTC

        v_xn_tr = [0] * NTC
        v_xnfm = [0] * NTC
        xall = x_seg[:].rearrange("p a d -> p (a d)")
        tmp_sq4 = O_acc[:].rearrange("p a d -> p (a d)")
        rec("vector", _wait, sem="sX", val=xw[-1])
        rec("vector", _wait, sem="sVE", val=last["e3_evict"])
        rec("vector", _tt, out=tmp_sq4, in0=xall, in1=xall, op=ALU.mult, sem="sVE")
        vsq = adv("sVE", 1)
        rec("vector", _red, out=rt[:, 0:NTC], in_=x_seg[:], op=ALU.add, sem="sVE")
        adv("sVE", 1)
        rec("vector", _wait, sem="sVE", val=vsq)
        rec("vector", _red, out=rt[:, 4:4 + NTC],
            in_=tmp_sq4.rearrange("p (a d) -> p a d", a=NTC), op=ALU.add, sem="sVE")
        vred = adv("sVE", 1)
        rec("vector", _wait, sem="sVE", val=vred)
        rec("vector", _ts, out=rt[:, 0:NTC], in0=rt[:, 0:NTC], s1=1.0 / D, op0=ALU.mult, sem="sVE")
        vm_ = adv("sVE", 1)
        rec("vector", _wait, sem="sVE", val=vm_)
        rec("vector", _ts, out=rt[:, 4:4 + NTC], in0=rt[:, 4:4 + NTC], s1=1.0 / D,
            op0=ALU.mult, sem="sVE")
        vq_ = adv("sVE", 1)
        rec("vector", _wait, sem="sVE", val=vq_)
        rec("vector", _tt, out=rt[:, 8:8 + NTC], in0=rt[:, 0:NTC], in1=rt[:, 0:NTC],
            op=ALU.mult, sem="sVE")
        vmm2 = adv("sVE", 1)
        rec("vector", _wait, sem="sVE", val=vmm2)
        rec("vector", _tt, out=rt[:, 8:8 + NTC], in0=rt[:, 4:4 + NTC], in1=rt[:, 8:8 + NTC],
            op=ALU.subtract, sem="sVE")
        vvar = adv("sVE", 1)
        rec("scalar", _wait, sem="sVE", val=vvar)
        rec("scalar", _act, out=rt[:, 8:8 + NTC], in_=rt[:, 8:8 + NTC], func=AF.Ln, bias=1e-5,
            sem="sSC")
        vln = adv("sSC", 1)
        rec("scalar", _wait, sem="sSC", val=vln)
        rec("scalar", _act, out=rt[:, 12:12 + NTC], in_=rt[:, 8:8 + NTC], func=AF.Exp, scale=-0.5,
            sem="sSC")
        vrs = adv("sSC", 1)
        rec("vector", _wait, sem="sSC", val=vrs)
        rec("vector", _tt, out=rt[:, 16:16 + NTC], in0=rt[:, 0:NTC], in1=rt[:, 12:12 + NTC],
            op=ALU.mult, sem="sVE")
        vmr = adv("sVE", 1)
        rec("vector", _wait, sem="sVE", val=vmr)
        rec("vector", _ts, out=rt[:, 16:16 + NTC], in0=rt[:, 16:16 + NTC], s1=-1.0,
            op0=ALU.mult, sem="sVE")
        vnm = adv("sVE", 1)
        for tc in range(NTC):
            rec("scalar", _wait, sem="sVE", val=vnm)
            if tc > 1:
                rec("scalar", _wait, sem="sPE", val=v_xn_tr[tc - 2])
            rec("scalar", _act, out=xn_tmp[:, tc % 2, :], in_=x_seg[:, tc, :], func=AF.Identity,
                scale=rt[:, 12 + tc:13 + tc], bias=rt[:, 16 + tc:17 + tc], sem="sSC")
            vxn = adv("sSC", 1)
            rec("tensor", _wait, sem="sSC", val=vxn)
            rec("tensor", _wait, sem="sVE", val=last["b2_free"])
            for g in range(CG):
                rec("tensor", _tr, out=bankbf(2, g * 128, 128),
                    in_=xn_tmp[:, tc % 2, g * 128:(g + 1) * 128], ident=identb[:],
                    sem="sPE" if g == CG - 1 else None)
            v_xn_tr[tc] = adv("sPE", 1)
            rec("vector", _wait, sem="sPE", val=v_xn_tr[tc])
            rec("vector", _cp,
                out=xn_fm[:, :, tc * 128:(tc + 1) * 128],
                in_=bankbf(2, 0, 1024).rearrange("p (g t) -> p g t", g=CG), sem="sVE")
            v_xnfm[tc] = adv("sVE", 1)
            last["b2_free"] = v_xnfm[tc]

        for tc in range(NTC):
            rec("tensor", _wait, sem="sVE", val=v_xnfm[tc])
            for cg in range(CG):
                rec("tensor", _mm, out=bank(3, tc * E_EXP, E_EXP),
                    lhsT=xn_fm[:, cg, tc * 128:(tc + 1) * 128],
                    rhs=gate_sb[:, cg, :], start=(cg == 0), stop=(cg == CG - 1),
                    sem="sPE" if cg == CG - 1 else None)
        v_lg = adv("sPE", NTC)
        lg3 = bank(3, 0, NTC * E_EXP).rearrange("p (a e) -> p a e", a=NTC)
        rec("vector", _wait, sem="sPE", val=v_lg)
        rec("vector", _red, out=rt[:, 10:10 + NTC], in_=lg3, op=ALU.max, sem="sVE")
        vm1 = adv("sVE", 1)
        rec("vector", _wait, sem="sVE", val=vm1)
        m1b = rt[:, 10:10 + NTC].unsqueeze(2).broadcast_to([128, NTC, E_EXP])
        rec("vector", _tt, out=rt3a[:], in0=lg3, in1=m1b, op=ALU.is_equal, sem="sVE")
        veq1 = adv("sVE", 1)
        rec("vector", _wait, sem="sVE", val=veq1)
        rec("vector", _stt, out=rt3b[:], in0=rt3a[:], scalar=-1e30, in1=lg3,
            op0=ALU.mult, op1=ALU.add, sem="sVE")
        vl2 = adv("sVE", 1)
        rec("vector", _wait, sem="sVE", val=vl2)
        rec("vector", _red, out=rt[:, 14:14 + NTC], in_=rt3b[:], op=ALU.max, sem="sVE")
        vm2 = adv("sVE", 1)
        rec("vector", _wait, sem="sVE", val=vm2)
        rec("vector", _tt, out=rt[:, 18:18 + NTC], in0=rt[:, 14:14 + NTC],
            in1=rt[:, 10:10 + NTC], op=ALU.subtract, sem="sVE")
        vd = adv("sVE", 1)
        m2b = rt[:, 14:14 + NTC].unsqueeze(2).broadcast_to([128, NTC, E_EXP])
        rec("vector", _tt, out=rt3c[:], in0=rt3b[:], in1=m2b, op=ALU.is_equal, sem="sVE")
        veq2 = adv("sVE", 1)
        rec("scalar", _wait, sem="sVE", val=vd)
        rec("scalar", _act, out=rt[:, 10:10 + NTC], in_=rt[:, 18:18 + NTC], func=AF.Exp, sem="sSC")
        vex = adv("sSC", 1)
        rec("vector", _wait, sem="sSC", val=vex)
        rec("vector", _wait, sem="sVE", val=veq2)
        rec("vector", _ts, out=rt[:, 14:14 + NTC], in0=rt[:, 10:10 + NTC], s1=1.0,
            op0=ALU.add, sem="sVE")
        vp1 = adv("sVE", 1)
        rec("vector", _wait, sem="sVE", val=vp1)
        rec("vector", _recip, out=rt[:, 18:18 + NTC], in_=rt[:, 14:14 + NTC], sem="sVE")
        vw1 = adv("sVE", 1)
        rec("vector", _wait, sem="sVE", val=vw1)
        rec("vector", _tt, out=rt[:, 14:14 + NTC], in0=rt[:, 18:18 + NTC],
            in1=rt[:, 10:10 + NTC], op=ALU.mult, sem="sVE")
        vw2 = adv("sVE", 1)
        rec("vector", _wait, sem="sVE", val=max(vw2, veq2))
        w1b = rt[:, 18:18 + NTC].unsqueeze(2).broadcast_to([128, NTC, E_EXP])
        w2b = rt[:, 14:14 + NTC].unsqueeze(2).broadcast_to([128, NTC, E_EXP])
        rec("vector", _tt, out=mask01[:], in0=rt3a[:], in1=rt3c[:], op=ALU.add, sem="sVE")
        adv("sVE", 1)
        rec("vector", _tt, out=rt3b[:], in0=rt3a[:], in1=w1b, op=ALU.mult, sem="sVE")
        vt1 = adv("sVE", 1)
        rec("vector", _wait, sem="sVE", val=vt1)
        rec("vector", _tt, out=rt3c[:], in0=rt3c[:], in1=w2b, op=ALU.mult, sem="sVE")
        vt2 = adv("sVE", 1)
        rec("vector", _wait, sem="sVE", val=vt2)
        rec("vector", _tt, out=wgt01[:], in0=rt3b[:], in1=rt3c[:], op=ALU.add, sem="sVE")
        v_route = adv("sVE", 1)

        rec("sync", _wait, sem="sG2", val=v_kvscr)
        rec("sync", _wait, sem="sPE", val=max(last["e_pe_done"], last["kv_user"]))
        rec("sync", _wait, sem="sOW", val=C["sOW"])
        if cfg["debug"]:
            rec("sync", _wait, sem="sOUT", val=C["sOUT"])
        rec("sync", _dma, out=kvw_sb[:],
            in_=qw_scr.rearrange("(cg p) o -> p cg o", p=128), sem="sQW")
        v_qw = adv("sQW", 16)
        v_qh = [0] * H
        v_qe = [0] * H
        v_fm = [0] * H
        v_qp = [0] * H
        v_ae = [0] * H

        def b_qhat(h):
            kb = 0 if h % 2 == 0 else 4
            rec("tensor", _wait, sem="sVE", val=v_xnfm[-1])
            rec("tensor", _wait, sem="sQW", val=v_qw)
            if h >= 2:
                rec("tensor", _wait, sem="sSC", val=v_qe[h - 2])
            for cg in range(CG):
                rec("tensor", _mm, out=bank(kb), lhsT=kvw_sb[:, cg, h * 128:(h + 1) * 128],
                    rhs=xn_fm[:, cg, :], start=(cg == 0), stop=(cg == CG - 1),
                    sem="sPE" if cg == CG - 1 else None)
            v_qh[h] = adv("sPE", 1)

        def b_ke(h):
            kb = 0 if h % 2 == 0 else 4
            rec("scalar", _wait, sem="sPE", val=v_qh[h])
            if h == 0:
                rec("scalar", _wait, sem="sOUT", val=C["sOUT"])
            rec("scalar", _scp, out=Khat2[:, h % 2, :], in_=bank(kb), sem="sSC")
            v_qe[h] = adv("sSC", 1)

        def b_fm(h):
            rec("tensor", _wait, sem="sSC", val=v_qe[h])
            if h >= 1:
                rec("tensor", _wait, sem="sSC", val=v_ae[h - 1])
                rec("tensor", _wait, sem="sVE", val=v_qp[h - 1])
            rec("tensor", _mm, out=bank(1), lhsT=fm_sb[:, 0, :], rhs=Khat2[:, h % 2, :],
                start=True, stop=True)
            rec("tensor", _mm, out=bank(5), lhsT=fm_sb[:, 1, :], rhs=Khat2[:, h % 2, :],
                start=True, stop=True, sem="sPE")
            v_fm[h] = adv("sPE", 1)

        def b_fme(h):
            rec("scalar", _wait, sem="sPE", val=v_fm[h])
            if h >= 1:
                rec("scalar", _wait, sem="sVE", val=v_qp[h - 1])
            rec("scalar", _mul, out=A1s[:], in_=bank(1), mul=S_ATTN, sem="sSC")
            v_ae[h] = adv("sSC", 1)
            rec("vector", _wait, sem="sSC", val=v_ae[h])
            rec("vector", _tt, out=Qp_fm[:, h, :], in0=A1s[:], in1=bank(5), op=ALU.mult, sem="sVE")
            v_qp[h] = adv("sVE", 1)

        v_h1 = []
        e1 = dict(next=0, w0=0)
        E1BANKS = [2, 3, 6, 7]

        def emit_e1_group():
            og_abs = e1["next"]
            if og_abs >= F1 // 128:
                return
            e1["next"] += 1
            if og_abs % 4 == 0:
                os_ = og_abs // 4
                rec("sync", _wait, sem="sPE", val=max(last["w0_user"], last["e_pe_done"]))
                rec("sync", _dma, out=w01_sb[:],
                    in_=w0_scr[:, os_ * 512:(os_ + 1) * 512].rearrange(
                        "(cg p) o -> p cg o", p=128), sem="sW0")
                e1["w0"] = adv("sW0", 16)
            bk = E1BANKS[og_abs % 4]
            rec("tensor", _wait, sem="sW0", val=e1["w0"])
            if og_abs == 0:
                rec("tensor", _wait, sem="sVE", val=v_route)
            if len(v_h1) >= 4:
                rec("tensor", _wait, sem="sSC", val=v_h1[-4])
            for cg in range(CG):
                rec("tensor", _mm, out=bank(bk),
                    lhsT=w01_sb[:, cg, (og_abs % 4) * 128:(og_abs % 4) * 128 + 128],
                    rhs=xn_fm[:, cg, :], start=(cg == 0), stop=(cg == CG - 1),
                    sem="sPE" if cg == CG - 1 else None)
            vm = adv("sPE", 1)
            last["w0_user"] = vm
            rec("scalar", _wait, sem="sPE", val=vm)
            rec("scalar", _act, out=H1[:, og_abs, :], in_=bank(bk), func=AF.Relu,
                scale=scol[:, 0:1], sem="sSC")
            v_h1.append(adv("sSC", 1))

        b_qhat(0); b_ke(0); b_qhat(1); b_ke(1)
        for h in range(H):
            b_fm(h)
            if h + 2 < H:
                b_qhat(h + 2)
            b_fme(h)
            if h + 2 < H:
                b_ke(h + 2)
            for _ in range(4):
                emit_e1_group()
        while e1["next"] < F1 // 128:
            emit_e1_group()
        v_h1_done = v_h1[-1]
        v_qp_done = v_qp[-1]
        last["kv_user"] = max(last["kv_user"], v_fm[-1])

        fil = dict(ring=0, cons=[0, 0], vcons=[0, 0], next_h2=0, v_h2_last=0,
                   w1_user=last["w0_user"], slot=0, nslots=nexp * 10 + NTC)

        def maybe_fill():
            fil["slot"] += 1
            target = (F1 // 128) * fil["slot"] // fil["nslots"]
            while fil["next_h2"] < target:
                emit_h2_slice()

        def emit_h2_slice():
            o2g_abs = fil["next_h2"]
            if o2g_abs >= F1 // 128:
                return
            fil["next_h2"] += 1
            off = 3584
            rec("sync", _wait, sem="sPE", val=fil["w1_user"])
            rec("sync", _dma, out=w1v_sb[:],
                in_=w1_scr[:, :, o2g_abs * 128:(o2g_abs + 1) * 128].rearrange("g p o -> p g o"),
                sem="sW1A")
            v_w1 = adv("sW1A", 16)
            rec("tensor", _wait, sem="sW1A", val=v_w1)
            rec("tensor", _wait, sem="sSC", val=max(v_h1_done, fil["cons"][0]))
            rec("tensor", _wait, sem="sVE", val=fil["vcons"][0])
            for o1g in range(F1 // 128):
                rec("tensor", _mm, out=pflat(off, 512),
                    lhsT=w1v_sb[:, o1g, :],
                    rhs=H1[:, o1g, :], start=(o1g == 0), stop=(o1g == F1 // 128 - 1),
                    sem="sPE" if o1g == F1 // 128 - 1 else None)
            vm = adv("sPE", 1)
            fil["w1_user"] = vm
            rec("scalar", _wait, sem="sPE", val=vm)
            rec("scalar", _act, out=H2[:, o2g_abs, :], in_=pflat(off, 512), func=AF.Relu,
                scale=scol[:, 1:2], sem="sSC")
            fil["cons"][0] = adv("sSC", 1)
            fil["v_h2_last"] = fil["cons"][0]

        for ei in range(nexp):
            rec("sync", _wait, sem="sG2", val=v_kvscr)
            rec("sync", _wait, sem="sPE", val=max(last["kv_user"], last["e_pe_done"]))
            rec("sync", _dma, out=kvw_sb[:],
                in_=kv_scr[ei, 0].rearrange("(cg p) o -> p cg o", p=128), sem="sKV")
            v_kv = adv("sKV", 16)
            kw_v = kvw_sb
            vw_v = kvw_sb


            v_kh = [0] * H
            v_ke = [0] * H
            v_fmk = [0] * H
            v_kph = [0] * H
            v_aek = [0] * H
            rec("tensor", _wait, sem="sVE", val=max(v_qp_done, last["c5_ve_done"]))
            rec("tensor", _wait, sem="sSC", val=v_h1_done)

            def c_khat(h):
                kb = [0, 4, 6][h % 3]
                rec("tensor", _wait, sem="sKV", val=v_kv)
                if h >= 3:
                    rec("tensor", _wait, sem="sSC", val=v_ke[h - 3])
                for cg in range(CG):
                    rec("tensor", _mm, out=bank(kb), lhsT=kw_v[:, cg, h * 128:(h + 1) * 128],
                        rhs=xn_fm[:, cg, :], start=(cg == 0), stop=(cg == CG - 1),
                        sem="sPE" if cg == CG - 1 else None)
                v_kh[h] = adv("sPE", 1)

            def c_ke(h):
                kb = [0, 4, 6][h % 3]
                rec("scalar", _wait, sem="sPE", val=v_kh[h])
                rec("scalar", _scp, out=Khat2[:, h % 3, :], in_=bank(kb), sem="sSC")
                v_ke[h] = adv("sSC", 1)

            def c_fm(h):
                rec("tensor", _wait, sem="sSC", val=v_ke[h])
                if h >= 1:
                    rec("tensor", _wait, sem="sSC", val=v_aek[h - 1])
                    rec("tensor", _wait, sem="sVE", val=v_kph[h - 1])
                rec("tensor", _mm, out=bank(1), lhsT=fm_sb[:, 2, :], rhs=Khat2[:, h % 3, :],
                    start=True, stop=True)
                rec("tensor", _mm, out=bank(5), lhsT=fm_sb[:, 3, :], rhs=Khat2[:, h % 3, :],
                    start=True, stop=True)
                for tc in range(NTC):
                    rec("tensor", _mm, out=bank(2, tc * 128, 128),
                        lhsT=Khat2[:, h % 3, tc * 128:(tc + 1) * 128], rhs=fm_sb[:, 2, :],
                        start=True, stop=True)
                for tc in range(NTC):
                    rec("tensor", _mm, out=bank(3, tc * 128, 128),
                        lhsT=Khat2[:, h % 3, tc * 128:(tc + 1) * 128], rhs=fm_sb[:, 3, :],
                        start=True, stop=True, sem="sPE" if tc == NTC - 1 else None)
                v_fmk[h] = adv("sPE", 1)

            def c_fme(h):
                rec("scalar", _wait, sem="sPE", val=v_fmk[h])
                if h >= 1:
                    rec("scalar", _wait, sem="sVE", val=v_kph[h - 1])
                rec("scalar", _scp, out=A1s[:], in_=bank(1), sem="sSC")
                rec("scalar", _scp, out=A1s2[:], in_=bank(2), sem="sSC")
                adv("sSC", 1)
                v_aek[h] = adv("sSC", 1)
                rec("vector", _wait, sem="sSC", val=v_aek[h])
                rec("vector", _tt, out=Kp_fm[:, h, :], in0=A1s[:], in1=bank(5), op=ALU.mult,
                    sem="sVE")
                rec("vector", _tt, out=Kp_tm[:, :, h, :],
                    in0=A1s2[:].rearrange("p (a d) -> p a d", a=NTC),
                    in1=bank(3).rearrange("p (a d) -> p a d", a=NTC), op=ALU.mult, sem="sVE")
                adv("sVE", 1)
                v_kph[h] = adv("sVE", 1)

            c_khat(0); c_ke(0); c_khat(1); c_ke(1); c_khat(2); c_ke(2)
            for h in range(H):
                c_fm(h)
                if h + 3 < H:
                    c_khat(h + 3)
                c_fme(h)
                if h + 3 < H:
                    c_ke(h + 3)
                if h % 2 == 1:
                    maybe_fill()

            rec("sync", _wait, sem="sPE", val=v_kh[H - 1])
            rec("sync", _dma, out=kvw_sb[:],
                in_=kv_scr[ei, 1].rearrange("(cg p) o -> p cg o", p=128), sem="sKV")
            v_kv2 = adv("sKV", 16)

            rec("vector", _wait, sem="sVE", val=v_route)
            v_vp = []
            v_vp_pe = 0
            for tc in range(NTC):
                for vs in range(2):
                    bk = [0, 1, 5][(tc * 2 + vs) % 3]
                    if tc == 0 and vs == 0:
                        rec("tensor", _wait, sem="sKV", val=v_kv2)
                    if len(v_vp) < 3:
                        rec("tensor", _wait, sem="sVE", val=v_kph[-1])
                    if len(v_vp) >= 3:
                        rec("tensor", _wait, sem="sVE", val=v_vp[-3])
                    for cg in range(CG):
                        rec("tensor", _mm, out=bank(bk),
                            lhsT=xn_fm[:, cg, tc * 128:(tc + 1) * 128],
                            rhs=vw_v[:, cg, vs * 512:(vs + 1) * 512],
                            start=(cg == 0), stop=(cg == CG - 1),
                            sem="sPE" if cg == CG - 1 else None)
                    vvh = adv("sPE", 1)
                    v_vp_pe = vvh
                    rec("vector", _wait, sem="sPE", val=vvh)
                    rec("vector", _ts, out=Vp_tm[:, tc, vs * 4:(vs + 1) * 4, :],
                        in0=bank(bk).rearrange("p (a d) -> p a d", a=4),
                        s1=mask01[:, tc, ei:ei + 1], op0=ALU.mult, sem="sVE")
                    v_vp.append(adv("sVE", 1))
                if tc % 2 == 1:
                    maybe_fill()
            v_vp_done = v_vp[-1]
            last["kv_user"] = v_vp_pe

            has_state = not first_seg
            rec("tensor", _wait, sem="sVE", val=v_vp_done)
            rec("tensor", _wait, sem="sSC", val=C["sSC"])
            v_sbfcopy = C["sSC"]
            if not first_seg:
                rec("sync", _wait, sem="sSV", val=C["sSV"])
                rec("sync", _wait, sem="sSC", val=C["sSC"])
                rec("sync", _wait, sem="sVE", val=C["sVE"])
                rec("sync", _dma, out=S_mast[:], in_=st_scr[ei], sem="sST")
                v_strest = adv("sST", 16)
                rec("scalar", _wait, sem="sST", val=v_strest)
                rec("scalar", _wait, sem="sPE", val=v_vp_pe)
                rec("scalar", _scp, out=S_bf[:].rearrange("p h v -> p (h v)"),
                    in_=S_mast[:].rearrange("p h v -> p (h v)"), sem="sSC")
                v_sbfcopy = adv("sSC", 1)

            p_even = [0, 512, 1024, 896]
            p_odd = [1536, 2048, 1280, 2432]
            a_off = [0, 512, 896, 1152]
            slot_n = [512, 384, 256, 128]
            v_aT = [0] * H
            v_omm = [0] * H
            v_dtt = [0] * H
            v_oc = [0] * H
            v_oev = [0] * H
            v_dadd = [0] * H

            def c5_delta(j):
                off = 3584
                rec("tensor", _wait, sem="sSC", val=fil["cons"][0])
                rec("tensor", _wait, sem="sVE", val=fil["vcons"][0])
                for jc in range(NTC):
                    rec("tensor", _mm, out=pflat(off, 128),
                        lhsT=Kp_tm[:, jc, j, :], rhs=Vp_tm[:, jc, j, :],
                        start=(jc == 0), stop=(jc == NTC - 1),
                        sem="sPE" if jc == NTC - 1 else None)
                vm = adv("sPE", 1)
                rec("vector", _wait, sem="sPE", val=vm)
                if j == 0:
                    rec("vector", _wait, sem="sSC", val=v_sbfcopy)
                    if first_seg:
                        rec("vector", _wait, sem="sSV", val=C["sSV"])
                if first_seg:
                    rec("vector", _cp, out=S_mast[:, j, :], in_=pflat(off, 128), sem="sVE")
                else:
                    rec("vector", _stt, out=S_mast[:, j, :], in0=pflat(off, 128),
                        scalar=1.0, in1=S_mast[:, j, :], op0=ALU.mult, op1=ALU.add, sem="sVE")
                fil["vcons"][0] = adv("sVE", 1)
                v_dadd[j] = fil["vcons"][0]

            def c5_av(j):
                par = j % 2
                o_off = 2560 if par == 0 else 3072
                rec("tensor", _wait, sem="sVE", val=v_dtt[j])
                rec("tensor", _wait, sem="sSC", val=v_oc[j])
                if j >= 2:
                    rec("tensor", _wait, sem="sVE", val=v_oev[j - 2])
                if has_state and j == 0:
                    rec("tensor", _wait, sem="sSC", val=v_sbfcopy)
                for il in range(NTC):
                    nmm = il + 1 + (1 if has_state else 0)
                    k = 0
                    for jc in range(il + 1):
                        k += 1
                        rec("tensor", _mm, out=pflat(o_off + il * 128, 128),
                            lhsT=aT_big[:, par, a_off[jc] + (il - jc) * 128:
                                        a_off[jc] + (il - jc) * 128 + 128],
                            rhs=Vp_tm[:, jc, j, :], start=(k == 1), stop=(k == nmm),
                            sem="sPE" if (il == NTC - 1 and k == nmm) else None)
                    if has_state:
                        k += 1
                        rec("tensor", _mm, out=pflat(o_off + il * 128, 128),
                            lhsT=Qp_fm[:, j, il * 128:(il + 1) * 128], rhs=S_bf[:, j, :],
                            start=False, stop=True,
                            sem="sPE" if (il == NTC - 1) else None)
                v_omm[j] = adv("sPE", 1)
                rec("vector", _wait, sem="sPE", val=v_omm[j])
                for il in range(NTC):
                    wcol = wgt01[:, il, ei:ei + 1]
                    if ei == 0:
                        rec("vector", _ts, out=O_acc[:, il, j * 128:(j + 1) * 128],
                            in0=pflat(o_off + il * 128, 128), s1=wcol, op0=ALU.mult, sem="sVE")
                    else:
                        rec("vector", _stt, out=O_acc[:, il, j * 128:(j + 1) * 128],
                            in0=pflat(o_off + il * 128, 128), scalar=wcol,
                            in1=O_acc[:, il, j * 128:(j + 1) * 128],
                            op0=ALU.mult, op1=ALU.add, sem="sVE")
                adv("sVE", 3)
                v_oev[j] = adv("sVE", 1)

            for h in range(H):
                par = h % 2
                poff = p_even if par == 0 else p_odd
                if h >= 2:
                    rec("tensor", _wait, sem="sVE", val=v_dtt[h - 2])
                    rec("tensor", _wait, sem="sSC", val=v_oc[h - 2])
                for jc in (0, 1, 3):
                    rec("tensor", _mm, out=pflat(poff[jc], slot_n[jc]),
                        lhsT=Kp_fm[:, h, jc * 128:(jc + 1) * 128],
                        rhs=Qp_fm[:, h, jc * 128:512], start=True, stop=True)
                c5_delta(h)
                if h >= 1:
                    rec("tensor", _wait, sem="sVE", val=v_dtt[h - 1])
                    rec("tensor", _wait, sem="sSC", val=v_oc[h - 1])
                rec("tensor", _mm, out=pflat(poff[2], slot_n[2]),
                    lhsT=Kp_fm[:, h, 256:384], rhs=Qp_fm[:, h, 256:512],
                    start=True, stop=True, sem="sPE")
                v_aT[h] = adv("sPE", 1)
                rec("vector", _wait, sem="sPE", val=v_aT[h])
                for jc in range(NTC):
                    rec("vector", _tt, out=aT_big[:, par, a_off[jc]: a_off[jc] + 128],
                        in0=pflat(poff[jc], 128), in1=tri_s[:], op=ALU.mult,
                        sem="sVE")
                adv("sVE", 3)
                v_dtt[h] = adv("sVE", 1)
                rec("scalar", _wait, sem="sPE", val=v_aT[h])
                rec("scalar", _wait, sem="sVE", val=v_dtt[h])
                for jc, w in ((0, 384), (1, 256), (2, 128)):
                    rec("scalar", _scp,
                        out=aT_big[:, par, a_off[jc] + 128: a_off[jc] + 128 + w],
                        in_=pflat(poff[jc] + 128, w), sem="sSC")
                adv("sSC", 2)
                v_oc[h] = adv("sSC", 1)

                if h >= 1:
                    c5_av(h - 1)
                if h % 2 == 1:
                    maybe_fill()
            c5_av(H - 1)

            if not last_seg:
                rec("scalar", _wait, sem="sVE", val=v_dadd[H - 1])
                rec("scalar", _dma, out=st_scr[ei], in_=S_mast[:], sem="sSV")
                last["stsave"][ei] = adv("sSV", 16)
            last["c5_ve_done"] = C["sVE"]

        c_pe_done = C["sPE"]

        v_onrm_tr = [0] * NTC
        rec("vector", _wait, sem="sVE", val=v_oev[H - 1])
        vsq_d = 0
        for tc in range(NTC):
            o3 = O_acc[:, tc, :].rearrange("p (h v) -> p h v", h=H)
            if vsq_d:
                rec("vector", _wait, sem="sVE", val=C["sVE"])
            rec("vector", _tt, out=tmp_sq[:].rearrange("p (h v) -> p h v", h=H),
                in0=o3, in1=o3, op=ALU.mult, sem="sVE")
            vsq_d = adv("sVE", 1)
            rec("vector", _wait, sem="sVE", val=vsq_d)
            rec("vector", _red, out=rms_s[:, tc, :],
                in_=tmp_sq[:].rearrange("p (h v) -> p h v", h=H), op=ALU.add, sem="sVE")
            adv("sVE", 1)
        vred_d = C["sVE"]
        rec("scalar", _wait, sem="sVE", val=vred_d)
        rec("scalar", _act, out=rms_s[:].rearrange("p a h -> p (a h)"),
            in_=rms_s[:].rearrange("p a h -> p (a h)"), func=AF.Ln, scale=1.0 / DK,
            bias=1e-5, sem="sSC")
        vln_d = adv("sSC", 1)
        rec("scalar", _wait, sem="sSC", val=vln_d)
        rec("scalar", _act, out=rms_r[:].rearrange("p a h -> p (a h)"),
            in_=rms_s[:].rearrange("p a h -> p (a h)"), func=AF.Exp, scale=-0.5, sem="sSC")
        vrs_d = adv("sSC", 1)
        for tc in range(NTC):
            maybe_fill()
            o3 = O_acc[:, tc, :].rearrange("p (h v) -> p h v", h=H)
            rec("vector", _wait, sem="sSC", val=vrs_d)
            if tc > 1:
                rec("vector", _wait, sem="sPE", val=v_onrm_tr[tc - 2])
            rb = rms_r[:, tc, :].unsqueeze(2).broadcast_to([128, H, DK])
            rec("vector", _tt, out=Onrm[:, tc % 2, :].rearrange("p (h v) -> p h v", h=H),
                in0=o3, in1=rb, op=ALU.mult, sem="sVE")
            vON = adv("sVE", 1)
            rec("tensor", _wait, sem="sVE", val=vON)
            rec("tensor", _wait, sem="sVE", val=last["b2_free"])
            for g in range(CG):
                rec("tensor", _tr, out=bankbf(2, g * 128, 128),
                    in_=Onrm[:, tc % 2, g * 128:(g + 1) * 128], ident=identb[:],
                    sem="sPE" if g == CG - 1 else None)
            v_onrm_tr[tc] = adv("sPE", 1)
            rec("vector", _wait, sem="sPE", val=v_onrm_tr[tc])
            rec("vector", _cp, out=Onrm_fm[:, :, tc * 128:(tc + 1) * 128],
                in_=bankbf(2, 0, 1024).rearrange("p (g t) -> p g t", g=CG), sem="sVE")
            vev = adv("sVE", 1)
            last["b2_free"] = vev
        v_onrm_done = last["b2_free"]

        while fil["next_h2"] < F1 // 128:
            emit_h2_slice()
        v_h2_done = fil["v_h2_last"]

        rec("sync", _wait, sem="sPE", val=c_pe_done)
        rec("sync", _wait, sem="sOW", val=C["sOW"])
        rec("sync", _dma, out=owT_sb[:],
            in_=ow_scr.rearrange("(cg p) o -> p cg o", p=128), sem="sOW")
        v_ow = adv("sOW", 16)
        rec("tensor", _wait, sem="sSC", val=v_h2_done)
        rec("tensor", _wait, sem="sVE", val=v_onrm_done)
        rec("tensor", _wait, sem="sOW", val=v_ow)
        for tc in range(NTC):
            for g in range(CG):
                for ds in range(2):
                    rec("tensor", _mm, out=pflat(tc * 1024 + ds * 512, 512),
                        lhsT=Onrm_fm[:, g, tc * 128:(tc + 1) * 128],
                        rhs=owT_sb[:, g, ds * 512:(ds + 1) * 512],
                        start=(g == 0), stop=False,
                        sem="sPE" if (tc == NTC - 1 and g == CG - 1 and ds == 1) else None)
        v_attn_mm = adv("sPE", 1)
        for o2s in range(8):
            par = o2s % 2
            w2sem = "sW2" if par == 0 else "sW2B"
            rec("sync", _wait, sem="sPE", val=max(last["w2_user"][par], c_pe_done))
            rec("sync", _dma, out=w2_sb[:, par],
                in_=w2_scr[o2s * 512:(o2s + 1) * 512, :].rearrange("(s p) d -> p s d", p=128),
                sem=w2sem)
            v_w2 = adv(w2sem, 16)
            rec("tensor", _wait, sem=w2sem, val=v_w2)
            lastslice = (o2s == 7)
            for tc in range(NTC):
                for sub in range(4):
                    o2g = o2s * 4 + sub
                    for ds in range(2):
                        is_last = lastslice and sub == 3
                        rec("tensor", _mm,
                            out=pflat(tc * 1024 + ds * 512, 512),
                            lhsT=H2[:, o2g, tc * 128:(tc + 1) * 128],
                            rhs=w2_sb[:, par, sub, ds * 512:(ds + 1) * 512],
                            start=False, stop=is_last,
                            sem="sPE" if (tc == NTC - 1 and sub == 3 and ds == 1) else None)
            last["w2_user"][par] = adv("sPE", 1)
        v_e3 = C["sPE"]
        last["w2_user"] = [v_e3, v_e3]
        for tc in range(NTC):
            par = tc % 2
            rec("vector", _wait, sem="sPE", val=v_e3)
            if tc >= 2:
                rec("vector", _wait, sem="sOUT", val=C["sOUT"])
            rec("vector", _tt, out=out_sb[:, par, :], in0=pflat(tc * 1024, 1024),
                in1=x_seg[:, tc, :], op=ALU.add, sem="sVE")
            vev = adv("sVE", 1)
            rec("scalar", _wait, sem="sVE", val=vev)
            rec("scalar", _dma, out=out_d[b, t0 + tc * 128: t0 + (tc + 1) * 128, :],
                in_=out_sb[:, par, :], sem="sOUT")
            last["out_user"][par] = adv("sOUT", 16)
        last["e3_evict"] = C["sVE"]
        last["b2_free"] = C["sVE"]
        last["e_pe_done"] = v_e3

        if cfg["debug"] and seg == 0:
            rec("scalar", _wait, sem="sVE", val=C["sVE"])
            for name, srcap in [("xn_fm", xn_fm[:]), ("wgt01", wgt01[:]), ("mask01", mask01[:]),
                              ("Qp", Qp_fm[:]), ("Oacc", O_acc[:]), ("H1", H1[:]),
                              ("H2", H2[:]), ("Onrmfm", Onrm_fm[:])]:
                rec("scalar", _dma, out=dbg[name], in_=srcap, sem="sOUT")
                adv("sOUT", 16)

    rec("scalar", _wait, sem="sOUT", val=C["sOUT"])
    rec("scalar", _wait, sem="sSV", val=C["sSV"])

    chk = {k: 0 for k in SEMS}
    for _en, _fn, _kw in REC:
        s = _kw.get("sem")
        if s and _fn is not _wait:
            chk[s] += 16 if _fn is _dma else 1
    chk["sGP"] += 15
    assert chk == C, f"sem count mismatch: {chk} vs {C}"

    with es:
        block = nc.Block()
        with block:
            for eng_name in ["sync", "gpsimd", "tensor", "vector", "scalar"]:
                def body(e, _n=eng_name):
                    for en, fn, kw in REC:
                        if en == _n:
                            fn(e, **kw)
                getattr(block, eng_name)(body)
    return nc


def kernel(**inputs):
    cfg = dict(DEFAULT_CFG)
    x = np.ascontiguousarray(inputs["L0_attn_x"])
    n_cores = 8
    shared = make_shared_inputs(inputs)
    nc = build_program(cfg)
    in_maps = []
    for c in range(n_cores):
        m = dict(shared)
        m["x"] = np.ascontiguousarray(x[2 * c: 2 * c + 2])
        in_maps.append(m)
    res = run_bass_kernel_spmd(nc, in_maps, core_ids=list(range(n_cores)))
    return np.concatenate([r["out"] for r in res.results], axis=0)


def make_shared_inputs(inputs):
    return dict(
        gate_wT=np.ascontiguousarray(inputs["L0_attn_gate_w"].T),
        q_wT=np.ascontiguousarray(inputs["L0_attn_q_w"].T),
        k_wT=np.ascontiguousarray(np.transpose(inputs["L0_attn_k_w"], (0, 2, 1))),
        v_wT=np.ascontiguousarray(np.transpose(inputs["L0_attn_v_w"], (0, 2, 1))),
        fmq1T=np.ascontiguousarray(inputs["L0_attn_fmq1_w"].T),
        fmq2T=np.ascontiguousarray(inputs["L0_attn_fmq2_w"].T),
        fmk1T=np.ascontiguousarray(inputs["L0_attn_fmk1_w"].T),
        fmk2T=np.ascontiguousarray(inputs["L0_attn_fmk2_w"].T),
        norm_w=np.ascontiguousarray(np.asarray(inputs["L0_attn_norm_w"]).reshape(DK, 1)),
        o_wT=np.ascontiguousarray(inputs["L0_attn_o_w"].T),
        k0=np.ascontiguousarray(inputs["L0_ffn_k0"]).view(np.uint8),
        k1=np.ascontiguousarray(inputs["L0_ffn_k1"]).view(np.uint8),
        k2=np.ascontiguousarray(inputs["L0_ffn_k2"]).view(np.uint8),
        s0=np.asarray(inputs["L0_ffn_s0"], dtype=np.float32).reshape(1, 1),
        s1=np.asarray(inputs["L0_ffn_s1"], dtype=np.float32).reshape(1, 1),
        s2=np.asarray(inputs["L0_ffn_s2"], dtype=np.float32).reshape(1, 1),
    )


# version 14
# speedup vs baseline: 1.1717x; 1.0005x over previous
import numpy as np
import ml_dtypes
from contextlib import ExitStack
import concourse.bass as bass
import concourse.mybir as mybir
from concourse.bass_utils import run_bass_kernel_spmd

F32 = mybir.dt.float32
BF16 = mybir.dt.bfloat16
U8 = mybir.dt.uint8
AF = mybir.ActivationFunctionType
ALU = mybir.AluOpType
AX = mybir.AxisListType

D = 1024
E_EXP = 8
H = 8
DK = 128
SEG = 512
NTC = SEG // 128
CG = D // 128
S_ATTN = float(DK) ** -0.5
F1 = D * 4

DEFAULT_CFG = dict(nbatch=2, segs_per_batch=8, nexp=8, debug=False)


def build_program(cfg):
    nb = cfg["nbatch"]
    spb = cfg["segs_per_batch"]
    nexp = cfg["nexp"]
    seqlen = spb * SEG
    nseg = nb * spb

    nc = bass.Bass("TRN2", target_bir_lowering=False)

    x_d = nc.dram_tensor("x", [nb, seqlen, D], F32, kind="ExternalInput").ap()
    gate_d = nc.dram_tensor("gate_wT", [D, E_EXP], F32, kind="ExternalInput").ap()
    qwT_d = nc.dram_tensor("q_wT", [D, D], F32, kind="ExternalInput").ap()
    kwT_d = nc.dram_tensor("k_wT", [E_EXP, D, D], F32, kind="ExternalInput").ap()
    vwT_d = nc.dram_tensor("v_wT", [E_EXP, D, D], F32, kind="ExternalInput").ap()
    fm_d = [nc.dram_tensor(n, [DK, DK], F32, kind="ExternalInput").ap()
            for n in ["fmq1T", "fmq2T", "fmk1T", "fmk2T"]]
    nw_d = nc.dram_tensor("norm_w", [DK, 1], F32, kind="ExternalInput").ap()
    owT_d = nc.dram_tensor("o_wT", [D, D], F32, kind="ExternalInput").ap()
    k0_d = nc.dram_tensor("k0", [2 * D, F1], U8, kind="ExternalInput").ap()
    k1_d = nc.dram_tensor("k1", [F1, F1], U8, kind="ExternalInput").ap()
    k2_d = nc.dram_tensor("k2", [F1, D], U8, kind="ExternalInput").ap()
    s0_d = nc.dram_tensor("s0", [1, 1], F32, kind="ExternalInput").ap()
    s1_d = nc.dram_tensor("s1", [1, 1], F32, kind="ExternalInput").ap()
    s2_d = nc.dram_tensor("s2", [1, 1], F32, kind="ExternalInput").ap()
    out_d = nc.dram_tensor("out", [nb, seqlen, D], F32, kind="ExternalOutput").ap()

    kv_scr = nc.dram_tensor("kv_scr", [E_EXP, 2, D, D], BF16, kind="Internal").ap()
    w0_scr = nc.dram_tensor("w0_scr", [D, F1], BF16, kind="Internal").ap()
    w1_scr = nc.dram_tensor("w1_scr", [F1 // 128, 128, F1], BF16, kind="Internal").ap()
    w2_scr = nc.dram_tensor("w2_scr", [F1, D], BF16, kind="Internal").ap()
    st_scr = nc.dram_tensor("st_scr", [E_EXP, 128, H, DK], F32, kind="Internal").ap()
    qw_scr = nc.dram_tensor("qw_scr", [D, D], BF16, kind="Internal").ap()
    ow_scr = nc.dram_tensor("ow_scr", [D, D], BF16, kind="Internal").ap()

    dbg = {}
    if cfg["debug"]:
        for n, shape, dt in [
            ("xn_fm", [128, CG, SEG], BF16), ("wgt01", [128, NTC, E_EXP], F32),
            ("mask01", [128, NTC, E_EXP], F32), ("Qp", [128, H, SEG], BF16),
            ("Kp", [128, H, SEG], BF16), ("Kptm", [128, NTC, H, DK], BF16),
            ("Vp", [128, NTC, H, DK], BF16),
            ("Oacc", [128, NTC, D], F32), ("H1", [128, F1 // 128, SEG], BF16),
            ("H2", [128, F1 // 128, SEG], BF16), ("Onrmfm", [128, CG, SEG], BF16),
        ]:
            dbg[n] = nc.dram_tensor("dbg_" + n, shape, dt, kind="ExternalOutput").ap()

    es = ExitStack()
    SEMS = {}
    for s in ["sX", "sKV", "sQW", "sOW", "sST", "sW0", "sW1A", "sW1B", "sW2", "sW2B",
              "sP", "sOUT", "sSV", "sPS", "sG1", "sG2", "sPE", "sVE", "sSC", "sGP"]:
        SEMS[s] = es.enter_context(nc.semaphore(s))
    C = {k: 0 for k in SEMS}

    def adv(sem, k):
        C[sem] += k
        return C[sem]

    def sb(name, shape, dt):
        return es.enter_context(nc.sbuf_tensor(name, shape, dt))

    gate_sb = sb("gate_sb", [128, CG, E_EXP], BF16)
    fm_sb = sb("fm_sb", [128, 4, DK], BF16)
    identb = sb("identb", [128, 128], BF16)
    identf = sb("identf", [128, 128], F32)
    tri_s = sb("tri_s", [128, 128], F32)
    tri_b = sb("tri_b", [128, 128], BF16)
    onesr = sb("onesr", [1, 128], F32)
    scol = sb("scol", [128, 4], F32)
    nw_sb = sb("nw_sb", [128, 1], F32)
    ssc_sb = sb("ssc_sb", [1, 4], F32)
    S_mast = sb("S_mast", [128, H, DK], F32)
    S_bf = sb("S_bf", [128, H, DK], BF16)

    x_seg = sb("x_seg", [128, NTC, D], F32)
    xn_fm = sb("xn_fm", [128, CG, SEG], BF16)
    Qp_fm = sb("Qp_fm", [128, H, SEG], BF16)
    wgt01 = sb("wgt01", [128, NTC, E_EXP], F32)
    mask01 = sb("mask01", [128, NTC, E_EXP], F32)
    O_acc = sb("O_acc", [128, NTC, D], F32)
    Onrm_fm = sb("Onrm_fm", [128, CG, SEG], BF16)
    xn_tmp = sb("xn_tmp", [128, 2, D], BF16)
    Onrm = xn_tmp
    tmp_sq = sb("tmp_sq", [128, D], F32)
    rt = sb("rt", [128, 24], F32)
    rt3a = sb("rt3a", [128, NTC, E_EXP], F32)
    rt3b = sb("rt3b", [128, NTC, E_EXP], F32)
    rt3c = sb("rt3c", [128, NTC, E_EXP], F32)
    rms_s = sb("rms_s", [128, NTC, H], F32)
    rms_r = sb("rms_r", [128, NTC, H], F32)
    cst = sb("cst", [128, 6], F32)
    _cvals = [1e-5, -0.5, 1.0 / DK, S_ATTN, 1.0, 0.0]
    for _ci, _cv in enumerate(_cvals):
        nc.const_aps.register(F32, _cv, cst[:, _ci:_ci + 1])

    ARENA_BYTES = 124928
    arena = sb("arena", [128, ARENA_BYTES // 2], BF16)

    def carve(off_bytes, shape, dt):
        nelem = 1
        for s in shape[1:]:
            nelem *= s
        esz = {BF16: 2, F32: 4, U8: 1}[dt]
        base = arena[:].bitcast(dt) if dt != BF16 else arena[:]
        v = base[:, off_bytes // esz: off_bytes // esz + nelem]
        if len(shape) > 2:
            dims = "".join(chr(ord('a') + i) for i in range(len(shape) - 1))
            v = v.rearrange(f"p ({' '.join(dims)}) -> p {' '.join(dims)}",
                            **{chr(ord('a') + i): shape[1 + i] for i in range(len(shape) - 1)})
        return v

    kvw_sb = carve(0, [128, CG, D], BF16)
    Khat2 = carve(16384, [128, 3, SEG], BF16)
    Kp_fm = carve(19456, [128, H, SEG], BF16)
    Kp_tm = carve(27648, [128, NTC, H, DK], BF16)
    Vp_tm = carve(35840, [128, NTC, H, DK], BF16)
    aT_big = carve(44032, [128, 2, 1280], BF16)
    A1s = carve(49152, [128, SEG], BF16)
    A1s2 = carve(50176, [128, SEG], BF16)
    H1 = carve(51200, [128, F1 // 128, SEG], BF16)
    H2 = carve(83968, [128, F1 // 128, SEG], BF16)
    w01_sb = carve(116736, [128, CG, 512], BF16)
    w1v_sb = carve(116736, [128, F1 // 128, 128], BF16)
    owT_sb = carve(0, [128, CG, D], BF16)
    out_sb = carve(16384, [128, 2, D], F32)
    w2_sb = carve(27648, [128, 2, 4, D], BF16)
    pstg_a = carve(0, [128, F1], U8)
    pstg_b = carve(4096, [128, F1], U8)
    pstg_f = carve(8192, [128, 512], F32)
    pstg_o = carve(10240, [128, F1], BF16)

    pALL = es.enter_context(nc.psum_tensor("pALL", [128, 4096], F32))

    def bank(i, lo=0, n=512):
        return pALL[:, i * 512 + lo: i * 512 + lo + n]

    def pflat(lo, n):
        return pALL[:, lo: lo + n]

    def bankbf(i, lo=0, n=1024):
        return pALL[:].bitcast(BF16)[:, i * 1024 + lo: i * 1024 + lo + n]

    def pbf(off, n):
        return pALL[:].bitcast(BF16)[:, off: off + n]

    REC = []

    def rec(eng, fn, **kw):
        REC.append((eng, fn, kw))

    def _dma(e, out=None, in_=None, sem=None):
        e.dma_start(out=out, in_=in_).then_inc(SEMS[sem], 16)

    def _wait(e, sem=None, val=0):
        e.wait_ge(SEMS[sem], val)

    def _mm(e, out=None, lhsT=None, rhs=None, start=True, stop=True, sem=None):
        m = e.matmul(out, lhsT, rhs, start=start, stop=stop)
        if sem:
            m.then_inc(SEMS[sem], 1)

    def _tr(e, out=None, in_=None, ident=None, sem=None):
        m = e.transpose(out, in_, ident)
        if sem:
            m.then_inc(SEMS[sem], 1)

    def _cp(e, out=None, in_=None, sem=None):
        m = e.tensor_copy(out=out, in_=in_)
        if sem:
            m.then_inc(SEMS[sem], 1)

    def _scp(e, out=None, in_=None, sem=None):
        m = e.copy(out=out, in_=in_)
        if sem:
            m.then_inc(SEMS[sem], 1)

    def _tt(e, out=None, in0=None, in1=None, op=None, sem=None):
        m = e.tensor_tensor(out=out, in0=in0, in1=in1, op=op)
        if sem:
            m.then_inc(SEMS[sem], 1)

    def _ts(e, out=None, in0=None, s1=None, s2=None, op0=None, op1=None, sem=None):
        kw = {}
        if op1 is not None:
            kw["op1"] = op1
        m = e.tensor_scalar(out=out, in0=in0, scalar1=s1, scalar2=s2, op0=op0, **kw)
        if sem:
            m.then_inc(SEMS[sem], 1)

    def _stt(e, out=None, in0=None, scalar=None, in1=None, op0=None, op1=None, sem=None):
        m = e.scalar_tensor_tensor(out=out, in0=in0, scalar=scalar, in1=in1, op0=op0, op1=op1)
        if sem:
            m.then_inc(SEMS[sem], 1)

    def _red(e, out=None, in_=None, op=None, sem=None):
        m = e.tensor_reduce(out=out, in_=in_, axis=AX.X, op=op)
        if sem:
            m.then_inc(SEMS[sem], 1)

    def _act(e, out=None, in_=None, func=None, bias=0.0, scale=1.0, sem=None):
        m = e.activation(out=out, in_=in_, func=func, bias=bias, scale=scale)
        if sem:
            m.then_inc(SEMS[sem], 1)

    def _mul(e, out=None, in_=None, mul=None, sem=None):
        m = e.mul(out=out, in_=in_, mul=mul)
        if sem:
            m.then_inc(SEMS[sem], 1)

    def _recip(e, out=None, in_=None, sem=None):
        m = e.reciprocal(out=out, in_=in_)
        if sem:
            m.then_inc(SEMS[sem], 1)

    def prep_consts(e):
        e.memset(identb[:], 0.0).then_inc(SEMS["sGP"], 1)
        e.memset(identf[:], 0.0).then_inc(SEMS["sGP"], 1)
        e.memset(tri_s[:], 1.0).then_inc(SEMS["sGP"], 1)
        e.memset(tri_b[:], 1.0).then_inc(SEMS["sGP"], 1)
        e.memset(onesr[:], 1.0).then_inc(SEMS["sGP"], 1)
        for _ci, _cv in enumerate([1e-5, -0.5, 1.0 / DK, S_ATTN, 1.0, 0.0]):
            e.memset(cst[:, _ci:_ci + 1], _cv).then_inc(SEMS["sGP"], 1)
        e.wait_ge(SEMS["sGP"], 11)
        e.affine_select(out=identb[:], in_=identb[:], compare_op=ALU.not_equal,
                        fill=1.0, base=0, pattern=[[-1, 128]],
                        channel_multiplier=1).then_inc(SEMS["sGP"], 1)
        e.affine_select(out=identf[:], in_=identf[:], compare_op=ALU.not_equal,
                        fill=1.0, base=0, pattern=[[-1, 128]],
                        channel_multiplier=1).then_inc(SEMS["sGP"], 1)
        e.affine_select(out=tri_s[:], in_=tri_s[:], compare_op=ALU.is_ge,
                        fill=0.0, base=0, pattern=[[1, 128]],
                        channel_multiplier=-1).then_inc(SEMS["sGP"], 1)
        e.affine_select(out=tri_b[:], in_=tri_b[:], compare_op=ALU.is_ge,
                        fill=0.0, base=0, pattern=[[1, 128]],
                        channel_multiplier=-1).then_inc(SEMS["sGP"], 1)
    rec("gpsimd", prep_consts)
    v_consts = adv("sGP", 15)

    small_loads = [
        (gate_sb[:], gate_d.rearrange("(cg p) e -> p cg e", p=128)),
    ]
    for i in range(4):
        small_loads.append((fm_sb[:, i, :], fm_d[i]))
    for dst, srcap in small_loads:
        rec("gpsimd", _dma, out=dst, in_=srcap, sem="sG1")
    v_small = adv("sG1", 16 * len(small_loads))

    rec("sync", _dma, out=nw_sb[:], in_=nw_d, sem="sP")
    rec("sync", _dma, out=ssc_sb[:, 0:1], in_=s0_d, sem="sP")
    rec("sync", _dma, out=ssc_sb[:, 1:2], in_=s1_d, sem="sP")
    rec("sync", _dma, out=ssc_sb[:, 2:3], in_=s2_d, sem="sP")
    v_scal = adv("sP", 64)
    rec("sync", _wait, sem="sP", val=v_scal)

    for e_i in range(nexp):
        rec("gpsimd", _dma, out=kv_scr[e_i, 0], in_=kwT_d[e_i], sem="sG2")
        rec("gpsimd", _dma, out=kv_scr[e_i, 1], in_=vwT_d[e_i], sem="sG2")
    rec("gpsimd", _dma, out=qw_scr, in_=qwT_d, sem="sG2")
    v_kvscr = adv("sG2", 16 * (2 * nexp + 1))

    rec("tensor", _wait, sem="sP", val=v_scal)
    rec("tensor", _wait, sem="sGP", val=v_consts)
    rec("tensor", _mm, out=bank(3, 0, 3), lhsT=onesr[:], rhs=ssc_sb[:, 0:3],
        start=True, stop=True, sem="sPE")
    v_scolmm = adv("sPE", 1)
    rec("vector", _wait, sem="sPE", val=v_scolmm)
    rec("vector", _ts, out=scol[:, 0:3], in0=bank(3, 0, 3), s1=2.0, op0=ALU.mult, sem="sVE")
    v1 = adv("sVE", 1)
    rec("vector", _wait, sem="sVE", val=v1)
    rec("vector", _ts, out=scol[:, 2:3], in0=bank(3, 2, 1), s1=4.0, op0=ALU.mult, sem="sVE")
    rec("vector", _ts, out=scol[:, 3:4], in0=bank(3, 2, 1), s1=2.0, op0=ALU.mult, sem="sVE")
    adv("sVE", 2)
    v_scol = C["sVE"]

    rec("vector", _wait, sem="sP", val=v_scal)
    for cg in range(CG):
        for os_ in range(2):
            srcap = owT_d.rearrange("(cg p) o -> cg p o", p=128)[cg][:, os_ * 512:(os_ + 1) * 512]
            rec("sync", _dma, out=pstg_f[:], in_=srcap, sem="sP")
            vload = adv("sP", 16)
            rec("vector", _wait, sem="sP", val=vload)
            rec("vector", _ts, out=pstg_o[:, 0:512], in0=pstg_f[:],
                s1=nw_sb[:, 0:1], op0=ALU.mult, sem="sVE")
            vconv = adv("sVE", 1)
            rec("scalar", _wait, sem="sVE", val=vconv)
            rec("scalar", _dma, out=ow_scr[cg * 128:(cg + 1) * 128, os_ * 512:(os_ + 1) * 512],
                in_=pstg_o[:, 0:512], sem="sPS")
            vstore = adv("sPS", 16)
            rec("sync", _wait, sem="sPS", val=vstore)

    for cg in range(CG):
        rec("sync", _dma, out=pstg_a[:], in_=k0_d[cg * 128:(cg + 1) * 128, :], sem="sP")
        rec("sync", _dma, out=pstg_b[:], in_=k0_d[D + cg * 128: D + (cg + 1) * 128, :], sem="sP")
        vload = adv("sP", 32)
        rec("vector", _wait, sem="sP", val=vload)
        rec("vector", _tt, out=pstg_o[:], in0=pstg_a[:], in1=pstg_b[:], op=ALU.subtract, sem="sVE")
        vconv = adv("sVE", 1)
        rec("scalar", _wait, sem="sVE", val=vconv)
        rec("scalar", _dma, out=w0_scr[cg * 128:(cg + 1) * 128, :], in_=pstg_o[:], sem="sPS")
        vstore = adv("sPS", 16)
        rec("sync", _wait, sem="sPS", val=vstore)

    for g in range(F1 // 128):
        rec("sync", _dma, out=pstg_a[:], in_=k1_d[g * 128:(g + 1) * 128, :], sem="sP")
        vload = adv("sP", 16)
        rec("vector", _wait, sem="sP", val=vload)
        rec("vector", _ts, out=pstg_o[:], in0=pstg_a[:], s1=2.0, s2=-1.0,
            op0=ALU.mult, op1=ALU.add, sem="sVE")
        vconv = adv("sVE", 1)
        rec("scalar", _wait, sem="sVE", val=vconv)
        rec("scalar", _dma, out=w1_scr[g], in_=pstg_o[:], sem="sPS")
        vstore = adv("sPS", 16)
        rec("sync", _wait, sem="sPS", val=vstore)

    rec("vector", _wait, sem="sVE", val=v_scol)
    for g in range(F1 // 128):
        rec("sync", _dma, out=pstg_a[:, 0:D], in_=k2_d[g * 128:(g + 1) * 128, :], sem="sP")
        vload = adv("sP", 16)
        rec("vector", _wait, sem="sP", val=vload)
        rec("vector", _ts, out=pstg_o[:, 0:D], in0=pstg_a[:, 0:D], s1=scol[:, 2:3],
            s2=scol[:, 3:4], op0=ALU.mult, op1=ALU.subtract, sem="sVE")
        vconv = adv("sVE", 1)
        rec("scalar", _wait, sem="sVE", val=vconv)
        rec("scalar", _dma, out=w2_scr[g * 128:(g + 1) * 128, :], in_=pstg_o[:, 0:D], sem="sPS")
        vstore = adv("sPS", 16)
        rec("sync", _wait, sem="sPS", val=vstore)

    v_prep_sS = C["sPS"]
    v_prep_sVE = C["sVE"]

    for eng in ["tensor", "vector", "scalar"]:
        rec(eng, _wait, sem="sG1", val=v_small)
        rec(eng, _wait, sem="sG2", val=v_kvscr)
        rec(eng, _wait, sem="sPS", val=v_prep_sS)
        rec(eng, _wait, sem="sVE", val=v_prep_sVE)
        rec(eng, _wait, sem="sGP", val=v_consts)

    last = dict(
        kv_user=0, e3_evict=0, e_pe_done=0,
        w0_user=0, w1_user=[0, 0], w2_user=[0, 0], out_user=[0, 0], tb_free=[0, 0],
        stsave=[0] * E_EXP, c5_ve_done=0,
    )

    for seg in range(nseg):
        b = seg // spb
        sib = seg % spb
        t0 = sib * SEG
        first_seg = (sib == 0)
        last_seg = (sib == spb - 1)

        rec("sync", _wait, sem="sVE", val=last["e3_evict"])
        rec("sync", _wait, sem="sX", val=C["sX"])
        if cfg["debug"] and seg > 0:
            for eng in ["vector", "scalar", "tensor"]:
                rec(eng, _wait, sem="sOUT", val=C["sOUT"])
        for tc in range(NTC):
            rec("sync", _dma, out=x_seg[:, tc, :],
                in_=x_d[b, t0 + tc * 128: t0 + (tc + 1) * 128, :], sem="sX")
            adv("sX", 16)
        xw_total = C["sX"]
        xw = [xw_total] * NTC

        v_xn_tr = [0] * NTC
        v_xnfm = [0] * NTC
        xall = x_seg[:].rearrange("p a d -> p (a d)")
        tmp_sq4 = O_acc[:].rearrange("p a d -> p (a d)")
        rec("vector", _wait, sem="sX", val=xw[-1])
        rec("vector", _wait, sem="sVE", val=last["e3_evict"])
        rec("vector", _tt, out=tmp_sq4, in0=xall, in1=xall, op=ALU.mult, sem="sVE")
        vsq = adv("sVE", 1)
        rec("vector", _red, out=rt[:, 0:NTC], in_=x_seg[:], op=ALU.add, sem="sVE")
        adv("sVE", 1)
        rec("vector", _wait, sem="sVE", val=vsq)
        rec("vector", _red, out=rt[:, 4:4 + NTC],
            in_=tmp_sq4.rearrange("p (a d) -> p a d", a=NTC), op=ALU.add, sem="sVE")
        vred = adv("sVE", 1)
        rec("vector", _wait, sem="sVE", val=vred)
        rec("vector", _ts, out=rt[:, 0:NTC], in0=rt[:, 0:NTC], s1=1.0 / D, op0=ALU.mult, sem="sVE")
        vm_ = adv("sVE", 1)
        rec("vector", _wait, sem="sVE", val=vm_)
        rec("vector", _ts, out=rt[:, 4:4 + NTC], in0=rt[:, 4:4 + NTC], s1=1.0 / D,
            op0=ALU.mult, sem="sVE")
        vq_ = adv("sVE", 1)
        rec("vector", _wait, sem="sVE", val=vq_)
        rec("vector", _tt, out=rt[:, 8:8 + NTC], in0=rt[:, 0:NTC], in1=rt[:, 0:NTC],
            op=ALU.mult, sem="sVE")
        vmm2 = adv("sVE", 1)
        rec("vector", _wait, sem="sVE", val=vmm2)
        rec("vector", _tt, out=rt[:, 8:8 + NTC], in0=rt[:, 4:4 + NTC], in1=rt[:, 8:8 + NTC],
            op=ALU.subtract, sem="sVE")
        vvar = adv("sVE", 1)
        rec("scalar", _wait, sem="sVE", val=vvar)
        rec("scalar", _act, out=rt[:, 8:8 + NTC], in_=rt[:, 8:8 + NTC], func=AF.Ln, bias=1e-5,
            sem="sSC")
        vln = adv("sSC", 1)
        rec("scalar", _wait, sem="sSC", val=vln)
        rec("scalar", _act, out=rt[:, 12:12 + NTC], in_=rt[:, 8:8 + NTC], func=AF.Exp, scale=-0.5,
            sem="sSC")
        vrs = adv("sSC", 1)
        rec("vector", _wait, sem="sSC", val=vrs)
        rec("vector", _tt, out=rt[:, 16:16 + NTC], in0=rt[:, 0:NTC], in1=rt[:, 12:12 + NTC],
            op=ALU.mult, sem="sVE")
        vmr = adv("sVE", 1)
        rec("vector", _wait, sem="sVE", val=vmr)
        rec("vector", _ts, out=rt[:, 16:16 + NTC], in0=rt[:, 16:16 + NTC], s1=-1.0,
            op0=ALU.mult, sem="sVE")
        vnm = adv("sVE", 1)
        for tc in range(NTC):
            rec("scalar", _wait, sem="sVE", val=vnm)
            if tc > 1:
                rec("scalar", _wait, sem="sPE", val=v_xn_tr[tc - 2])
            rec("scalar", _act, out=xn_tmp[:, tc % 2, :], in_=x_seg[:, tc, :], func=AF.Identity,
                scale=rt[:, 12 + tc:13 + tc], bias=rt[:, 16 + tc:17 + tc], sem="sSC")
            vxn = adv("sSC", 1)
            rec("tensor", _wait, sem="sSC", val=vxn)
            rec("tensor", _wait, sem="sVE", val=last["tb_free"][tc % 2])
            for g in range(CG):
                rec("tensor", _tr, out=bankbf(2 + tc % 2, g * 128, 128),
                    in_=xn_tmp[:, tc % 2, g * 128:(g + 1) * 128], ident=identb[:],
                    sem="sPE" if g == CG - 1 else None)
            v_xn_tr[tc] = adv("sPE", 1)
            rec("vector", _wait, sem="sPE", val=v_xn_tr[tc])
            rec("vector", _cp,
                out=xn_fm[:, :, tc * 128:(tc + 1) * 128],
                in_=bankbf(2 + tc % 2, 0, 1024).rearrange("p (g t) -> p g t", g=CG), sem="sVE")
            v_xnfm[tc] = adv("sVE", 1)
            last["tb_free"][tc % 2] = v_xnfm[tc]

        for tc in range(NTC):
            rec("tensor", _wait, sem="sVE", val=v_xnfm[-1] if tc == 0 else v_xnfm[tc])
            for cg in range(CG):
                rec("tensor", _mm, out=bank(3, tc * E_EXP, E_EXP),
                    lhsT=xn_fm[:, cg, tc * 128:(tc + 1) * 128],
                    rhs=gate_sb[:, cg, :], start=(cg == 0), stop=(cg == CG - 1),
                    sem="sPE" if cg == CG - 1 else None)
        v_lg = adv("sPE", NTC)
        lg3 = bank(3, 0, NTC * E_EXP).rearrange("p (a e) -> p a e", a=NTC)
        rec("vector", _wait, sem="sPE", val=v_lg)
        rec("vector", _red, out=rt[:, 10:10 + NTC], in_=lg3, op=ALU.max, sem="sVE")
        vm1 = adv("sVE", 1)
        rec("vector", _wait, sem="sVE", val=vm1)
        m1b = rt[:, 10:10 + NTC].unsqueeze(2).broadcast_to([128, NTC, E_EXP])
        rec("vector", _tt, out=rt3a[:], in0=lg3, in1=m1b, op=ALU.is_equal, sem="sVE")
        veq1 = adv("sVE", 1)
        rec("vector", _wait, sem="sVE", val=veq1)
        rec("vector", _stt, out=rt3b[:], in0=rt3a[:], scalar=-1e30, in1=lg3,
            op0=ALU.mult, op1=ALU.add, sem="sVE")
        vl2 = adv("sVE", 1)
        rec("vector", _wait, sem="sVE", val=vl2)
        rec("vector", _red, out=rt[:, 14:14 + NTC], in_=rt3b[:], op=ALU.max, sem="sVE")
        vm2 = adv("sVE", 1)
        rec("vector", _wait, sem="sVE", val=vm2)
        rec("vector", _tt, out=rt[:, 18:18 + NTC], in0=rt[:, 14:14 + NTC],
            in1=rt[:, 10:10 + NTC], op=ALU.subtract, sem="sVE")
        vd = adv("sVE", 1)
        m2b = rt[:, 14:14 + NTC].unsqueeze(2).broadcast_to([128, NTC, E_EXP])
        rec("vector", _tt, out=rt3c[:], in0=rt3b[:], in1=m2b, op=ALU.is_equal, sem="sVE")
        veq2 = adv("sVE", 1)
        rec("scalar", _wait, sem="sVE", val=vd)
        rec("scalar", _act, out=rt[:, 10:10 + NTC], in_=rt[:, 18:18 + NTC], func=AF.Exp, sem="sSC")
        vex = adv("sSC", 1)
        rec("vector", _wait, sem="sSC", val=vex)
        rec("vector", _wait, sem="sVE", val=veq2)
        rec("vector", _ts, out=rt[:, 14:14 + NTC], in0=rt[:, 10:10 + NTC], s1=1.0,
            op0=ALU.add, sem="sVE")
        vp1 = adv("sVE", 1)
        rec("vector", _wait, sem="sVE", val=vp1)
        rec("vector", _recip, out=rt[:, 18:18 + NTC], in_=rt[:, 14:14 + NTC], sem="sVE")
        vw1 = adv("sVE", 1)
        rec("vector", _wait, sem="sVE", val=vw1)
        rec("vector", _tt, out=rt[:, 14:14 + NTC], in0=rt[:, 18:18 + NTC],
            in1=rt[:, 10:10 + NTC], op=ALU.mult, sem="sVE")
        vw2 = adv("sVE", 1)
        rec("vector", _wait, sem="sVE", val=max(vw2, veq2))
        w1b = rt[:, 18:18 + NTC].unsqueeze(2).broadcast_to([128, NTC, E_EXP])
        w2b = rt[:, 14:14 + NTC].unsqueeze(2).broadcast_to([128, NTC, E_EXP])
        rec("vector", _tt, out=mask01[:], in0=rt3a[:], in1=rt3c[:], op=ALU.add, sem="sVE")
        adv("sVE", 1)
        rec("vector", _tt, out=rt3b[:], in0=rt3a[:], in1=w1b, op=ALU.mult, sem="sVE")
        vt1 = adv("sVE", 1)
        rec("vector", _wait, sem="sVE", val=vt1)
        rec("vector", _tt, out=rt3c[:], in0=rt3c[:], in1=w2b, op=ALU.mult, sem="sVE")
        vt2 = adv("sVE", 1)
        rec("vector", _wait, sem="sVE", val=vt2)
        rec("vector", _tt, out=wgt01[:], in0=rt3b[:], in1=rt3c[:], op=ALU.add, sem="sVE")
        v_route = adv("sVE", 1)

        rec("sync", _wait, sem="sG2", val=v_kvscr)
        rec("sync", _wait, sem="sPE", val=max(last["e_pe_done"], last["kv_user"]))
        rec("sync", _wait, sem="sOW", val=C["sOW"])
        if cfg["debug"]:
            rec("sync", _wait, sem="sOUT", val=C["sOUT"])
        rec("sync", _dma, out=kvw_sb[:],
            in_=qw_scr.rearrange("(cg p) o -> p cg o", p=128), sem="sQW")
        v_qw = adv("sQW", 16)
        v_qh = [0] * H
        v_qe = [0] * H
        v_fm = [0] * H
        v_qp = [0] * H
        v_ae = [0] * H

        def b_qhat(h):
            kb = 0 if h % 2 == 0 else 4
            rec("tensor", _wait, sem="sVE", val=v_xnfm[-1])
            rec("tensor", _wait, sem="sQW", val=v_qw)
            if h >= 2:
                rec("tensor", _wait, sem="sSC", val=v_qe[h - 2])
            for cg in range(CG):
                rec("tensor", _mm, out=bank(kb), lhsT=kvw_sb[:, cg, h * 128:(h + 1) * 128],
                    rhs=xn_fm[:, cg, :], start=(cg == 0), stop=(cg == CG - 1),
                    sem="sPE" if cg == CG - 1 else None)
            v_qh[h] = adv("sPE", 1)

        def b_ke(h):
            kb = 0 if h % 2 == 0 else 4
            rec("scalar", _wait, sem="sPE", val=v_qh[h])
            if h == 0:
                rec("scalar", _wait, sem="sOUT", val=C["sOUT"])
            rec("scalar", _scp, out=Khat2[:, h % 2, :], in_=bank(kb), sem="sSC")
            v_qe[h] = adv("sSC", 1)

        def b_fm(h):
            rec("tensor", _wait, sem="sSC", val=v_qe[h])
            if h >= 1:
                rec("tensor", _wait, sem="sSC", val=v_ae[h - 1])
                rec("tensor", _wait, sem="sVE", val=v_qp[h - 1])
            rec("tensor", _mm, out=bank(1), lhsT=fm_sb[:, 0, :], rhs=Khat2[:, h % 2, :],
                start=True, stop=True)
            rec("tensor", _mm, out=bank(5), lhsT=fm_sb[:, 1, :], rhs=Khat2[:, h % 2, :],
                start=True, stop=True, sem="sPE")
            v_fm[h] = adv("sPE", 1)

        def b_fme(h):
            rec("scalar", _wait, sem="sPE", val=v_fm[h])
            if h >= 1:
                rec("scalar", _wait, sem="sVE", val=v_qp[h - 1])
            rec("scalar", _mul, out=A1s[:], in_=bank(1), mul=S_ATTN, sem="sSC")
            v_ae[h] = adv("sSC", 1)
            rec("vector", _wait, sem="sSC", val=v_ae[h])
            rec("vector", _tt, out=Qp_fm[:, h, :], in0=A1s[:], in1=bank(5), op=ALU.mult, sem="sVE")
            v_qp[h] = adv("sVE", 1)

        v_h1 = []
        e1 = dict(next=0, w0=0)
        E1BANKS = [2, 3, 6, 7]

        def emit_e1_group():
            og_abs = e1["next"]
            if og_abs >= F1 // 128:
                return
            e1["next"] += 1
            if og_abs % 4 == 0:
                os_ = og_abs // 4
                rec("sync", _wait, sem="sPE", val=max(last["w0_user"], last["e_pe_done"]))
                rec("sync", _dma, out=w01_sb[:],
                    in_=w0_scr[:, os_ * 512:(os_ + 1) * 512].rearrange(
                        "(cg p) o -> p cg o", p=128), sem="sW0")
                e1["w0"] = adv("sW0", 16)
            bk = E1BANKS[og_abs % 4]
            rec("tensor", _wait, sem="sW0", val=e1["w0"])
            if og_abs == 0:
                rec("tensor", _wait, sem="sVE", val=v_route)
            if len(v_h1) >= 4:
                rec("tensor", _wait, sem="sSC", val=v_h1[-4])
            for cg in range(CG):
                rec("tensor", _mm, out=bank(bk),
                    lhsT=w01_sb[:, cg, (og_abs % 4) * 128:(og_abs % 4) * 128 + 128],
                    rhs=xn_fm[:, cg, :], start=(cg == 0), stop=(cg == CG - 1),
                    sem="sPE" if cg == CG - 1 else None)
            vm = adv("sPE", 1)
            last["w0_user"] = vm
            rec("scalar", _wait, sem="sPE", val=vm)
            rec("scalar", _act, out=H1[:, og_abs, :], in_=bank(bk), func=AF.Relu,
                scale=scol[:, 0:1], sem="sSC")
            v_h1.append(adv("sSC", 1))

        b_qhat(0); b_ke(0); b_qhat(1); b_ke(1)
        for h in range(H):
            b_fm(h)
            if h + 2 < H:
                b_qhat(h + 2)
            b_fme(h)
            if h + 2 < H:
                b_ke(h + 2)
            for _ in range(4):
                emit_e1_group()
        while e1["next"] < F1 // 128:
            emit_e1_group()
        v_h1_done = v_h1[-1]
        v_qp_done = v_qp[-1]
        last["kv_user"] = max(last["kv_user"], v_fm[-1])

        fil = dict(ring=0, cons=[0, 0], vcons=[0, 0], next_h2=0, v_h2_last=0,
                   w1_user=last["w0_user"], slot=0, nslots=nexp * 10 + NTC)

        def maybe_fill():
            fil["slot"] += 1
            target = (F1 // 128) * fil["slot"] // fil["nslots"]
            while fil["next_h2"] < target:
                emit_h2_slice()

        def emit_h2_slice():
            o2g_abs = fil["next_h2"]
            if o2g_abs >= F1 // 128:
                return
            fil["next_h2"] += 1
            off = 3584
            rec("sync", _wait, sem="sPE", val=fil["w1_user"])
            rec("sync", _dma, out=w1v_sb[:],
                in_=w1_scr[:, :, o2g_abs * 128:(o2g_abs + 1) * 128].rearrange("g p o -> p g o"),
                sem="sW1A")
            v_w1 = adv("sW1A", 16)
            rec("tensor", _wait, sem="sW1A", val=v_w1)
            rec("tensor", _wait, sem="sSC", val=max(v_h1_done, fil["cons"][0]))
            rec("tensor", _wait, sem="sVE", val=fil["vcons"][0])
            for o1g in range(F1 // 128):
                rec("tensor", _mm, out=pflat(off, 512),
                    lhsT=w1v_sb[:, o1g, :],
                    rhs=H1[:, o1g, :], start=(o1g == 0), stop=(o1g == F1 // 128 - 1),
                    sem="sPE" if o1g == F1 // 128 - 1 else None)
            vm = adv("sPE", 1)
            fil["w1_user"] = vm
            rec("scalar", _wait, sem="sPE", val=vm)
            rec("scalar", _act, out=H2[:, o2g_abs, :], in_=pflat(off, 512), func=AF.Relu,
                scale=scol[:, 1:2], sem="sSC")
            fil["cons"][0] = adv("sSC", 1)
            fil["v_h2_last"] = fil["cons"][0]

        for ei in range(nexp):
            rec("sync", _wait, sem="sG2", val=v_kvscr)
            rec("sync", _wait, sem="sPE", val=max(last["kv_user"], last["e_pe_done"]))
            rec("sync", _dma, out=kvw_sb[:],
                in_=kv_scr[ei, 0].rearrange("(cg p) o -> p cg o", p=128), sem="sKV")
            v_kv = adv("sKV", 16)
            kw_v = kvw_sb
            vw_v = kvw_sb


            v_kh = [0] * H
            v_ke = [0] * H
            v_fmk = [0] * H
            v_kph = [0] * H
            v_aek = [0] * H
            rec("tensor", _wait, sem="sVE", val=max(v_qp_done, last["c5_ve_done"]))
            rec("tensor", _wait, sem="sSC", val=v_h1_done)

            def c_khat(h):
                kb = [0, 4, 6][h % 3]
                rec("tensor", _wait, sem="sKV", val=v_kv)
                if h >= 3:
                    rec("tensor", _wait, sem="sSC", val=v_ke[h - 3])
                for cg in range(CG):
                    rec("tensor", _mm, out=bank(kb), lhsT=kw_v[:, cg, h * 128:(h + 1) * 128],
                        rhs=xn_fm[:, cg, :], start=(cg == 0), stop=(cg == CG - 1),
                        sem="sPE" if cg == CG - 1 else None)
                v_kh[h] = adv("sPE", 1)

            def c_ke(h):
                kb = [0, 4, 6][h % 3]
                rec("scalar", _wait, sem="sPE", val=v_kh[h])
                rec("scalar", _scp, out=Khat2[:, h % 3, :], in_=bank(kb), sem="sSC")
                v_ke[h] = adv("sSC", 1)

            def c_fm(h):
                rec("tensor", _wait, sem="sSC", val=v_ke[h])
                if h >= 1:
                    rec("tensor", _wait, sem="sSC", val=v_aek[h - 1])
                    rec("tensor", _wait, sem="sVE", val=v_kph[h - 1])
                rec("tensor", _mm, out=bank(1), lhsT=fm_sb[:, 2, :], rhs=Khat2[:, h % 3, :],
                    start=True, stop=True)
                rec("tensor", _mm, out=bank(5), lhsT=fm_sb[:, 3, :], rhs=Khat2[:, h % 3, :],
                    start=True, stop=True)
                for tc in range(NTC):
                    rec("tensor", _mm, out=bank(2, tc * 128, 128),
                        lhsT=Khat2[:, h % 3, tc * 128:(tc + 1) * 128], rhs=fm_sb[:, 2, :],
                        start=True, stop=True)
                for tc in range(NTC):
                    rec("tensor", _mm, out=bank(3, tc * 128, 128),
                        lhsT=Khat2[:, h % 3, tc * 128:(tc + 1) * 128], rhs=fm_sb[:, 3, :],
                        start=True, stop=True, sem="sPE" if tc == NTC - 1 else None)
                v_fmk[h] = adv("sPE", 1)

            def c_fme(h):
                rec("scalar", _wait, sem="sPE", val=v_fmk[h])
                if h >= 1:
                    rec("scalar", _wait, sem="sVE", val=v_kph[h - 1])
                rec("scalar", _scp, out=A1s[:], in_=bank(1), sem="sSC")
                rec("scalar", _scp, out=A1s2[:], in_=bank(2), sem="sSC")
                adv("sSC", 1)
                v_aek[h] = adv("sSC", 1)
                rec("vector", _wait, sem="sSC", val=v_aek[h])
                rec("vector", _tt, out=Kp_fm[:, h, :], in0=A1s[:], in1=bank(5), op=ALU.mult,
                    sem="sVE")
                rec("vector", _tt, out=Kp_tm[:, :, h, :],
                    in0=A1s2[:].rearrange("p (a d) -> p a d", a=NTC),
                    in1=bank(3).rearrange("p (a d) -> p a d", a=NTC), op=ALU.mult, sem="sVE")
                adv("sVE", 1)
                v_kph[h] = adv("sVE", 1)

            c_khat(0); c_ke(0); c_khat(1); c_ke(1); c_khat(2); c_ke(2)
            for h in range(H):
                c_fm(h)
                if h + 3 < H:
                    c_khat(h + 3)
                c_fme(h)
                if h + 3 < H:
                    c_ke(h + 3)
                if h % 2 == 1:
                    maybe_fill()

            rec("sync", _wait, sem="sPE", val=v_kh[H - 1])
            rec("sync", _dma, out=kvw_sb[:],
                in_=kv_scr[ei, 1].rearrange("(cg p) o -> p cg o", p=128), sem="sKV")
            v_kv2 = adv("sKV", 16)

            rec("vector", _wait, sem="sVE", val=v_route)
            v_vp = []
            v_vp_pe = 0
            for tc in range(NTC):
                for vs in range(2):
                    bk = [0, 1, 5][(tc * 2 + vs) % 3]
                    if tc == 0 and vs == 0:
                        rec("tensor", _wait, sem="sKV", val=v_kv2)
                    if len(v_vp) < 3:
                        rec("tensor", _wait, sem="sVE", val=v_kph[-1])
                    if len(v_vp) >= 3:
                        rec("tensor", _wait, sem="sVE", val=v_vp[-3])
                    for cg in range(CG):
                        rec("tensor", _mm, out=bank(bk),
                            lhsT=xn_fm[:, cg, tc * 128:(tc + 1) * 128],
                            rhs=vw_v[:, cg, vs * 512:(vs + 1) * 512],
                            start=(cg == 0), stop=(cg == CG - 1),
                            sem="sPE" if cg == CG - 1 else None)
                    vvh = adv("sPE", 1)
                    v_vp_pe = vvh
                    rec("vector", _wait, sem="sPE", val=vvh)
                    rec("vector", _ts, out=Vp_tm[:, tc, vs * 4:(vs + 1) * 4, :],
                        in0=bank(bk).rearrange("p (a d) -> p a d", a=4),
                        s1=mask01[:, tc, ei:ei + 1], op0=ALU.mult, sem="sVE")
                    v_vp.append(adv("sVE", 1))
                if tc % 2 == 1:
                    maybe_fill()
            v_vp_done = v_vp[-1]
            last["kv_user"] = v_vp_pe

            has_state = not first_seg
            rec("tensor", _wait, sem="sVE", val=v_vp_done)
            rec("tensor", _wait, sem="sSC", val=C["sSC"])
            v_sbfcopy = C["sSC"]
            if not first_seg:
                rec("sync", _wait, sem="sSV", val=C["sSV"])
                rec("sync", _wait, sem="sSC", val=C["sSC"])
                rec("sync", _wait, sem="sVE", val=C["sVE"])
                rec("sync", _dma, out=S_mast[:], in_=st_scr[ei], sem="sST")
                v_strest = adv("sST", 16)
                rec("scalar", _wait, sem="sST", val=v_strest)
                rec("scalar", _wait, sem="sPE", val=v_vp_pe)
                rec("scalar", _scp, out=S_bf[:].rearrange("p h v -> p (h v)"),
                    in_=S_mast[:].rearrange("p h v -> p (h v)"), sem="sSC")
                v_sbfcopy = adv("sSC", 1)

            p_even = [0, 512, 1024, 896]
            p_odd = [1536, 2048, 1280, 2432]
            a_off = [0, 512, 896, 1152]
            slot_n = [512, 384, 256, 128]
            v_aT = [0] * H
            v_omm = [0] * H
            v_dtt = [0] * H
            v_oc = [0] * H
            v_oev = [0] * H
            v_dadd = [0] * H

            def c5_delta(j):
                off = 3584
                rec("tensor", _wait, sem="sSC", val=fil["cons"][0])
                rec("tensor", _wait, sem="sVE", val=fil["vcons"][0])
                for jc in range(NTC):
                    rec("tensor", _mm, out=pflat(off, 128),
                        lhsT=Kp_tm[:, jc, j, :], rhs=Vp_tm[:, jc, j, :],
                        start=(jc == 0), stop=(jc == NTC - 1),
                        sem="sPE" if jc == NTC - 1 else None)
                vm = adv("sPE", 1)
                rec("vector", _wait, sem="sPE", val=vm)
                if j == 0:
                    rec("vector", _wait, sem="sSC", val=v_sbfcopy)
                    if first_seg:
                        rec("vector", _wait, sem="sSV", val=C["sSV"])
                if first_seg:
                    rec("vector", _cp, out=S_mast[:, j, :], in_=pflat(off, 128), sem="sVE")
                else:
                    rec("vector", _stt, out=S_mast[:, j, :], in0=pflat(off, 128),
                        scalar=1.0, in1=S_mast[:, j, :], op0=ALU.mult, op1=ALU.add, sem="sVE")
                fil["vcons"][0] = adv("sVE", 1)
                v_dadd[j] = fil["vcons"][0]

            def c5_av(j):
                par = j % 2
                o_off = 2560 if par == 0 else 3072
                rec("tensor", _wait, sem="sVE", val=v_dtt[j])
                rec("tensor", _wait, sem="sSC", val=v_oc[j])
                if j >= 2:
                    rec("tensor", _wait, sem="sVE", val=v_oev[j - 2])
                if has_state and j == 0:
                    rec("tensor", _wait, sem="sSC", val=v_sbfcopy)
                for il in range(NTC):
                    nmm = il + 1 + (1 if has_state else 0)
                    k = 0
                    for jc in range(il + 1):
                        k += 1
                        rec("tensor", _mm, out=pflat(o_off + il * 128, 128),
                            lhsT=aT_big[:, par, a_off[jc] + (il - jc) * 128:
                                        a_off[jc] + (il - jc) * 128 + 128],
                            rhs=Vp_tm[:, jc, j, :], start=(k == 1), stop=(k == nmm),
                            sem="sPE" if (il == NTC - 1 and k == nmm) else None)
                    if has_state:
                        k += 1
                        rec("tensor", _mm, out=pflat(o_off + il * 128, 128),
                            lhsT=Qp_fm[:, j, il * 128:(il + 1) * 128], rhs=S_bf[:, j, :],
                            start=False, stop=True,
                            sem="sPE" if (il == NTC - 1) else None)
                v_omm[j] = adv("sPE", 1)
                rec("vector", _wait, sem="sPE", val=v_omm[j])
                for il in range(NTC):
                    wcol = wgt01[:, il, ei:ei + 1]
                    if ei == 0:
                        rec("vector", _ts, out=O_acc[:, il, j * 128:(j + 1) * 128],
                            in0=pflat(o_off + il * 128, 128), s1=wcol, op0=ALU.mult, sem="sVE")
                    else:
                        rec("vector", _stt, out=O_acc[:, il, j * 128:(j + 1) * 128],
                            in0=pflat(o_off + il * 128, 128), scalar=wcol,
                            in1=O_acc[:, il, j * 128:(j + 1) * 128],
                            op0=ALU.mult, op1=ALU.add, sem="sVE")
                adv("sVE", 3)
                v_oev[j] = adv("sVE", 1)

            for h in range(H):
                par = h % 2
                poff = p_even if par == 0 else p_odd
                if h >= 2:
                    rec("tensor", _wait, sem="sVE", val=v_dtt[h - 2])
                    rec("tensor", _wait, sem="sSC", val=v_oc[h - 2])
                for jc in (0, 1, 3):
                    rec("tensor", _mm, out=pflat(poff[jc], slot_n[jc]),
                        lhsT=Kp_fm[:, h, jc * 128:(jc + 1) * 128],
                        rhs=Qp_fm[:, h, jc * 128:512], start=True, stop=True)
                c5_delta(h)
                if h >= 1:
                    rec("tensor", _wait, sem="sVE", val=v_dtt[h - 1])
                    rec("tensor", _wait, sem="sSC", val=v_oc[h - 1])
                rec("tensor", _mm, out=pflat(poff[2], slot_n[2]),
                    lhsT=Kp_fm[:, h, 256:384], rhs=Qp_fm[:, h, 256:512],
                    start=True, stop=True, sem="sPE")
                v_aT[h] = adv("sPE", 1)
                rec("vector", _wait, sem="sPE", val=v_aT[h])
                for jc in range(NTC):
                    rec("vector", _tt, out=aT_big[:, par, a_off[jc]: a_off[jc] + 128],
                        in0=pflat(poff[jc], 128), in1=tri_s[:], op=ALU.mult,
                        sem="sVE")
                adv("sVE", 3)
                v_dtt[h] = adv("sVE", 1)
                rec("scalar", _wait, sem="sPE", val=v_aT[h])
                rec("scalar", _wait, sem="sVE", val=v_dtt[h])
                for jc, w in ((0, 384), (1, 256), (2, 128)):
                    rec("scalar", _scp,
                        out=aT_big[:, par, a_off[jc] + 128: a_off[jc] + 128 + w],
                        in_=pflat(poff[jc] + 128, w), sem="sSC")
                adv("sSC", 2)
                v_oc[h] = adv("sSC", 1)

                if h >= 1:
                    c5_av(h - 1)
                if h % 2 == 1:
                    maybe_fill()
            c5_av(H - 1)

            if not last_seg:
                rec("scalar", _wait, sem="sVE", val=v_dadd[H - 1])
                rec("scalar", _dma, out=st_scr[ei], in_=S_mast[:], sem="sSV")
                last["stsave"][ei] = adv("sSV", 16)
            last["c5_ve_done"] = C["sVE"]

        c_pe_done = C["sPE"]

        v_onrm_tr = [0] * NTC
        rec("vector", _wait, sem="sVE", val=v_oev[H - 1])
        vsq_d = 0
        for tc in range(NTC):
            o3 = O_acc[:, tc, :].rearrange("p (h v) -> p h v", h=H)
            if vsq_d:
                rec("vector", _wait, sem="sVE", val=C["sVE"])
            rec("vector", _tt, out=tmp_sq[:].rearrange("p (h v) -> p h v", h=H),
                in0=o3, in1=o3, op=ALU.mult, sem="sVE")
            vsq_d = adv("sVE", 1)
            rec("vector", _wait, sem="sVE", val=vsq_d)
            rec("vector", _red, out=rms_s[:, tc, :],
                in_=tmp_sq[:].rearrange("p (h v) -> p h v", h=H), op=ALU.add, sem="sVE")
            adv("sVE", 1)
        vred_d = C["sVE"]
        rec("scalar", _wait, sem="sVE", val=vred_d)
        rec("scalar", _act, out=rms_s[:].rearrange("p a h -> p (a h)"),
            in_=rms_s[:].rearrange("p a h -> p (a h)"), func=AF.Ln, scale=1.0 / DK,
            bias=1e-5, sem="sSC")
        vln_d = adv("sSC", 1)
        rec("scalar", _wait, sem="sSC", val=vln_d)
        rec("scalar", _act, out=rms_r[:].rearrange("p a h -> p (a h)"),
            in_=rms_s[:].rearrange("p a h -> p (a h)"), func=AF.Exp, scale=-0.5, sem="sSC")
        vrs_d = adv("sSC", 1)
        for tc in range(NTC):
            maybe_fill()
            o3 = O_acc[:, tc, :].rearrange("p (h v) -> p h v", h=H)
            rec("vector", _wait, sem="sSC", val=vrs_d)
            if tc > 1:
                rec("vector", _wait, sem="sPE", val=v_onrm_tr[tc - 2])
            rb = rms_r[:, tc, :].unsqueeze(2).broadcast_to([128, H, DK])
            rec("vector", _tt, out=Onrm[:, tc % 2, :].rearrange("p (h v) -> p h v", h=H),
                in0=o3, in1=rb, op=ALU.mult, sem="sVE")
            vON = adv("sVE", 1)
            rec("tensor", _wait, sem="sVE", val=vON)
            rec("tensor", _wait, sem="sVE", val=last["tb_free"][tc % 2])
            for g in range(CG):
                rec("tensor", _tr, out=bankbf(2 + tc % 2, g * 128, 128),
                    in_=Onrm[:, tc % 2, g * 128:(g + 1) * 128], ident=identb[:],
                    sem="sPE" if g == CG - 1 else None)
            v_onrm_tr[tc] = adv("sPE", 1)
            rec("vector", _wait, sem="sPE", val=v_onrm_tr[tc])
            rec("vector", _cp, out=Onrm_fm[:, :, tc * 128:(tc + 1) * 128],
                in_=bankbf(2 + tc % 2, 0, 1024).rearrange("p (g t) -> p g t", g=CG), sem="sVE")
            vev = adv("sVE", 1)
            last["tb_free"][tc % 2] = vev
        v_onrm_done = max(last["tb_free"])

        while fil["next_h2"] < F1 // 128:
            emit_h2_slice()
        v_h2_done = fil["v_h2_last"]

        rec("sync", _wait, sem="sPE", val=c_pe_done)
        rec("sync", _wait, sem="sOW", val=C["sOW"])
        rec("sync", _dma, out=owT_sb[:],
            in_=ow_scr.rearrange("(cg p) o -> p cg o", p=128), sem="sOW")
        v_ow = adv("sOW", 16)
        rec("tensor", _wait, sem="sSC", val=v_h2_done)
        rec("tensor", _wait, sem="sVE", val=v_onrm_done)
        rec("tensor", _wait, sem="sOW", val=v_ow)
        for tc in range(NTC):
            for g in range(CG):
                for ds in range(2):
                    rec("tensor", _mm, out=pflat(tc * 1024 + ds * 512, 512),
                        lhsT=Onrm_fm[:, g, tc * 128:(tc + 1) * 128],
                        rhs=owT_sb[:, g, ds * 512:(ds + 1) * 512],
                        start=(g == 0), stop=False,
                        sem="sPE" if (tc == NTC - 1 and g == CG - 1 and ds == 1) else None)
        v_attn_mm = adv("sPE", 1)
        for o2s in range(8):
            par = o2s % 2
            w2sem = "sW2" if par == 0 else "sW2B"
            rec("sync", _wait, sem="sPE", val=max(last["w2_user"][par], c_pe_done))
            rec("sync", _dma, out=w2_sb[:, par],
                in_=w2_scr[o2s * 512:(o2s + 1) * 512, :].rearrange("(s p) d -> p s d", p=128),
                sem=w2sem)
            v_w2 = adv(w2sem, 16)
            rec("tensor", _wait, sem=w2sem, val=v_w2)
            lastslice = (o2s == 7)
            for tc in range(NTC):
                for sub in range(4):
                    o2g = o2s * 4 + sub
                    for ds in range(2):
                        is_last = lastslice and sub == 3
                        rec("tensor", _mm,
                            out=pflat(tc * 1024 + ds * 512, 512),
                            lhsT=H2[:, o2g, tc * 128:(tc + 1) * 128],
                            rhs=w2_sb[:, par, sub, ds * 512:(ds + 1) * 512],
                            start=False, stop=is_last,
                            sem="sPE" if (tc == NTC - 1 and sub == 3 and ds == 1) else None)
            last["w2_user"][par] = adv("sPE", 1)
        v_e3 = C["sPE"]
        last["w2_user"] = [v_e3, v_e3]
        for tc in range(NTC):
            par = tc % 2
            rec("vector", _wait, sem="sPE", val=v_e3)
            if tc >= 2:
                rec("vector", _wait, sem="sOUT", val=C["sOUT"])
            rec("vector", _tt, out=out_sb[:, par, :], in0=pflat(tc * 1024, 1024),
                in1=x_seg[:, tc, :], op=ALU.add, sem="sVE")
            vev = adv("sVE", 1)
            rec("scalar", _wait, sem="sVE", val=vev)
            rec("scalar", _dma, out=out_d[b, t0 + tc * 128: t0 + (tc + 1) * 128, :],
                in_=out_sb[:, par, :], sem="sOUT")
            last["out_user"][par] = adv("sOUT", 16)
        last["e3_evict"] = C["sVE"]
        last["tb_free"] = [C["sVE"], C["sVE"]]
        last["e_pe_done"] = v_e3

        if cfg["debug"] and seg == 0:
            rec("scalar", _wait, sem="sVE", val=C["sVE"])
            for name, srcap in [("xn_fm", xn_fm[:]), ("wgt01", wgt01[:]), ("mask01", mask01[:]),
                              ("Qp", Qp_fm[:]), ("Oacc", O_acc[:]), ("H1", H1[:]),
                              ("H2", H2[:]), ("Onrmfm", Onrm_fm[:])]:
                rec("scalar", _dma, out=dbg[name], in_=srcap, sem="sOUT")
                adv("sOUT", 16)

    rec("scalar", _wait, sem="sOUT", val=C["sOUT"])
    rec("scalar", _wait, sem="sSV", val=C["sSV"])

    chk = {k: 0 for k in SEMS}
    for _en, _fn, _kw in REC:
        s = _kw.get("sem")
        if s and _fn is not _wait:
            chk[s] += 16 if _fn is _dma else 1
    chk["sGP"] += 15
    assert chk == C, f"sem count mismatch: {chk} vs {C}"

    with es:
        block = nc.Block()
        with block:
            for eng_name in ["sync", "gpsimd", "tensor", "vector", "scalar"]:
                def body(e, _n=eng_name):
                    for en, fn, kw in REC:
                        if en == _n:
                            fn(e, **kw)
                getattr(block, eng_name)(body)
    return nc


def kernel(**inputs):
    cfg = dict(DEFAULT_CFG)
    x = np.ascontiguousarray(inputs["L0_attn_x"])
    n_cores = 8
    shared = make_shared_inputs(inputs)
    nc = build_program(cfg)
    in_maps = []
    for c in range(n_cores):
        m = dict(shared)
        m["x"] = np.ascontiguousarray(x[2 * c: 2 * c + 2])
        in_maps.append(m)
    res = run_bass_kernel_spmd(nc, in_maps, core_ids=list(range(n_cores)))
    return np.concatenate([r["out"] for r in res.results], axis=0)


def make_shared_inputs(inputs):
    return dict(
        gate_wT=np.ascontiguousarray(inputs["L0_attn_gate_w"].T),
        q_wT=np.ascontiguousarray(inputs["L0_attn_q_w"].T),
        k_wT=np.ascontiguousarray(np.transpose(inputs["L0_attn_k_w"], (0, 2, 1))),
        v_wT=np.ascontiguousarray(np.transpose(inputs["L0_attn_v_w"], (0, 2, 1))),
        fmq1T=np.ascontiguousarray(inputs["L0_attn_fmq1_w"].T),
        fmq2T=np.ascontiguousarray(inputs["L0_attn_fmq2_w"].T),
        fmk1T=np.ascontiguousarray(inputs["L0_attn_fmk1_w"].T),
        fmk2T=np.ascontiguousarray(inputs["L0_attn_fmk2_w"].T),
        norm_w=np.ascontiguousarray(np.asarray(inputs["L0_attn_norm_w"]).reshape(DK, 1)),
        o_wT=np.ascontiguousarray(inputs["L0_attn_o_w"].T),
        k0=np.ascontiguousarray(inputs["L0_ffn_k0"]).view(np.uint8),
        k1=np.ascontiguousarray(inputs["L0_ffn_k1"]).view(np.uint8),
        k2=np.ascontiguousarray(inputs["L0_ffn_k2"]).view(np.uint8),
        s0=np.asarray(inputs["L0_ffn_s0"], dtype=np.float32).reshape(1, 1),
        s1=np.asarray(inputs["L0_ffn_s1"], dtype=np.float32).reshape(1, 1),
        s2=np.asarray(inputs["L0_ffn_s2"], dtype=np.float32).reshape(1, 1),
    )
